# version 05
# baseline (speedup 1.0000x reference)
import time
import numpy as np
from contextlib import ExitStack

import concourse.bass as bass
import concourse.mybir as mybir
from concourse.bass_utils import run_bass_kernel_spmd

F32 = mybir.dt.float32
BF16 = mybir.dt.bfloat16
I32 = mybir.dt.int32
AF = mybir.ActivationFunctionType
ALU = mybir.AluOpType

NCORES = 8
D = 2048
NCH = 16
T = 2048
E = 8
IFF = 1408
NI = 11
RT = 1024
TS = 512
NSUB = 4
EPS = 1e-5

STAGE = 3
DEBUG = False



def build(stage=STAGE):
    nc = bass.Bass("TRN2", target_bir_lowering=False)

    xin = nc.dram_tensor("xin", [D, T], F32, kind="ExternalInput").ap()
    wg_h = nc.dram_tensor("fwg", [E, D, IFF], F32, kind="ExternalInput").ap()
    wu_h = nc.dram_tensor("fwu", [E, D, IFF], F32, kind="ExternalInput").ap()
    wd_h = nc.dram_tensor("fwd", [E, IFF, D], F32, kind="ExternalInput").ap()
    wr_h = nc.dram_tensor("fwr", [128, 128], F32, kind="ExternalInput").ap()
    wut_h = nc.dram_tensor("awut", [D, D], F32, kind="ExternalInput").ap()
    wvt_h = nc.dram_tensor("awvt", [D, D], F32, kind="ExternalInput").ap()
    wg1t_h = nc.dram_tensor("awg1t", [D, 512], F32, kind="ExternalInput").ap()
    wg2t_h = nc.dram_tensor("awg2t", [512, D], F32, kind="ExternalInput").ap()
    bu_h = nc.dram_tensor("abu", [128, 16], F32, kind="ExternalInput").ap()
    bv_h = nc.dram_tensor("abv", [128, 16], F32, kind="ExternalInput").ap()
    bg1_h = nc.dram_tensor("abg1", [128, 4], F32, kind="ExternalInput").ap()
    bg2_h = nc.dram_tensor("abg2", [128, 16], F32, kind="ExternalInput").ap()
    cw_h = nc.dram_tensor("acw", [128, 48], F32, kind="ExternalInput").ap()
    cb_h = nc.dram_tensor("acb", [128, 16], F32, kind="ExternalInput").ap()
    ident_h = nc.dram_tensor("ident", [128, 128], F32, kind="ExternalInput").ap()
    esel_h = nc.dram_tensor("esel", [8, 8 * 128], F32, kind="ExternalInput").ap()
    par_h = nc.dram_tensor("par", [1, 16], I32, kind="ExternalInput").ap()
    epoch_h = nc.dram_tensor("epoch", [1, 16], I32, kind="ExternalInput").ap()
    modd_h = nc.dram_tensor("modd", [128, 1], F32, kind="ExternalInput").ap()
    mevn_h = nc.dram_tensor("mevn", [128, 1], F32, kind="ExternalInput").ap()
    yout = nc.dram_tensor("yout", [D, T], F32, kind="ExternalOutput").ap()
    if DEBUG:
        dx1 = nc.dram_tensor("dx1", [D, T], F32, kind="ExternalOutput").ap()
        dxn2 = nc.dram_tensor("dxn2", [D, T], BF16, kind="ExternalOutput").ap()
        dcum = nc.dram_tensor("dcum", [D, T], F32, kind="ExternalOutput").ap()
        dgo = nc.dram_tensor("dgo", [D, T], BF16, kind="ExternalOutput").ap()

    x1_s = nc.dram_tensor("x1_s", [D, T], F32)
    x2_s = nc.dram_tensor("x2_s", [D, T], F32)
    xn2_s = nc.dram_tensor("xn2_s", [D, T], BF16)
    cum_s = nc.dram_tensor("cum_s", [D, T], F32)
    wcu = nc.dram_tensor("wcu", [E * 2 * NI, NCH * 128 * 128], BF16)
    wcd = nc.dram_tensor("wcd", [E * NCH, NI * 128 * 128], BF16)
    shm = nc.dram_tensor("shmx", [256, 48], F32, addr_space="Shared")
    flg = nc.dram_tensor("flgx", [2, 16], I32, addr_space="Shared")

    xin_r = xin.rearrange("(c p) t -> p c t", p=128)
    y_r = yout.rearrange("(c p) t -> p c t", p=128)
    x1_r = x1_s.ap().rearrange("(c p) t -> p c t", p=128)
    x2_r = x2_s.ap().rearrange("(c p) t -> p c t", p=128)
    xn2_r = xn2_s.ap().rearrange("(c p) t -> p c t", p=128)
    cum_r = cum_s.ap().rearrange("(c p) t -> p c t", p=128)
    wg_r = wg_h.rearrange("e (c p) (i f) -> e i p c f", p=128, f=128)
    wu_r = wu_h.rearrange("e (c p) (i f) -> e i p c f", p=128, f=128)
    wd_r = wd_h.rearrange("e (ci p) (d f) -> e d p ci f", p=128, f=128)
    wut_rc = wut_h.rearrange("(c p) (rc f) -> rc p c f", p=128, f=128)
    wvt_rc = wvt_h.rearrange("(c p) (rc f) -> rc p c f", p=128, f=128)
    wg1_rc = wg1t_h.rearrange("(c p) (hc f) -> hc p c f", p=128, f=128)
    wg2_rc = wg2t_h.rearrange("(hc p) (rc f) -> rc p hc f", p=128, f=128)
    wcu_v = wcu.ap().rearrange("n (p f) -> n p f", p=128)
    wcd_v = wcd.ap().rearrange("n (p f) -> n p f", p=128)

    ENGS = ("sync", "gpsimd", "vector", "scalar", "tensor")
    Q = {e: [] for e in ENGS}

    def EM(eng, fn):
        Q[eng].append(fn)

    class Ctr:
        def __init__(self, name):
            self.cm = nc.semaphore(name)
            self.h = self.cm.__enter__()
            self.n = 0

    sx = Ctr("sx")
    sc = Ctr("sc")
    sg = Ctr("sg")
    cv = Ctr("cv")
    cs = Ctr("cs")
    cg = Ctr("cg")
    ct = Ctr("ct")
    ALLC = [sx, sc, sg, cv, cs, cg, ct]

    def WT(eng, ctr, val):
        if val <= 0:
            return
        EM(eng, lambda En, h=ctr.h, v=val: En.wait_ge(h, v))

    def DMA(eng, ctr, out, in_):
        def f(En, o=out, i=in_, h=ctr.h):
            En.dma_start(out=o, in_=i).then_inc(h, 16)
        EM(eng, f)
        ctr.n += 16
        return ctr.n

    ENG2CTR = {"vector": cv, "scalar": cs, "gpsimd": cg, "tensor": ct}

    def OP(eng, fn, flush=True):
        ctr = ENG2CTR[eng]
        if flush and ctr.n > 0:
            WT(eng, ctr, ctr.n)
        def g(En, fn=fn, h=ctr.h):
            fn(En).then_inc(h, 1)
        EM(eng, g)
        ctr.n += 1
        return ctr.n

    def MM(out, lhsT, rhs, start, stop, inc=False):
        if inc:
            def f(En, o=out, l=lhsT, r=rhs, st=start, sp=stop, h=ct.h):
                En.matmul(o, l, r, start=st, stop=sp).then_inc(h, 1)
            EM("tensor", f)
            ct.n += 1
            return ct.n
        else:
            def f(En, o=out, l=lhsT, r=rhs, st=start, sp=stop):
                En.matmul(o, l, r, start=st, stop=sp)
            EM("tensor", f)
            return None

    def barrier():
        for e in ENGS:
            for c in ALLC:
                WT(e, c, c.n)

    bank_free = [[] for _ in range(8)]

    def use_bank(b):
        for (c, v) in bank_free[b]:
            WT("tensor", c, v)
        bank_free[b] = []

    def release_bank(b, ctr, val):
        bank_free[b].append((ctr, val))

    est = ExitStack()

    def sb(name, shape, dt):
        return est.enter_context(nc.sbuf_tensor(name, shape, dt))

    bu_sb = sb("bu_sb", [128, 16], F32)
    bv_sb = sb("bv_sb", [128, 16], F32)
    bg2_sb = sb("bg2_sb", [128, 16], F32)
    cb_sb = sb("cb_sb", [128, 16], F32)
    bg1_sb = sb("bg1_sb", [128, 4], F32)
    cw_sb = sb("cw_sb", [128, 48], F32)
    wr_sb = sb("wr_sb", [128, 128], F32)
    wrb_sb = sb("wrb_sb", [128, 128], BF16)
    id_sb = sb("id_sb", [128, 128], F32)
    esel_sb = sb("esel_sb", [8, 8 * 128], F32)
    ones_b = sb("ones_b", [128, 1], BF16)
    ones1 = sb("ones1", [1, 128], F32)
    onesc = sb("onesc", [128, 512], F32)
    crows = sb("crows", [8, 2048], F32)
    lnsm = sb("lnsm", [1, 4 * 512], F32)
    lg_sb = sb("lg_sb", [8, 512], F32)
    tm_sb = sb("tm_sb", [128, 32], F32)
    tm2_sb = sb("tm2_sb", [128, 32], F32)
    pay_sb = sb("pay_sb", [128, 48], F32)
    slA_sb = sb("slA_sb", [128, 48], F32)
    slB_sb = sb("slB_sb", [128, 48], F32)
    offs_sb = sb("offs_sb", [128, 16], F32)
    modd_sb = sb("modd_sb", [128, 1], F32)
    mevn_sb = sb("mevn_sb", [128, 1], F32)
    par_sb = sb("par_sb", [1, 16], I32)
    ep_sb = sb("ep_sb", [1, 16], I32)
    wgb = sb("wgb", [128, 2 * 2048], BF16)
    wub = sb("wub", [128, 2 * 2048], BF16)
    wdb = sb("wdb", [128, 2 * 1408], BF16)
    xbfc = sb("xbfc", [128, 2 * 512], BF16)
    epsv = sb("epsv", [1, 1], F32)
    sqfc = sb("sqfc", [128, 2 * 512], BF16)

    ps = [est.enter_context(nc.psum_tensor(f"ps{i}", [128, 512], F32)) for i in range(8)]

    small_loads = [
        (bu_sb[:], bu_h), (bv_sb[:], bv_h), (bg2_sb[:], bg2_h), (cb_sb[:], cb_h),
        (bg1_sb[:], bg1_h), (cw_sb[:], cw_h), (wr_sb[:], wr_h), (id_sb[:], ident_h),
        (modd_sb[:], modd_h), (mevn_sb[:], mevn_h), (par_sb[:], par_h), (ep_sb[:], epoch_h),
        (esel_sb[:], esel_h),
    ]
    for dst, srcv in small_loads:
        DMA("sync", sx, dst, srcv)
    v_pro = sx.n
    OP("vector", lambda En: En.memset(ones_b[:], 1.0))
    OP("vector", lambda En: En.memset(ones1[:], 1.0))
    OP("vector", lambda En: En.memset(onesc[:], 1.0))
    OP("vector", lambda En: En.memset(epsv[:], EPS))
    WT("vector", sx, v_pro)
    OP("vector", lambda En: En.tensor_copy(wrb_sb[:], wr_sb[:]))
    pro_done = cv.n

    xn2war = {0: 0, 1: 0}
    xn2war_c = [None]
    PREV_DN = [0]
    BGQ = []

    def drain_bg(n=1):
        for _ in range(n):
            if not BGQ:
                return
            BGQ.pop(0)()

    def flush_bg():
        while BGQ:
            BGQ.pop(0)()

    def ln_thunks(chunk_src, xn_dst, c_off, with_router, src_guard=None,
                  post_norm=None, stq=None):
        th = []
        stq_e, stq_c = stq
        state = {"war": {0: 0, 1: 0, 2: 0, 3: 0}, "ld": {}, "bmm": 0, "norm_done": 0}

        def stats_group(c0):
            def f(c0=c0):
                if c0 == 0:
                    use_bank(6)
                    use_bank(7)
                    if src_guard is not None:
                        WT(stq_e, src_guard[0], src_guard[1])
                for c in range(c0, c0 + 4):
                    sl = c % 2
                    WT(stq_e, cv, state["war"][sl])
                    lv = DMA(stq_e, stq_c, xring[:, sl * 512:(sl + 1) * 512], chunk_src(c))
                    WT("vector", stq_c, lv)
                    WT("vector", ct, state["ld"].get(c - 2, 0))
                    vcp = OP("vector", lambda En, sl=sl: En.tensor_copy(
                        xbfc[:, sl * 512:(sl + 1) * 512], xring[:, sl * 512:(sl + 1) * 512]))
                    state["war"][sl] = vcp
                    WT("scalar", ct, state["ld"].get(c - 2, 0))
                    WT("scalar", cv, vcp)
                    scp = OP("scalar", lambda En, sl=sl: En.activation(
                        out=sqfc[:, sl * 512:(sl + 1) * 512],
                        in_=xbfc[:, sl * 512:(sl + 1) * 512], func=AF.Square))
                    WT("tensor", cv, vcp)
                    WT("tensor", cs, scp)
                    MM(ps[6][0:1, :], ones_b[:], xbfc[:, sl * 512:(sl + 1) * 512],
                       start=(c == 0), stop=(c == NCH - 1), inc=False)
                    state["ld"][c] = MM(ps[7][0:1, :], ones_b[:], sqfc[:, sl * 512:(sl + 1) * 512],
                                        start=(c == 0), stop=(c == NCH - 1), inc=True)
                    if c + 2 <= NCH - 1:
                        pass
            return f

        def stats_group_guarded(c0):
            base = stats_group(c0)
            def f():
                base()
            return f

        for c0 in (0, 4, 8, 12):
            th.append(stats_group(c0))

        def meanvar():
            stats_done = ct.n
            WT("scalar", ct, stats_done)
            s1 = OP("scalar", lambda En: En.mul(lnsm[0:1, 0:512], ps[6][0:1, :], 1.0 / D))
            s2 = OP("scalar", lambda En: En.mul(lnsm[0:1, 512:1024], ps[7][0:1, :], 1.0 / D))
            release_bank(6, cs, s2)
            release_bank(7, cs, s2)
            WT("vector", cs, s2)
            OP("vector", lambda En: En.tensor_mul(
                lnsm[0:1, 1536:2048], lnsm[0:1, 0:512], lnsm[0:1, 0:512]))
            OP("vector", lambda En: En.tensor_sub(
                lnsm[0:1, 512:1024], lnsm[0:1, 512:1024], lnsm[0:1, 1536:2048]))
            vv = cv.n
            WT("scalar", cv, vv)
            sd = OP("scalar", lambda En: En.activation(
                out=lnsm[0:1, 1024:1536], in_=lnsm[0:1, 512:1024], func=AF.Sqrt,
                bias=epsv[0:1, 0:1]))
            WT("vector", cs, sd)
            rst = OP("vector", lambda En: En.reciprocal(lnsm[0:1, 1024:1536], lnsm[0:1, 1024:1536]))
            use_bank(6)
            use_bank(7)
            WT("tensor", cv, rst)
            MM(ps[6][:], ones1[:], lnsm[0:1, 0:512], start=True, stop=True, inc=False)
            state["bmm"] = MM(ps[7][:], ones1[:], lnsm[0:1, 1024:1536],
                              start=True, stop=True, inc=True)
        th.append(meanvar)

        def norm_group(c0):
            def f(c0=c0):
                if c0 == 0:
                    WT("vector", ct, state["bmm"])
                for c in range(c0, c0 + 4):
                    sl = 2 + c % 2
                    WT(stq_e, cv, state["war"][sl])
                    lv = DMA(stq_e, stq_c, xring[:, sl * 512:(sl + 1) * 512], chunk_src(c))
                    WT("vector", stq_c, lv)
                    OP("vector", lambda En, sl=sl: En.tensor_sub(
                        xring[:, sl * 512:(sl + 1) * 512],
                        xring[:, sl * 512:(sl + 1) * 512], ps[6][:]))
                    if xn_dst is not None:
                        mv = OP("vector", lambda En, sl=sl, d=xn_dst(c): En.tensor_mul(
                            d, xring[:, sl * 512:(sl + 1) * 512], ps[7][:]))
                        state["war"][sl] = mv
                    else:
                        rsl = c % 2
                        WT("vector", xn2war_c[0], xn2war[rsl])
                        mv = OP("vector", lambda En, sl=sl, rsl=rsl: En.tensor_mul(
                            xn2ring[:, rsl * 512:(rsl + 1) * 512],
                            xring[:, sl * 512:(sl + 1) * 512], ps[7][:]))
                        state["war"][sl] = mv
                        post_norm(c, rsl, mv)
                if c0 == 12:
                    state["norm_done"] = cv.n
                    release_bank(6, cv, cv.n)
                    release_bank(7, cv, cv.n)
            return f

        for c0 in (0, 4, 8, 12):
            th.append(norm_group(c0))

        if with_router:
            def router1():
                use_bank(6)
                WT("tensor", cv, state["norm_done"])
                lgd = None
                for c in range(NCH):
                    lgd = MM(ps[6][0:8, :], wrb_sb[:, c * 8:(c + 1) * 8], xn_dst(c),
                             start=(c == 0), stop=(c == NCH - 1), inc=(c == NCH - 1))
                WT("scalar", ct, lgd)
                lge = OP("scalar", lambda En: En.copy(lg_sb[:], ps[6][0:8, :]))
                release_bank(6, cs, lge)
                use_bank(7)
                for j in range(4):
                    WT("tensor", cs, cs.n)
                    tj = MM(ps[7][0:128, 0:8], lg_sb[0:8, j * 128:(j + 1) * 128],
                            id_sb[0:8, 0:8], start=True, stop=True, inc=True)
                    WT("scalar", ct, tj)
                    OP("scalar", lambda En, j=j: En.copy(tm_sb[:, j * 8:(j + 1) * 8],
                                                         ps[7][0:128, 0:8]))
                release_bank(7, cs, cs.n)
            th.append(router1)

            def router2():
                WT("scalar", cs, cs.n)
                ex = OP("scalar", lambda En: En.activation(out=tm_sb[:], in_=tm_sb[:], func=AF.Exp))
                WT("vector", cs, ex)
                for j in range(4):
                    b0 = j * 8
                    OP("vector", lambda En, b0=b0: En.reduce_max(
                        offs_sb[:, 0:1], tm_sb[:, b0:b0 + 8], axis=mybir.AxisListType.X))
                    OP("vector", lambda En, b0=b0: En.tensor_scalar(
                        out=tm2_sb[:, b0:b0 + 8], in0=tm_sb[:, b0:b0 + 8],
                        scalar1=offs_sb[:, 0:1], scalar2=1.0, op0=ALU.is_equal, op1=ALU.mult))
                    OP("vector", lambda En, b0=b0: En.scalar_tensor_tensor(
                        out=tm2_sb[:, b0:b0 + 8], in0=tm2_sb[:, b0:b0 + 8], scalar=-1e30,
                        in1=tm_sb[:, b0:b0 + 8], op0=ALU.mult, op1=ALU.add))
                    OP("vector", lambda En, b0=b0: En.reduce_max(
                        offs_sb[:, 1:2], tm2_sb[:, b0:b0 + 8], axis=mybir.AxisListType.X))
                    OP("vector", lambda En: En.tensor_add(offs_sb[:, 2:3], offs_sb[:, 0:1], offs_sb[:, 1:2]))
                    OP("vector", lambda En: En.reciprocal(offs_sb[:, 2:3], offs_sb[:, 2:3]))
                    OP("vector", lambda En, b0=b0: En.tensor_scalar(
                        out=tm2_sb[:, b0:b0 + 8], in0=tm_sb[:, b0:b0 + 8],
                        scalar1=offs_sb[:, 1:2], scalar2=1.0, op0=ALU.is_ge, op1=ALU.mult))
                    OP("vector", lambda En, b0=b0: En.tensor_mul(
                        tm2_sb[:, b0:b0 + 8], tm2_sb[:, b0:b0 + 8], tm_sb[:, b0:b0 + 8]))
                    OP("vector", lambda En, b0=b0: En.tensor_scalar(
                        out=tm2_sb[:, b0:b0 + 8], in0=tm2_sb[:, b0:b0 + 8],
                        scalar1=offs_sb[:, 2:3], scalar2=0.5, op0=ALU.mult, op1=ALU.mult))
                c2done = cv.n
                use_bank(6)
                for j in range(4):
                    WT("tensor", cv, c2done)
                    if j > 0:
                        WT("tensor", cs, cs.n)
                    tj = MM(ps[6][0:8, 0:128], tm2_sb[:, j * 8:(j + 1) * 8],
                            id_sb[:, :], start=True, stop=True, inc=True)
                    WT("scalar", ct, tj)
                    OP("scalar", lambda En, j=j: En.copy(
                        crows[0:8, c_off + j * 128: c_off + (j + 1) * 128], ps[6][0:8, 0:128]))
                release_bank(6, cs, cs.n)
            th.append(router2)
        return th

    upfree = {0: [], 1: []}
    dnfree = {0: [], 1: []}

    def load_up_col(call, rnd, e, i, cast_mode):
        slot = i % 2
        for (cc, vv2) in upfree[slot]:
            WT("gpsimd" if cast_mode else "sync", cc, vv2)
        upfree[slot] = []
        wg_dst = wgb[:, slot * 2048:(slot + 1) * 2048].rearrange("p (c f) -> p c f", f=128)
        wu_dst = wub[:, slot * 2048:(slot + 1) * 2048].rearrange("p (c f) -> p c f", f=128)
        if cast_mode:
            v1 = DMA("gpsimd", sg, wg_dst, wg_r[e, i])
            v2 = DMA("gpsimd", sg, wu_dst, wu_r[e, i])
            WT("scalar", sg, v2)
            cw1 = DMA("scalar", sc, wcu_v[(e * 2 + 0) * NI + i],
                      wgb[:, slot * 2048:(slot + 1) * 2048])
            cw2 = DMA("scalar", sc, wcu_v[(e * 2 + 1) * NI + i],
                      wub[:, slot * 2048:(slot + 1) * 2048])
            return (sg, v2), (sc, cw2)
        else:
            v1 = DMA("sync", sx, wgb[:, slot * 2048:(slot + 1) * 2048],
                     wcu_v[(e * 2 + 0) * NI + i])
            v2 = DMA("sync", sx, wub[:, slot * 2048:(slot + 1) * 2048],
                     wcu_v[(e * 2 + 1) * NI + i])
            return (sx, v2), None

    def load_dn_col(call, rnd, e, d, cast_mode):
        slot = d % 2
        for (cc, vv2) in dnfree[slot]:
            WT("gpsimd" if cast_mode else "sync", cc, vv2)
        dnfree[slot] = []
        wd_dst = wdb[:, slot * 1408:(slot + 1) * 1408].rearrange("p (ci f) -> p ci f", f=128)
        if cast_mode:
            v1 = DMA("gpsimd", sg, wd_dst, wd_r[e, d])
            WT("scalar", sg, v1)
            cwv = DMA("scalar", sc, wcd_v[e * NCH + d],
                      wdb[:, slot * 1408:(slot + 1) * 1408])
            return (sg, v1), (sc, cwv)
        else:
            v1 = DMA("sync", sx, wdb[:, slot * 1408:(slot + 1) * 1408],
                     wcd_v[e * NCH + d])
            return (sx, v1), None

    def fin_quick(call, rnd, src_r, acc_sb, stq):
        stq_e, stq_c = stq
        if call == 0 and stage >= 2:
            dst = x1_r
        else:
            dst = y_r
        war = {0: 0, 1: 0}
        last = 0
        for s in range(2):
            toff = rnd * RT + s * TS
            for c in range(NCH):
                sl = c % 2
                WT(stq_e, stq_c, war[sl])
                lv = DMA(stq_e, stq_c, xring[:, sl * 512:(sl + 1) * 512],
                         src_r[:, c, toff:toff + TS])
                WT("vector", stq_c, lv)
                av = OP("vector", lambda En, c=c, s=s, sl=sl: En.tensor_add(
                    xring[:, sl * 512:(sl + 1) * 512], xring[:, sl * 512:(sl + 1) * 512],
                    acc_sb[:, c * RT + s * TS: c * RT + (s + 1) * TS]))
                WT(stq_e, cv, av)
                war[sl] = DMA(stq_e, stq_c, dst[:, c, toff:toff + TS],
                              xring[:, sl * 512:(sl + 1) * 512])
                last = war[sl]
        return dst, last

    def fin_ln2_thunks(rnd, store_done, stq):
        ths = []
        stq_e, stq_c = stq
        xn2war_c[0] = stq_c
        for s in range(2):
            toff = rnd * RT + s * TS

            def post_norm(c, rsl, mv, toff=toff, rnd=rnd, s=s):
                if rnd == 0 and s == 0:
                    OP("vector", lambda En, c=c, rsl=rsl: En.tensor_copy(
                        pay_sb[:, 16 + c:17 + c], xn2ring[:, rsl * 512:rsl * 512 + 1]))
                if rnd == 1 and s == 1:
                    OP("vector", lambda En, c=c, rsl=rsl: En.tensor_copy(
                        pay_sb[:, 32 + c:33 + c], xn2ring[:, rsl * 512 + 511:rsl * 512 + 512]))
                WT(stq_e, cv, cv.n)
                xn2war[rsl] = DMA(stq_e, stq_c, xn2_r[:, c, toff:toff + TS],
                                  xn2ring[:, rsl * 512:(rsl + 1) * 512])

            ths += ln_thunks(
                lambda c, toff=toff: x1_r[:, c, toff:toff + TS],
                None, toff, False, src_guard=(stq_c, store_done), post_norm=post_norm,
                stq=stq)
        return ths

    def emit_experts(call, rnd, xn_r_sb, acc_sb):
        cast_mode = (call == 0 and rnd == 0)
        t0 = rnd * RT
        zz = OP("vector", lambda En: En.memset(acc_sb[:], 0.0))
        prev_dn_end = PREV_DN[0]
        for e in range(E):
            use_bank(4)
            use_bank(5)
            WT("tensor", cs, cs.n)
            MM(ps[4][:], esel_sb[0:8, e * 128:(e + 1) * 128], crows[0:8, t0:t0 + TS],
               start=True, stop=True, inc=False)
            cb2 = MM(ps[5][:], esel_sb[0:8, e * 128:(e + 1) * 128],
                     crows[0:8, t0 + TS:t0 + 2 * TS], start=True, stop=True, inc=True)
            WT("scalar", ct, cb2)
            cbe1 = OP("scalar", lambda En: En.copy(cbc[:, 0:512], ps[4][:]))
            cbe2 = OP("scalar", lambda En: En.copy(cbc[:, 512:1024], ps[5][:]))
            release_bank(4, cs, cbe2)
            release_bank(5, cs, cbe2)
            for i in range(NI):
                (lc, lv), cwp = load_up_col(call, rnd, e, i, cast_mode)
                WT("tensor", lc, lv)
                slot = i % 2
                for s in range(2):
                    use_bank(s)
                    use_bank(2 + s)
                    gmm = None
                    for c in range(NCH):
                        gmm = MM(ps[s][:], wgb[:, slot * 2048 + c * 128: slot * 2048 + (c + 1) * 128],
                                 xn_r_sb[:, c * T + t0 + s * TS: c * T + t0 + (s + 1) * TS],
                                 start=(c == 0), stop=(c == NCH - 1), inc=(c == NCH - 1))
                    for c in range(NCH):
                        umm = MM(ps[2 + s][:], wub[:, slot * 2048 + c * 128: slot * 2048 + (c + 1) * 128],
                                 xn_r_sb[:, c * T + t0 + s * TS: c * T + t0 + (s + 1) * TS],
                                 start=(c == 0), stop=(c == NCH - 1), inc=(c == NCH - 1))
                    WT("scalar", ct, gmm)
                    sil = OP("scalar", lambda En, s=s: En.activation(
                        out=t1sc[:, s * 512:(s + 1) * 512], in_=ps[s][:], func=AF.Silu))
                    release_bank(s, cs, sil)
                    WT("vector", cs, sil)
                    WT("vector", ct, umm)
                    t2v = OP("vector", lambda En, s=s: En.tensor_mul(
                        t2sc[:, s * 512:(s + 1) * 512], t1sc[:, s * 512:(s + 1) * 512], ps[2 + s][:]))
                    release_bank(2 + s, cv, t2v)
                    WT("vector", cs, cbe2)
                    if prev_dn_end:
                        WT("vector", ct, prev_dn_end)
                    hv = OP("vector", lambda En, i=i, s=s: En.tensor_mul(
                        hbuf[:, i * RT + s * TS: i * RT + (s + 1) * TS],
                        t2sc[:, s * 512:(s + 1) * 512], cbc[:, s * 512:(s + 1) * 512]))
                upfree[slot].append((ct, ct.n))
                if cwp is not None:
                    upfree[slot].append(cwp)
                drain_bg(1)
            hready = cv.n
            for d in range(NCH):
                (lc, lv), cwp = load_dn_col(call, rnd, e, d, cast_mode)
                WT("tensor", lc, lv)
                WT("tensor", cv, hready)
                slot = d % 2
                for s in range(2):
                    b = 4 + s
                    use_bank(b)
                    ymm = None
                    for ci in range(NI):
                        ymm = MM(ps[b][:], wdb[:, slot * 1408 + ci * 128: slot * 1408 + (ci + 1) * 128],
                                 hbuf[:, ci * RT + s * TS: ci * RT + (s + 1) * TS],
                                 start=(ci == 0), stop=(ci == NI - 1), inc=(ci == NI - 1))
                    WT("vector", ct, ymm)
                    av = OP("vector", lambda En, d=d, s=s, b=b: En.tensor_add(
                        acc_sb[:, d * RT + s * TS: d * RT + (s + 1) * TS],
                        acc_sb[:, d * RT + s * TS: d * RT + (s + 1) * TS], ps[b][:]))
                    release_bank(b, cv, av)
                dnfree[slot].append((ct, ct.n))
                if cwp is not None:
                    dnfree[slot].append(cwp)
                if d % 2 == 1:
                    drain_bg(1)
            prev_dn_end = ct.n
        PREV_DN[0] = ct.n

    def emit_ffn(call, src_r, xn_r_sb, acc_sb):
        def xn_dst(rnd, s):
            def f(c, rnd=rnd, s=s):
                return xn_r_sb[:, c * T + rnd * RT + s * TS: c * T + rnd * RT + (s + 1) * TS]
            return f

        cast0 = (call == 0)
        stq_r0 = ("sync", sx) if cast0 else ("gpsimd", sg)
        stq_r1 = ("gpsimd", sg)
        for s in range(2):
            for t in ln_thunks(lambda c, s=s: src_r[:, c, s * TS:(s + 1) * TS],
                               xn_dst(0, s), 0 * RT + s * TS, True, stq=stq_r0):
                t()
        WT("tensor", cv, cv.n)
        for s in range(2):
            BGQ.extend(ln_thunks(lambda c, s=s: src_r[:, c, RT + s * TS: RT + (s + 1) * TS],
                                 xn_dst(1, s), RT + s * TS, True, stq=stq_r0))
        emit_experts(call, 0, xn_r_sb, acc_sb)
        flush_bg()
        WT("tensor", cv, cv.n)

        dst0, st0 = fin_quick(call, 0, src_r, acc_sb, stq_r1)
        if call == 0 and stage >= 2:
            BGQ.extend(fin_ln2_thunks(0, st0, stq_r1))
        emit_experts(call, 1, xn_r_sb, acc_sb)
        flush_bg()
        dst1, st1 = fin_quick(call, 1, src_r, acc_sb, ("gpsimd", sg))
        if call == 0 and stage >= 2:
            for t in fin_ln2_thunks(1, st1, ("gpsimd", sg)):
                t()

    ffn_scope = ExitStack()
    scope_id = [0]

    def open_ffn_scope():
        k = scope_id[0]
        scope_id[0] += 1
        g = {}
        g["xn"] = ffn_scope.enter_context(nc.sbuf_tensor(f"xn_r_sb{k}", [128, NCH * T], BF16))
        g["acc"] = ffn_scope.enter_context(nc.sbuf_tensor(f"acc_sb{k}", [128, NCH * RT], BF16))
        g["h"] = ffn_scope.enter_context(nc.sbuf_tensor(f"hbuf{k}", [128, NI * RT], BF16))
        g["xr"] = ffn_scope.enter_context(nc.sbuf_tensor(f"xring{k}", [128, 4 * 512], F32))
        g["x2r"] = ffn_scope.enter_context(nc.sbuf_tensor(f"xn2ring{k}", [128, 2 * 512], BF16))
        g["t1"] = ffn_scope.enter_context(nc.sbuf_tensor(f"t1sc{k}", [128, 1024], F32))
        g["t2"] = ffn_scope.enter_context(nc.sbuf_tensor(f"t2sc{k}", [128, 1024], F32))
        g["cbc"] = ffn_scope.enter_context(nc.sbuf_tensor(f"cbc{k}", [128, 1024], F32))
        return g

    gsb = open_ffn_scope()
    xn_sbuf, acc_sbuf, hbuf = gsb["xn"], gsb["acc"], gsb["h"]
    xring, xn2ring = gsb["xr"], gsb["x2r"]
    t1sc, t2sc, cbc = gsb["t1"], gsb["t2"], gsb["cbc"]

    emit_ffn(0, xin_r, xn_sbuf, acc_sbuf)
    barrier()
    ffn_scope.close()

    if stage >= 2:
        attn_scope = ExitStack()
        xn2e = attn_scope.enter_context(nc.sbuf_tensor("xn2e", [128, NCH * 2056], BF16))
        gates = attn_scope.enter_context(nc.sbuf_tensor("gates", [128, NCH * T], BF16))
        sub_scope = ExitStack()
        hat = sub_scope.enter_context(nc.sbuf_tensor("hat", [128, 4 * T], BF16))

        def xn2sl(c, a, b):
            return xn2e[:, c * 2056 + 1 + a: c * 2056 + 1 + b]

        x2l = 0
        for c in range(NCH):
            x2l = DMA("sync", sx, xn2sl(c, 0, T), xn2_r[:, c, :])
        WT("tensor", sx, x2l)

        att_upfree = {0: [], 1: []}

        def load_attn_col(src_view, idx, slot, nch_k):
            for (cc, vv2) in att_upfree[slot]:
                WT("gpsimd", cc, vv2)
            att_upfree[slot] = []
            dst = wgb[:, slot * 2048: slot * 2048 + nch_k * 128].rearrange(
                "p (c f) -> p c f", f=128)
            v = DMA("gpsimd", sg, dst, src_view[idx])
            return sg, v

        for hc in range(4):
            lc, lv = load_attn_col(wg1_rc, hc, hc % 2, NCH)
            WT("tensor", lc, lv)
            slot = hc % 2
            for s4 in range(NSUB):
                b = s4 % 2
                use_bank(b)
                gm = None
                for c in range(NCH):
                    gm = MM(ps[b][:], wgb[:, slot * 2048 + c * 128: slot * 2048 + (c + 1) * 128],
                            xn2sl(c, s4 * TS, (s4 + 1) * TS),
                            start=(c == 0), stop=(c == NCH - 1), inc=(c == NCH - 1))
                WT("scalar", ct, gm)
                sv = OP("scalar", lambda En, hc=hc, s4=s4, b=b: En.activation(
                    out=hat[:, hc * T + s4 * TS: hc * T + (s4 + 1) * TS], in_=ps[b][:],
                    func=AF.Relu, bias=bg1_sb[:, hc:hc + 1]))
                release_bank(b, cs, sv)
            att_upfree[slot].append((ct, ct.n))

        for rc in range(NCH):
            lc, lv = load_attn_col(wg2_rc, rc, rc % 2, 4)
            WT("tensor", lc, lv)
            slot = rc % 2
            for s4 in range(NSUB):
                b = 2 + s4 % 2
                use_bank(b)
                gm = None
                for hc in range(4):
                    gm = MM(ps[b][:], wgb[:, slot * 2048 + hc * 128: slot * 2048 + (hc + 1) * 128],
                            hat[:, hc * T + s4 * TS: hc * T + (s4 + 1) * TS],
                            start=(hc == 0), stop=(hc == 3), inc=(hc == 3))
                WT("scalar", ct, gm)
                sv = OP("scalar", lambda En, rc=rc, s4=s4, b=b: En.activation(
                    out=gates[:, rc * T + s4 * TS: rc * T + (s4 + 1) * TS], in_=ps[b][:],
                    func=AF.Sigmoid, bias=bg2_sb[:, rc:rc + 1]))
                release_bank(b, cs, sv)
            att_upfree[slot].append((ct, ct.n))
        barrier()
        sub_scope.close()
        smalls = ExitStack()
        ugsc = smalls.enter_context(nc.sbuf_tensor("ugsc", [128, 512], F32))
        cumd = smalls.enter_context(nc.sbuf_tensor("cumd", [128, 2 * 512], F32))
        vgsc = smalls.enter_context(nc.sbuf_tensor("vgsc", [128, 512], F32))
        cnvs = smalls.enter_context(nc.sbuf_tensor("cnvs", [128, 512], F32))
        stg1 = smalls.enter_context(nc.sbuf_tensor("stg1", [128, 512], F32))
        stg2 = smalls.enter_context(nc.sbuf_tensor("stg2", [128, 512], F32))
        stg3 = smalls.enter_context(nc.sbuf_tensor("stg3", [128, 512], F32))

        cum_store = {0: 0, 1: 0}
        for rc in range(NCH):
            lc, lv = load_attn_col(wut_rc, rc, rc % 2, NCH)
            WT("tensor", lc, lv)
            slot = rc % 2
            for s4 in range(NSUB):
                b = s4
                use_bank(b)
                um = None
                for c in range(NCH):
                    um = MM(ps[b][:], wgb[:, slot * 2048 + c * 128: slot * 2048 + (c + 1) * 128],
                            xn2sl(c, s4 * TS, (s4 + 1) * TS),
                            start=(c == 0), stop=(c == NCH - 1), inc=(c == NCH - 1))
                WT("vector", ct, um)
                ug = OP("vector", lambda En, rc=rc, s4=s4, b=b: En.scalar_tensor_tensor(
                    out=ugsc[:], in0=ps[b][:], scalar=bu_sb[:, rc:rc + 1],
                    in1=gates[:, rc * T + s4 * TS: rc * T + (s4 + 1) * TS],
                    op0=ALU.add, op1=ALU.mult))
                release_bank(b, cv, ug)
                half = s4 % 2
                if s4 >= 2:
                    WT("vector", sx, cum_store[half])
                if s4 == 0:
                    scn = OP("vector", lambda En, half=half: En.tensor_tensor_scan(
                        out=cumd[:, half * 512:(half + 1) * 512], data0=onesc[:], data1=ugsc[:],
                        initial=0.0, op0=ALU.mult, op1=ALU.add))
                else:
                    ohalf = 1 - half
                    scn = OP("vector", lambda En, half=half, ohalf=ohalf: En.tensor_tensor_scan(
                        out=cumd[:, half * 512:(half + 1) * 512], data0=onesc[:], data1=ugsc[:],
                        initial=cumd[:, ohalf * 512 + 511: ohalf * 512 + 512],
                        op0=ALU.mult, op1=ALU.add))
                WT("sync", cv, scn)
                cum_store[half] = DMA("sync", sx, cum_r[:, rc, s4 * TS:(s4 + 1) * TS],
                                      cumd[:, half * 512:(half + 1) * 512])
            OP("vector", lambda En, rc=rc: En.tensor_copy(
                pay_sb[:, rc:rc + 1], cumd[:, 3 % 2 * 512 + 511: 3 % 2 * 512 + 512]))
            att_upfree[slot].append((ct, ct.n))
        paydone = cv.n

        pay_w = sx.n
        def exch_sync(En):
            with En.register("rp1") as rp1, En.register("rp2") as rp2:
                En.reg_load(rp1, par_sb[0:1, 0:1])
                En.reg_load(rp2, par_sb[0:1, 1:2])
                pv1 = En.snap(rp1, min_val=0, max_val=1)
                pv2 = En.snap(rp2, min_val=0, max_val=128)
                En.dma_start(out=shm.ap()[bass.ds(pv2, 128), :], in_=pay_sb[:]).then_inc(sx.h, 16)
                En.wait_ge(sx.h, pay_w + 16)
                En.dma_start(out=flg.ap()[bass.ds(pv1, 1), :], in_=ep_sb[:]).then_inc(sx.h, 16)
                En.wait_ge(sx.h, pay_w + 32)
        WT("sync", cv, paydone)
        EM("sync", exch_sync)
        sx.n += 32
        flag_w = sx.n

        def exch_poll(En):
            with En.register("pr") as pr, En.register("pe") as pe:
                En.reg_load(pe, ep_sb[0:1, 0:1])
                for q in (0, 1):
                    En.reg_mov(pr, 1)
                    with En.While(pr):
                        En.load(pr, flg.ap()[q:q + 1, 0:1])
                        En.reg_sub(pr, pr, pe)
        WT("gpsimd", cg, cg.n)
        EM("gpsimd", exch_poll)
        vA = DMA("gpsimd", sg, slA_sb[:], shm.ap()[0:128, :])
        vB = DMA("gpsimd", sg, slB_sb[:], shm.ap()[128:256, :])
        WT("vector", sg, vB)
        OP("vector", lambda En: En.tensor_scalar(
            out=offs_sb[:], in0=slA_sb[:, 0:16], scalar1=modd_sb[:, 0:1],
            scalar2=0.0, op0=ALU.mult, op1=ALU.add))
        for c in range(NCH):
            OP("vector", lambda En, c=c: En.tensor_scalar(
                out=xn2e[:, c * 2056: c * 2056 + 1], in0=slA_sb[:, 32 + c:33 + c],
                scalar1=modd_sb[:, 0:1], scalar2=0.0, op0=ALU.mult, op1=ALU.add))
            OP("vector", lambda En, c=c: En.tensor_scalar(
                out=xn2e[:, c * 2056 + 2049: c * 2056 + 2050], in0=slB_sb[:, 16 + c:17 + c],
                scalar1=mevn_sb[:, 0:1], scalar2=0.0, op0=ALU.mult, op1=ALU.add))
        exch_done = cv.n

        for rc in range(NCH):
            lc, lv = load_attn_col(wvt_rc, rc, rc % 2, NCH)
            WT("tensor", lc, lv)
            slot = rc % 2
            for s4 in range(NSUB):
                b = s4
                use_bank(b)
                vm = None
                for c in range(NCH):
                    vm = MM(ps[b][:], wgb[:, slot * 2048 + c * 128: slot * 2048 + (c + 1) * 128],
                            xn2sl(c, s4 * TS, (s4 + 1) * TS),
                            start=(c == 0), stop=(c == NCH - 1), inc=(c == NCH - 1))
                WT("sync", cv, cv.n)
                cl = DMA("sync", sx, stg3[:], cum_r[:, rc, s4 * TS:(s4 + 1) * TS])
                WT("vector", ct, vm)
                vg = OP("vector", lambda En, rc=rc, s4=s4, b=b: En.scalar_tensor_tensor(
                    out=vgsc[:], in0=ps[b][:], scalar=bv_sb[:, rc:rc + 1],
                    in1=gates[:, rc * T + s4 * TS: rc * T + (s4 + 1) * TS],
                    op0=ALU.add, op1=ALU.mult))
                release_bank(b, cv, vg)
                WT("vector", sx, cl)
                WT("vector", cv, exch_done)
                go = OP("vector", lambda En, rc=rc, s4=s4: En.scalar_tensor_tensor(
                    out=gates[:, rc * T + s4 * TS: rc * T + (s4 + 1) * TS],
                    in0=stg3[:], scalar=offs_sb[:, rc:rc + 1], in1=vgsc[:],
                    op0=ALU.add, op1=ALU.mult))
            att_upfree[slot].append((ct, ct.n))
        barrier()
        if DEBUG:
            for c in range(NCH):
                DMA("sync", sx, dgo.rearrange("(c p) t -> p c t", p=128)[:, c, :],
                    gates[:, c * T:(c + 1) * T])
            barrier()

        for dc in range(NCH):
            lc, lv = load_attn_col(wut_rc, dc, dc % 2, NCH)
            WT("tensor", lc, lv)
            slot = dc % 2
            for s4 in range(NSUB):
                b = 4 + 2 * (dc % 2) + (s4 % 2)
                use_bank(b)
                uvm = None
                for rc in range(NCH):
                    uvm = MM(ps[b][:], wgb[:, slot * 2048 + rc * 128: slot * 2048 + (rc + 1) * 128],
                             gates[:, rc * T + s4 * TS: rc * T + (s4 + 1) * TS],
                             start=(rc == 0), stop=(rc == NCH - 1), inc=(rc == NCH - 1))
                base = dc * 2056 + 1 + s4 * TS
                cva = OP("vector", lambda En, dc=dc, base=base: En.tensor_scalar(
                    out=cnvs[:], in0=xn2e[:, base - 1: base - 1 + TS],
                    scalar1=cw_sb[:, dc * 3: dc * 3 + 1], scalar2=cb_sb[:, dc:dc + 1],
                    op0=ALU.mult, op1=ALU.add))
                OP("vector", lambda En, dc=dc, base=base: En.scalar_tensor_tensor(
                    out=cnvs[:], in0=xn2e[:, base: base + TS],
                    scalar=cw_sb[:, dc * 3 + 1: dc * 3 + 2], in1=cnvs[:],
                    op0=ALU.mult, op1=ALU.add))
                OP("vector", lambda En, dc=dc, base=base: En.scalar_tensor_tensor(
                    out=cnvs[:], in0=xn2e[:, base + 1: base + 1 + TS],
                    scalar=cw_sb[:, dc * 3 + 2: dc * 3 + 3], in1=cnvs[:],
                    op0=ALU.mult, op1=ALU.add))
                WT("sync", cv, cv.n)
                xl = DMA("sync", sx, stg1[:], x1_r[:, dc, s4 * TS:(s4 + 1) * TS])
                WT("vector", ct, uvm)
                OP("vector", lambda En, b=b: En.tensor_add(cnvs[:], cnvs[:], ps[b][:]))
                x2v0 = cv.n
                release_bank(b, cv, x2v0)
                WT("vector", sx, xl)
                x2v = OP("vector", lambda En: En.tensor_add(stg2[:], cnvs[:], stg1[:]))
                WT("sync", cv, x2v)
                dst = x2_r if stage >= 3 else y_r
                DMA("sync", sx, dst[:, dc, s4 * TS:(s4 + 1) * TS], stg2[:])
            att_upfree[slot].append((ct, ct.n))
        barrier()
        smalls.close()
        attn_scope.close()

        if stage >= 3:
            gsb = open_ffn_scope()
            xn_sbuf, acc_sbuf, hbuf = gsb["xn"], gsb["acc"], gsb["h"]
            xring, xn2ring = gsb["xr"], gsb["x2r"]
            t1sc, t2sc, cbc = gsb["t1"], gsb["t2"], gsb["cbc"]
            emit_ffn(1, x2_r, xn_sbuf, acc_sbuf)
            barrier()
            ffn_scope.close()

    if DEBUG and stage >= 2:
        DMA("sync", sx, dx1, x1_s.ap())
        DMA("sync", sx, dxn2, xn2_s.ap())
        DMA("sync", sx, dcum, cum_s.ap())
    WT("sync", sx, sx.n)
    for c in ALLC:
        WT("sync", c, c.n)

    with nc.Block() as block:
        @block.sync
        def _(En):
            for f in Q["sync"]:
                f(En)

        @block.gpsimd
        def _(En):
            for f in Q["gpsimd"]:
                f(En)

        @block.vector
        def _(En):
            for f in Q["vector"]:
                f(En)

        @block.scalar
        def _(En):
            for f in Q["scalar"]:
                f(En)

        @block.tensor
        def _(En):
            for f in Q["tensor"]:
                f(En)

    est.close()
    return nc



def kernel(L0_attn_x, L0_attn_Wu, L0_attn_bu, L0_attn_Wv, L0_attn_bv,
           L0_attn_Wg1, L0_attn_bg1, L0_attn_Wg2, L0_attn_bg2,
           L0_attn_conv_w, L0_attn_conv_b,
           L0_ffn_Wr, L0_ffn_Wg, L0_ffn_Wu, L0_ffn_Wd):
    x = np.ascontiguousarray(np.asarray(L0_attn_x, dtype=np.float32))
    B, L, Dm = x.shape
    assert (B, L, Dm) == (4, 4096, 2048)

    wg = np.ascontiguousarray(np.asarray(L0_ffn_Wg, dtype=np.float32))
    wu = np.ascontiguousarray(np.asarray(L0_ffn_Wu, dtype=np.float32))
    wd = np.ascontiguousarray(np.asarray(L0_ffn_Wd, dtype=np.float32))
    wr = np.ascontiguousarray(np.asarray(L0_ffn_Wr, dtype=np.float32)
                              .reshape(16, 128, 8).transpose(1, 0, 2).reshape(128, 128))
    wut = np.ascontiguousarray(np.asarray(L0_attn_Wu, dtype=np.float32).T)
    wvt = np.ascontiguousarray(np.asarray(L0_attn_Wv, dtype=np.float32).T)
    wg1t = np.ascontiguousarray(np.asarray(L0_attn_Wg1, dtype=np.float32).T)
    wg2t = np.ascontiguousarray(np.asarray(L0_attn_Wg2, dtype=np.float32).T)
    def chan_swz(a, nch):
        a = np.asarray(a, dtype=np.float32)
        return np.ascontiguousarray(a.reshape(nch, 128).T)
    bu = chan_swz(L0_attn_bu, 16)
    bv = chan_swz(L0_attn_bv, 16)
    bg1 = chan_swz(L0_attn_bg1, 4)
    bg2 = chan_swz(L0_attn_bg2, 16)
    cw = np.ascontiguousarray(np.asarray(L0_attn_conv_w, dtype=np.float32)
                              .reshape(16, 128, 3).transpose(1, 0, 2).reshape(128, 48))
    cb = chan_swz(L0_attn_conv_b, 16)
    ident = np.eye(128, dtype=np.float32)
    esel = np.zeros((8, 8 * 128), dtype=np.float32)
    for e_ in range(8):
        esel[e_, e_ * 128:(e_ + 1) * 128] = 1.0
    epoch = np.full((1, 16), (int(time.time() * 1000) % (1 << 29)) | 1, np.int32)

    nc = build(STAGE)
    in_maps = []
    for c in range(NCORES):
        b = c // 2
        h = c % 2
        xs = np.ascontiguousarray(x[b, h * T:(h + 1) * T, :].T)
        par = np.zeros((1, 16), np.int32)
        par[0, 0] = h
        par[0, 1] = h * 128
        modd = np.full((128, 1), 1.0 if h == 1 else 0.0, np.float32)
        mevn = np.full((128, 1), 1.0 if h == 0 else 0.0, np.float32)
        in_maps.append({
            "xin": xs, "fwg": wg, "fwu": wu, "fwd": wd, "fwr": wr,
            "awut": wut, "awvt": wvt, "awg1t": wg1t, "awg2t": wg2t,
            "abu": bu, "abv": bv, "abg1": bg1, "abg2": bg2,
            "acw": cw, "acb": cb, "ident": ident, "esel": esel, "par": par,
            "epoch": epoch, "modd": modd, "mevn": mevn,
        })
    res = run_bass_kernel_spmd(nc, in_maps, core_ids=list(range(NCORES)))
    out = np.empty((4, 4096, 2048), np.float32)
    for c in range(NCORES):
        b = c // 2
        h = c % 2
        out[b, h * T:(h + 1) * T, :] = res.results[c]["yout"].T
    return out


# version 06
# speedup vs baseline: 1.2426x; 1.2426x over previous
import time
import numpy as np
from contextlib import ExitStack

import concourse.bass as bass
import concourse.mybir as mybir
from concourse.bass_utils import run_bass_kernel_spmd

F32 = mybir.dt.float32
BF16 = mybir.dt.bfloat16
I32 = mybir.dt.int32
AF = mybir.ActivationFunctionType
ALU = mybir.AluOpType

NCORES = 8
D = 2048
NCH = 16
T = 2048
E = 8
IFF = 1408
NI = 11
RT = 1024
TS = 512
NSUB = 4
EPS = 1e-5

STAGE = 3
DEBUG = False



def build(stage=STAGE):
    nc = bass.Bass("TRN2", target_bir_lowering=False)

    xin = nc.dram_tensor("xin", [D, T], F32, kind="ExternalInput").ap()
    wg_h = nc.dram_tensor("fwg", [E, D, IFF], F32, kind="ExternalInput").ap()
    wu_h = nc.dram_tensor("fwu", [E, D, IFF], F32, kind="ExternalInput").ap()
    wd_h = nc.dram_tensor("fwd", [E, IFF, D], F32, kind="ExternalInput").ap()
    wr_h = nc.dram_tensor("fwr", [128, 128], F32, kind="ExternalInput").ap()
    wut_h = nc.dram_tensor("awut", [D, D], F32, kind="ExternalInput").ap()
    wvt_h = nc.dram_tensor("awvt", [D, D], F32, kind="ExternalInput").ap()
    wg1t_h = nc.dram_tensor("awg1t", [D, 512], F32, kind="ExternalInput").ap()
    wg2t_h = nc.dram_tensor("awg2t", [512, D], F32, kind="ExternalInput").ap()
    bu_h = nc.dram_tensor("abu", [128, 16], F32, kind="ExternalInput").ap()
    bv_h = nc.dram_tensor("abv", [128, 16], F32, kind="ExternalInput").ap()
    bg1_h = nc.dram_tensor("abg1", [128, 4], F32, kind="ExternalInput").ap()
    bg2_h = nc.dram_tensor("abg2", [128, 16], F32, kind="ExternalInput").ap()
    cw_h = nc.dram_tensor("acw", [128, 48], F32, kind="ExternalInput").ap()
    cb_h = nc.dram_tensor("acb", [128, 16], F32, kind="ExternalInput").ap()
    ident_h = nc.dram_tensor("ident", [128, 128], F32, kind="ExternalInput").ap()
    esel_h = nc.dram_tensor("esel", [8, 8 * 128], F32, kind="ExternalInput").ap()
    par_h = nc.dram_tensor("par", [1, 16], I32, kind="ExternalInput").ap()
    epoch_h = nc.dram_tensor("epoch", [1, 16], I32, kind="ExternalInput").ap()
    modd_h = nc.dram_tensor("modd", [128, 1], F32, kind="ExternalInput").ap()
    mevn_h = nc.dram_tensor("mevn", [128, 1], F32, kind="ExternalInput").ap()
    yout = nc.dram_tensor("yout", [D, T], F32, kind="ExternalOutput").ap()
    if DEBUG:
        dx1 = nc.dram_tensor("dx1", [D, T], F32, kind="ExternalOutput").ap()
        dxn2 = nc.dram_tensor("dxn2", [D, T], BF16, kind="ExternalOutput").ap()
        dcum = nc.dram_tensor("dcum", [D, T], F32, kind="ExternalOutput").ap()
        dgo = nc.dram_tensor("dgo", [D, T], BF16, kind="ExternalOutput").ap()

    x1_s = nc.dram_tensor("x1_s", [D, T], F32)
    x2_s = nc.dram_tensor("x2_s", [D, T], F32)
    xn2_s = nc.dram_tensor("xn2_s", [D, T], BF16)
    cum_s = nc.dram_tensor("cum_s", [D, T], F32)
    wcu = nc.dram_tensor("wcu", [E * 2 * NI, NCH * 128 * 128], BF16)
    wcd = nc.dram_tensor("wcd", [E * NCH, NI * 128 * 128], BF16)
    shm = nc.dram_tensor("shmx", [256, 48], F32, addr_space="Shared")
    flg = nc.dram_tensor("flgx", [2, 16], I32, addr_space="Shared")

    xin_r = xin.rearrange("(c p) t -> p c t", p=128)
    y_r = yout.rearrange("(c p) t -> p c t", p=128)
    x1_r = x1_s.ap().rearrange("(c p) t -> p c t", p=128)
    x2_r = x2_s.ap().rearrange("(c p) t -> p c t", p=128)
    xn2_r = xn2_s.ap().rearrange("(c p) t -> p c t", p=128)
    cum_r = cum_s.ap().rearrange("(c p) t -> p c t", p=128)
    wg_r = wg_h.rearrange("e (c p) (i f) -> e i p c f", p=128, f=128)
    wu_r = wu_h.rearrange("e (c p) (i f) -> e i p c f", p=128, f=128)
    wd_r = wd_h.rearrange("e (ci p) (d f) -> e d p ci f", p=128, f=128)
    wut_rc = wut_h.rearrange("(c p) (rc f) -> rc p c f", p=128, f=128)
    wvt_rc = wvt_h.rearrange("(c p) (rc f) -> rc p c f", p=128, f=128)
    wg1_rc = wg1t_h.rearrange("(c p) (hc f) -> hc p c f", p=128, f=128)
    wg2_rc = wg2t_h.rearrange("(hc p) (rc f) -> rc p hc f", p=128, f=128)
    wcu_v = wcu.ap().rearrange("n (p f) -> n p f", p=128)
    wcd_v = wcd.ap().rearrange("n (p f) -> n p f", p=128)

    ENGS = ("sync", "gpsimd", "vector", "scalar", "tensor")
    Q = {e: [] for e in ENGS}

    def EM(eng, fn):
        Q[eng].append(fn)

    class Ctr:
        def __init__(self, name):
            self.cm = nc.semaphore(name)
            self.h = self.cm.__enter__()
            self.n = 0

    sx = Ctr("sx")
    sc = Ctr("sc")
    sg = Ctr("sg")
    cv = Ctr("cv")
    cs = Ctr("cs")
    cg = Ctr("cg")
    ct = Ctr("ct")
    ALLC = [sx, sc, sg, cv, cs, cg, ct]

    def WT(eng, ctr, val):
        if val <= 0:
            return
        EM(eng, lambda En, h=ctr.h, v=val: En.wait_ge(h, v))

    def DMA(eng, ctr, out, in_):
        def f(En, o=out, i=in_, h=ctr.h):
            En.dma_start(out=o, in_=i).then_inc(h, 16)
        EM(eng, f)
        ctr.n += 16
        return ctr.n

    ENG2CTR = {"vector": cv, "scalar": cs, "gpsimd": cg, "tensor": ct}

    def OP(eng, fn, flush=True):
        ctr = ENG2CTR[eng]
        if flush and ctr.n > 0:
            WT(eng, ctr, ctr.n)
        def g(En, fn=fn, h=ctr.h):
            fn(En).then_inc(h, 1)
        EM(eng, g)
        ctr.n += 1
        return ctr.n

    def MM(out, lhsT, rhs, start, stop, inc=False):
        if inc:
            def f(En, o=out, l=lhsT, r=rhs, st=start, sp=stop, h=ct.h):
                En.matmul(o, l, r, start=st, stop=sp).then_inc(h, 1)
            EM("tensor", f)
            ct.n += 1
            return ct.n
        else:
            def f(En, o=out, l=lhsT, r=rhs, st=start, sp=stop):
                En.matmul(o, l, r, start=st, stop=sp)
            EM("tensor", f)
            return None

    def barrier():
        for e in ENGS:
            for c in ALLC:
                WT(e, c, c.n)

    bank_free = [[] for _ in range(8)]

    def use_bank(b):
        for (c, v) in bank_free[b]:
            WT("tensor", c, v)
        bank_free[b] = []

    def release_bank(b, ctr, val):
        bank_free[b].append((ctr, val))

    est = ExitStack()

    def sb(name, shape, dt):
        return est.enter_context(nc.sbuf_tensor(name, shape, dt))

    bu_sb = sb("bu_sb", [128, 16], F32)
    bv_sb = sb("bv_sb", [128, 16], F32)
    bg2_sb = sb("bg2_sb", [128, 16], F32)
    cb_sb = sb("cb_sb", [128, 16], F32)
    bg1_sb = sb("bg1_sb", [128, 4], F32)
    cw_sb = sb("cw_sb", [128, 48], F32)
    wr_sb = sb("wr_sb", [128, 128], F32)
    wrb_sb = sb("wrb_sb", [128, 128], BF16)
    id_sb = sb("id_sb", [128, 128], F32)
    esel_sb = sb("esel_sb", [8, 8 * 128], F32)
    ones_b = sb("ones_b", [128, 1], BF16)
    ones1 = sb("ones1", [1, 128], F32)
    onesc = sb("onesc", [128, 512], F32)
    crows = sb("crows", [8, 2048], F32)
    lnsm = sb("lnsm", [1, 4 * 512], F32)
    lg_sb = sb("lg_sb", [8, 512], F32)
    tm_sb = sb("tm_sb", [128, 32], F32)
    tm2_sb = sb("tm2_sb", [128, 32], F32)
    pay_sb = sb("pay_sb", [128, 48], F32)
    slA_sb = sb("slA_sb", [128, 48], F32)
    slB_sb = sb("slB_sb", [128, 48], F32)
    offs_sb = sb("offs_sb", [128, 16], F32)
    modd_sb = sb("modd_sb", [128, 1], F32)
    mevn_sb = sb("mevn_sb", [128, 1], F32)
    par_sb = sb("par_sb", [1, 16], I32)
    ep_sb = sb("ep_sb", [1, 16], I32)
    wgb = sb("wgb", [128, 2 * 2048], BF16)
    wub = sb("wub", [128, 2 * 2048], BF16)
    wdb = sb("wdb", [128, 2 * 1408], BF16)
    xbfc = sb("xbfc", [128, 2 * 512], BF16)
    epsv = sb("epsv", [1, 1], F32)
    sqfc = sb("sqfc", [128, 2 * 512], BF16)

    ps = [est.enter_context(nc.psum_tensor(f"ps{i}", [128, 512], F32)) for i in range(8)]

    small_loads = [
        (bu_sb[:], bu_h), (bv_sb[:], bv_h), (bg2_sb[:], bg2_h), (cb_sb[:], cb_h),
        (bg1_sb[:], bg1_h), (cw_sb[:], cw_h), (wr_sb[:], wr_h), (id_sb[:], ident_h),
        (modd_sb[:], modd_h), (mevn_sb[:], mevn_h), (par_sb[:], par_h), (ep_sb[:], epoch_h),
        (esel_sb[:], esel_h),
    ]
    for dst, srcv in small_loads:
        DMA("sync", sx, dst, srcv)
    v_pro = sx.n
    OP("vector", lambda En: En.memset(ones_b[:], 1.0))
    OP("vector", lambda En: En.memset(ones1[:], 1.0))
    OP("vector", lambda En: En.memset(onesc[:], 1.0))
    OP("vector", lambda En: En.memset(epsv[:], EPS))
    WT("vector", sx, v_pro)
    OP("vector", lambda En: En.tensor_copy(wrb_sb[:], wr_sb[:]))
    pro_done = cv.n

    xn2war = {0: 0, 1: 0}
    xn2war_c = [None]
    PREV_DN = [0]
    BGQ = []

    def drain_bg(n=1):
        for _ in range(n):
            if not BGQ:
                return
            BGQ.pop(0)()

    def flush_bg():
        while BGQ:
            BGQ.pop(0)()

    def ln_thunks(chunk_src, xn_dst, c_off, with_router, src_guard=None,
                  post_norm=None, stq=None):
        th = []
        stq_e, stq_c = stq
        state = {"war": {0: 0, 1: 0, 2: 0, 3: 0}, "ld": {}, "bmm": 0, "norm_done": 0}

        def stats_group(c0):
            def f(c0=c0):
                if c0 == 0:
                    use_bank(6)
                    use_bank(7)
                    if src_guard is not None:
                        WT(stq_e, src_guard[0], src_guard[1])
                for c in range(c0, c0 + 4):
                    sl = c % 2
                    WT(stq_e, cv, state["war"][sl])
                    lv = DMA(stq_e, stq_c, xring[:, sl * 512:(sl + 1) * 512], chunk_src(c))
                    WT("vector", stq_c, lv)
                    WT("vector", ct, state["ld"].get(c - 2, 0))
                    vcp = OP("vector", lambda En, sl=sl: En.tensor_copy(
                        xbfc[:, sl * 512:(sl + 1) * 512], xring[:, sl * 512:(sl + 1) * 512]))
                    state["war"][sl] = vcp
                    WT("scalar", ct, state["ld"].get(c - 2, 0))
                    WT("scalar", cv, vcp)
                    scp = OP("scalar", lambda En, sl=sl: En.activation(
                        out=sqfc[:, sl * 512:(sl + 1) * 512],
                        in_=xbfc[:, sl * 512:(sl + 1) * 512], func=AF.Square))
                    WT("tensor", cv, vcp)
                    WT("tensor", cs, scp)
                    MM(ps[6][0:1, :], ones_b[:], xbfc[:, sl * 512:(sl + 1) * 512],
                       start=(c == 0), stop=(c == NCH - 1), inc=False)
                    state["ld"][c] = MM(ps[7][0:1, :], ones_b[:], sqfc[:, sl * 512:(sl + 1) * 512],
                                        start=(c == 0), stop=(c == NCH - 1), inc=True)
                    if c + 2 <= NCH - 1:
                        pass
            return f

        def stats_group_guarded(c0):
            base = stats_group(c0)
            def f():
                base()
            return f

        for c0 in (0, 4, 8, 12):
            th.append(stats_group(c0))

        def meanvar():
            stats_done = ct.n
            WT("scalar", ct, stats_done)
            s1 = OP("scalar", lambda En: En.mul(lnsm[0:1, 0:512], ps[6][0:1, :], 1.0 / D))
            s2 = OP("scalar", lambda En: En.mul(lnsm[0:1, 512:1024], ps[7][0:1, :], 1.0 / D))
            release_bank(6, cs, s2)
            release_bank(7, cs, s2)
            WT("vector", cs, s2)
            OP("vector", lambda En: En.tensor_mul(
                lnsm[0:1, 1536:2048], lnsm[0:1, 0:512], lnsm[0:1, 0:512]))
            OP("vector", lambda En: En.tensor_sub(
                lnsm[0:1, 512:1024], lnsm[0:1, 512:1024], lnsm[0:1, 1536:2048]))
            vv = cv.n
            WT("scalar", cv, vv)
            sd = OP("scalar", lambda En: En.activation(
                out=lnsm[0:1, 1024:1536], in_=lnsm[0:1, 512:1024], func=AF.Sqrt,
                bias=epsv[0:1, 0:1]))
            WT("vector", cs, sd)
            rst = OP("vector", lambda En: En.reciprocal(lnsm[0:1, 1024:1536], lnsm[0:1, 1024:1536]))
            use_bank(6)
            use_bank(7)
            WT("tensor", cv, rst)
            MM(ps[6][:], ones1[:], lnsm[0:1, 0:512], start=True, stop=True, inc=False)
            state["bmm"] = MM(ps[7][:], ones1[:], lnsm[0:1, 1024:1536],
                              start=True, stop=True, inc=True)
        th.append(meanvar)

        def norm_group(c0):
            def f(c0=c0):
                if c0 == 0:
                    WT("vector", ct, state["bmm"])
                for c in range(c0, c0 + 4):
                    sl = 2 + c % 2
                    WT(stq_e, cv, state["war"][sl])
                    lv = DMA(stq_e, stq_c, xring[:, sl * 512:(sl + 1) * 512], chunk_src(c))
                    WT("vector", stq_c, lv)
                    OP("vector", lambda En, sl=sl: En.tensor_sub(
                        xring[:, sl * 512:(sl + 1) * 512],
                        xring[:, sl * 512:(sl + 1) * 512], ps[6][:]))
                    if xn_dst is not None:
                        mv = OP("vector", lambda En, sl=sl, d=xn_dst(c): En.tensor_mul(
                            d, xring[:, sl * 512:(sl + 1) * 512], ps[7][:]))
                        state["war"][sl] = mv
                    else:
                        rsl = c % 2
                        WT("vector", xn2war_c[0], xn2war[rsl])
                        mv = OP("vector", lambda En, sl=sl, rsl=rsl: En.tensor_mul(
                            xn2ring[:, rsl * 512:(rsl + 1) * 512],
                            xring[:, sl * 512:(sl + 1) * 512], ps[7][:]))
                        state["war"][sl] = mv
                        post_norm(c, rsl, mv)
                if c0 == 12:
                    state["norm_done"] = cv.n
                    release_bank(6, cv, cv.n)
                    release_bank(7, cv, cv.n)
            return f

        for c0 in (0, 4, 8, 12):
            th.append(norm_group(c0))

        if with_router:
            def router1():
                use_bank(6)
                WT("tensor", cv, state["norm_done"])
                lgd = None
                for c in range(NCH):
                    lgd = MM(ps[6][0:8, :], wrb_sb[:, c * 8:(c + 1) * 8], xn_dst(c),
                             start=(c == 0), stop=(c == NCH - 1), inc=(c == NCH - 1))
                WT("scalar", ct, lgd)
                lge = OP("scalar", lambda En: En.copy(lg_sb[:], ps[6][0:8, :]))
                release_bank(6, cs, lge)
                use_bank(7)
                for j in range(4):
                    WT("tensor", cs, cs.n)
                    tj = MM(ps[7][0:128, 0:8], lg_sb[0:8, j * 128:(j + 1) * 128],
                            id_sb[0:8, 0:8], start=True, stop=True, inc=True)
                    WT("scalar", ct, tj)
                    OP("scalar", lambda En, j=j: En.copy(tm_sb[:, j * 8:(j + 1) * 8],
                                                         ps[7][0:128, 0:8]))
                release_bank(7, cs, cs.n)
            th.append(router1)

            def router2():
                WT("scalar", cs, cs.n)
                ex = OP("scalar", lambda En: En.activation(out=tm_sb[:], in_=tm_sb[:], func=AF.Exp))
                WT("vector", cs, ex)
                for j in range(4):
                    b0 = j * 8
                    OP("vector", lambda En, b0=b0: En.reduce_max(
                        offs_sb[:, 0:1], tm_sb[:, b0:b0 + 8], axis=mybir.AxisListType.X))
                    OP("vector", lambda En, b0=b0: En.tensor_scalar(
                        out=tm2_sb[:, b0:b0 + 8], in0=tm_sb[:, b0:b0 + 8],
                        scalar1=offs_sb[:, 0:1], scalar2=1.0, op0=ALU.is_equal, op1=ALU.mult))
                    OP("vector", lambda En, b0=b0: En.scalar_tensor_tensor(
                        out=tm2_sb[:, b0:b0 + 8], in0=tm2_sb[:, b0:b0 + 8], scalar=-1e30,
                        in1=tm_sb[:, b0:b0 + 8], op0=ALU.mult, op1=ALU.add))
                    OP("vector", lambda En, b0=b0: En.reduce_max(
                        offs_sb[:, 1:2], tm2_sb[:, b0:b0 + 8], axis=mybir.AxisListType.X))
                    OP("vector", lambda En: En.tensor_add(offs_sb[:, 2:3], offs_sb[:, 0:1], offs_sb[:, 1:2]))
                    OP("vector", lambda En: En.reciprocal(offs_sb[:, 2:3], offs_sb[:, 2:3]))
                    OP("vector", lambda En, b0=b0: En.tensor_scalar(
                        out=tm2_sb[:, b0:b0 + 8], in0=tm_sb[:, b0:b0 + 8],
                        scalar1=offs_sb[:, 1:2], scalar2=1.0, op0=ALU.is_ge, op1=ALU.mult))
                    OP("vector", lambda En, b0=b0: En.tensor_mul(
                        tm2_sb[:, b0:b0 + 8], tm2_sb[:, b0:b0 + 8], tm_sb[:, b0:b0 + 8]))
                    OP("vector", lambda En, b0=b0: En.tensor_scalar(
                        out=tm2_sb[:, b0:b0 + 8], in0=tm2_sb[:, b0:b0 + 8],
                        scalar1=offs_sb[:, 2:3], scalar2=0.5, op0=ALU.mult, op1=ALU.mult))
                c2done = cv.n
                use_bank(6)
                for j in range(4):
                    WT("tensor", cv, c2done)
                    if j > 0:
                        WT("tensor", cs, cs.n)
                    tj = MM(ps[6][0:8, 0:128], tm2_sb[:, j * 8:(j + 1) * 8],
                            id_sb[:, :], start=True, stop=True, inc=True)
                    WT("scalar", ct, tj)
                    OP("scalar", lambda En, j=j: En.copy(
                        crows[0:8, c_off + j * 128: c_off + (j + 1) * 128], ps[6][0:8, 0:128]))
                release_bank(6, cs, cs.n)
            th.append(router2)
        return th

    upfree = {0: [], 1: []}
    dnfree = {0: [], 1: []}

    def load_up_col(call, rnd, e, i, cast_mode):
        slot = i % 2
        for (cc, vv2) in upfree[slot]:
            WT("gpsimd" if cast_mode else "scalar", cc, vv2)
        upfree[slot] = []
        wg_dst = wgb[:, slot * 2048:(slot + 1) * 2048].rearrange("p (c f) -> p c f", f=128)
        wu_dst = wub[:, slot * 2048:(slot + 1) * 2048].rearrange("p (c f) -> p c f", f=128)
        if cast_mode:
            v1 = DMA("gpsimd", sg, wg_dst, wg_r[e, i])
            v2 = DMA("gpsimd", sg, wu_dst, wu_r[e, i])
            WT("scalar", sg, v2)
            cw1 = DMA("scalar", sc, wcu_v[(e * 2 + 0) * NI + i],
                      wgb[:, slot * 2048:(slot + 1) * 2048])
            cw2 = DMA("scalar", sc, wcu_v[(e * 2 + 1) * NI + i],
                      wub[:, slot * 2048:(slot + 1) * 2048])
            return (sg, v2), (sc, cw2)
        else:
            v1 = DMA("scalar", sc, wgb[:, slot * 2048:(slot + 1) * 2048],
                     wcu_v[(e * 2 + 0) * NI + i])
            v2 = DMA("scalar", sc, wub[:, slot * 2048:(slot + 1) * 2048],
                     wcu_v[(e * 2 + 1) * NI + i])
            return (sc, v2), None

    def load_dn_col(call, rnd, e, d, cast_mode):
        slot = d % 2
        for (cc, vv2) in dnfree[slot]:
            WT("gpsimd" if cast_mode else "scalar", cc, vv2)
        dnfree[slot] = []
        wd_dst = wdb[:, slot * 1408:(slot + 1) * 1408].rearrange("p (ci f) -> p ci f", f=128)
        if cast_mode:
            v1 = DMA("gpsimd", sg, wd_dst, wd_r[e, d])
            WT("scalar", sg, v1)
            cwv = DMA("scalar", sc, wcd_v[e * NCH + d],
                      wdb[:, slot * 1408:(slot + 1) * 1408])
            return (sg, v1), (sc, cwv)
        else:
            v1 = DMA("scalar", sc, wdb[:, slot * 1408:(slot + 1) * 1408],
                     wcd_v[e * NCH + d])
            return (sc, v1), None

    def fin_quick(call, rnd, src_r, acc_sb, stq):
        stq_e, stq_c = stq
        if call == 0 and stage >= 2:
            dst = x1_r
        else:
            dst = y_r
        war = {0: 0, 1: 0}
        last = 0
        for s in range(2):
            toff = rnd * RT + s * TS
            for c in range(NCH):
                sl = c % 2
                WT(stq_e, stq_c, war[sl])
                lv = DMA(stq_e, stq_c, xring[:, sl * 512:(sl + 1) * 512],
                         src_r[:, c, toff:toff + TS])
                WT("vector", stq_c, lv)
                av = OP("vector", lambda En, c=c, s=s, sl=sl: En.tensor_add(
                    xring[:, sl * 512:(sl + 1) * 512], xring[:, sl * 512:(sl + 1) * 512],
                    acc_sb[:, c * RT + s * TS: c * RT + (s + 1) * TS]))
                WT(stq_e, cv, av)
                war[sl] = DMA(stq_e, stq_c, dst[:, c, toff:toff + TS],
                              xring[:, sl * 512:(sl + 1) * 512])
                last = war[sl]
        return dst, last

    def fin_ln2_thunks(rnd, store_done, stq):
        ths = []
        stq_e, stq_c = stq
        xn2war_c[0] = stq_c
        for s in range(2):
            toff = rnd * RT + s * TS

            def post_norm(c, rsl, mv, toff=toff, rnd=rnd, s=s):
                if rnd == 0 and s == 0:
                    OP("vector", lambda En, c=c, rsl=rsl: En.tensor_copy(
                        pay_sb[:, 16 + c:17 + c], xn2ring[:, rsl * 512:rsl * 512 + 1]))
                if rnd == 1 and s == 1:
                    OP("vector", lambda En, c=c, rsl=rsl: En.tensor_copy(
                        pay_sb[:, 32 + c:33 + c], xn2ring[:, rsl * 512 + 511:rsl * 512 + 512]))
                WT(stq_e, cv, cv.n)
                xn2war[rsl] = DMA(stq_e, stq_c, xn2_r[:, c, toff:toff + TS],
                                  xn2ring[:, rsl * 512:(rsl + 1) * 512])

            ths += ln_thunks(
                lambda c, toff=toff: x1_r[:, c, toff:toff + TS],
                None, toff, False, src_guard=(stq_c, store_done), post_norm=post_norm,
                stq=stq)
        return ths

    def emit_experts(call, rnd, xn_r_sb, acc_sb):
        cast_mode = (call == 0 and rnd == 0)
        t0 = rnd * RT
        zz = OP("vector", lambda En: En.memset(acc_sb[:], 0.0))
        prev_dn_end = PREV_DN[0]
        for e in range(E):
            use_bank(4)
            use_bank(5)
            WT("tensor", cs, cs.n)
            MM(ps[4][:], esel_sb[0:8, e * 128:(e + 1) * 128], crows[0:8, t0:t0 + TS],
               start=True, stop=True, inc=False)
            cb2 = MM(ps[5][:], esel_sb[0:8, e * 128:(e + 1) * 128],
                     crows[0:8, t0 + TS:t0 + 2 * TS], start=True, stop=True, inc=True)
            WT("scalar", ct, cb2)
            cbe1 = OP("scalar", lambda En: En.copy(cbc[:, 0:512], ps[4][:]))
            cbe2 = OP("scalar", lambda En: En.copy(cbc[:, 512:1024], ps[5][:]))
            release_bank(4, cs, cbe2)
            release_bank(5, cs, cbe2)
            for i in range(NI):
                (lc, lv), cwp = load_up_col(call, rnd, e, i, cast_mode)
                WT("tensor", lc, lv)
                slot = i % 2
                for s in range(2):
                    use_bank(s)
                    use_bank(2 + s)
                    gmm = None
                    for c in range(NCH):
                        gmm = MM(ps[s][:], wgb[:, slot * 2048 + c * 128: slot * 2048 + (c + 1) * 128],
                                 xn_r_sb[:, c * T + t0 + s * TS: c * T + t0 + (s + 1) * TS],
                                 start=(c == 0), stop=(c == NCH - 1), inc=(c == NCH - 1))
                    for c in range(NCH):
                        umm = MM(ps[2 + s][:], wub[:, slot * 2048 + c * 128: slot * 2048 + (c + 1) * 128],
                                 xn_r_sb[:, c * T + t0 + s * TS: c * T + t0 + (s + 1) * TS],
                                 start=(c == 0), stop=(c == NCH - 1), inc=(c == NCH - 1))
                    WT("scalar", ct, gmm)
                    sil = OP("scalar", lambda En, s=s: En.activation(
                        out=t1sc[:, s * 512:(s + 1) * 512], in_=ps[s][:], func=AF.Silu))
                    release_bank(s, cs, sil)
                    WT("vector", cs, sil)
                    WT("vector", ct, umm)
                    t2v = OP("vector", lambda En, s=s: En.tensor_mul(
                        t2sc[:, s * 512:(s + 1) * 512], t1sc[:, s * 512:(s + 1) * 512], ps[2 + s][:]))
                    release_bank(2 + s, cv, t2v)
                    WT("vector", cs, cbe2)
                    if prev_dn_end:
                        WT("vector", ct, prev_dn_end)
                    hv = OP("vector", lambda En, i=i, s=s: En.tensor_mul(
                        hbuf[:, i * RT + s * TS: i * RT + (s + 1) * TS],
                        t2sc[:, s * 512:(s + 1) * 512], cbc[:, s * 512:(s + 1) * 512]))
                upfree[slot].append((ct, ct.n))
                if cwp is not None:
                    upfree[slot].append(cwp)
                drain_bg(1)
            hready = cv.n
            for d in range(NCH):
                (lc, lv), cwp = load_dn_col(call, rnd, e, d, cast_mode)
                WT("tensor", lc, lv)
                WT("tensor", cv, hready)
                slot = d % 2
                for s in range(2):
                    b = 4 + s
                    use_bank(b)
                    ymm = None
                    for ci in range(NI):
                        ymm = MM(ps[b][:], wdb[:, slot * 1408 + ci * 128: slot * 1408 + (ci + 1) * 128],
                                 hbuf[:, ci * RT + s * TS: ci * RT + (s + 1) * TS],
                                 start=(ci == 0), stop=(ci == NI - 1), inc=(ci == NI - 1))
                    WT("vector", ct, ymm)
                    av = OP("vector", lambda En, d=d, s=s, b=b: En.tensor_add(
                        acc_sb[:, d * RT + s * TS: d * RT + (s + 1) * TS],
                        acc_sb[:, d * RT + s * TS: d * RT + (s + 1) * TS], ps[b][:]))
                    release_bank(b, cv, av)
                dnfree[slot].append((ct, ct.n))
                if cwp is not None:
                    dnfree[slot].append(cwp)
                if d % 2 == 1:
                    drain_bg(1)
            prev_dn_end = ct.n
        PREV_DN[0] = ct.n

    def emit_ffn(call, src_r, xn_r_sb, acc_sb):
        def xn_dst(rnd, s):
            def f(c, rnd=rnd, s=s):
                return xn_r_sb[:, c * T + rnd * RT + s * TS: c * T + rnd * RT + (s + 1) * TS]
            return f

        cast0 = (call == 0)
        stq_r0 = ("sync", sx)
        stq_r1 = ("sync", sx)
        for s in range(2):
            for t in ln_thunks(lambda c, s=s: src_r[:, c, s * TS:(s + 1) * TS],
                               xn_dst(0, s), 0 * RT + s * TS, True, stq=stq_r0):
                t()
        WT("tensor", cv, cv.n)
        for s in range(2):
            BGQ.extend(ln_thunks(lambda c, s=s: src_r[:, c, RT + s * TS: RT + (s + 1) * TS],
                                 xn_dst(1, s), RT + s * TS, True, stq=stq_r0))
        emit_experts(call, 0, xn_r_sb, acc_sb)
        flush_bg()
        WT("tensor", cv, cv.n)

        dst0, st0 = fin_quick(call, 0, src_r, acc_sb, stq_r1)
        if call == 0 and stage >= 2:
            BGQ.extend(fin_ln2_thunks(0, st0, stq_r1))
        emit_experts(call, 1, xn_r_sb, acc_sb)
        flush_bg()
        dst1, st1 = fin_quick(call, 1, src_r, acc_sb, ("sync", sx))
        if call == 0 and stage >= 2:
            for t in fin_ln2_thunks(1, st1, ("sync", sx)):
                t()

    ffn_scope = ExitStack()
    scope_id = [0]

    def open_ffn_scope():
        k = scope_id[0]
        scope_id[0] += 1
        g = {}
        g["xn"] = ffn_scope.enter_context(nc.sbuf_tensor(f"xn_r_sb{k}", [128, NCH * T], BF16))
        g["acc"] = ffn_scope.enter_context(nc.sbuf_tensor(f"acc_sb{k}", [128, NCH * RT], BF16))
        g["h"] = ffn_scope.enter_context(nc.sbuf_tensor(f"hbuf{k}", [128, NI * RT], BF16))
        g["xr"] = ffn_scope.enter_context(nc.sbuf_tensor(f"xring{k}", [128, 4 * 512], F32))
        g["x2r"] = ffn_scope.enter_context(nc.sbuf_tensor(f"xn2ring{k}", [128, 2 * 512], BF16))
        g["t1"] = ffn_scope.enter_context(nc.sbuf_tensor(f"t1sc{k}", [128, 1024], F32))
        g["t2"] = ffn_scope.enter_context(nc.sbuf_tensor(f"t2sc{k}", [128, 1024], F32))
        g["cbc"] = ffn_scope.enter_context(nc.sbuf_tensor(f"cbc{k}", [128, 1024], F32))
        return g

    gsb = open_ffn_scope()
    xn_sbuf, acc_sbuf, hbuf = gsb["xn"], gsb["acc"], gsb["h"]
    xring, xn2ring = gsb["xr"], gsb["x2r"]
    t1sc, t2sc, cbc = gsb["t1"], gsb["t2"], gsb["cbc"]

    emit_ffn(0, xin_r, xn_sbuf, acc_sbuf)
    barrier()
    ffn_scope.close()

    if stage >= 2:
        attn_scope = ExitStack()
        xn2e = attn_scope.enter_context(nc.sbuf_tensor("xn2e", [128, NCH * 2056], BF16))
        gates = attn_scope.enter_context(nc.sbuf_tensor("gates", [128, NCH * T], BF16))
        sub_scope = ExitStack()
        hat = sub_scope.enter_context(nc.sbuf_tensor("hat", [128, 4 * T], BF16))

        def xn2sl(c, a, b):
            return xn2e[:, c * 2056 + 1 + a: c * 2056 + 1 + b]

        x2l = 0
        for c in range(NCH):
            x2l = DMA("sync", sx, xn2sl(c, 0, T), xn2_r[:, c, :])
        WT("tensor", sx, x2l)

        att_upfree = {0: [], 1: []}

        def load_attn_col(src_view, idx, slot, nch_k):
            for (cc, vv2) in att_upfree[slot]:
                WT("gpsimd", cc, vv2)
            att_upfree[slot] = []
            dst = wgb[:, slot * 2048: slot * 2048 + nch_k * 128].rearrange(
                "p (c f) -> p c f", f=128)
            v = DMA("gpsimd", sg, dst, src_view[idx])
            return sg, v

        for hc in range(4):
            lc, lv = load_attn_col(wg1_rc, hc, hc % 2, NCH)
            WT("tensor", lc, lv)
            slot = hc % 2
            for s4 in range(NSUB):
                b = s4 % 2
                use_bank(b)
                gm = None
                for c in range(NCH):
                    gm = MM(ps[b][:], wgb[:, slot * 2048 + c * 128: slot * 2048 + (c + 1) * 128],
                            xn2sl(c, s4 * TS, (s4 + 1) * TS),
                            start=(c == 0), stop=(c == NCH - 1), inc=(c == NCH - 1))
                WT("scalar", ct, gm)
                sv = OP("scalar", lambda En, hc=hc, s4=s4, b=b: En.activation(
                    out=hat[:, hc * T + s4 * TS: hc * T + (s4 + 1) * TS], in_=ps[b][:],
                    func=AF.Relu, bias=bg1_sb[:, hc:hc + 1]))
                release_bank(b, cs, sv)
            att_upfree[slot].append((ct, ct.n))

        for rc in range(NCH):
            lc, lv = load_attn_col(wg2_rc, rc, rc % 2, 4)
            WT("tensor", lc, lv)
            slot = rc % 2
            for s4 in range(NSUB):
                b = 2 + s4 % 2
                use_bank(b)
                gm = None
                for hc in range(4):
                    gm = MM(ps[b][:], wgb[:, slot * 2048 + hc * 128: slot * 2048 + (hc + 1) * 128],
                            hat[:, hc * T + s4 * TS: hc * T + (s4 + 1) * TS],
                            start=(hc == 0), stop=(hc == 3), inc=(hc == 3))
                WT("scalar", ct, gm)
                sv = OP("scalar", lambda En, rc=rc, s4=s4, b=b: En.activation(
                    out=gates[:, rc * T + s4 * TS: rc * T + (s4 + 1) * TS], in_=ps[b][:],
                    func=AF.Sigmoid, bias=bg2_sb[:, rc:rc + 1]))
                release_bank(b, cs, sv)
            att_upfree[slot].append((ct, ct.n))
        barrier()
        sub_scope.close()
        smalls = ExitStack()
        ugsc = smalls.enter_context(nc.sbuf_tensor("ugsc", [128, 512], F32))
        cumd = smalls.enter_context(nc.sbuf_tensor("cumd", [128, 2 * 512], F32))
        vgsc = smalls.enter_context(nc.sbuf_tensor("vgsc", [128, 512], F32))
        cnvs = smalls.enter_context(nc.sbuf_tensor("cnvs", [128, 512], F32))
        stg1 = smalls.enter_context(nc.sbuf_tensor("stg1", [128, 512], F32))
        stg2 = smalls.enter_context(nc.sbuf_tensor("stg2", [128, 512], F32))
        stg3 = smalls.enter_context(nc.sbuf_tensor("stg3", [128, 512], F32))

        cum_store = {0: 0, 1: 0}
        for rc in range(NCH):
            lc, lv = load_attn_col(wut_rc, rc, rc % 2, NCH)
            WT("tensor", lc, lv)
            slot = rc % 2
            for s4 in range(NSUB):
                b = s4
                use_bank(b)
                um = None
                for c in range(NCH):
                    um = MM(ps[b][:], wgb[:, slot * 2048 + c * 128: slot * 2048 + (c + 1) * 128],
                            xn2sl(c, s4 * TS, (s4 + 1) * TS),
                            start=(c == 0), stop=(c == NCH - 1), inc=(c == NCH - 1))
                WT("vector", ct, um)
                ug = OP("vector", lambda En, rc=rc, s4=s4, b=b: En.scalar_tensor_tensor(
                    out=ugsc[:], in0=ps[b][:], scalar=bu_sb[:, rc:rc + 1],
                    in1=gates[:, rc * T + s4 * TS: rc * T + (s4 + 1) * TS],
                    op0=ALU.add, op1=ALU.mult))
                release_bank(b, cv, ug)
                half = s4 % 2
                if s4 >= 2:
                    WT("vector", sx, cum_store[half])
                if s4 == 0:
                    scn = OP("vector", lambda En, half=half: En.tensor_tensor_scan(
                        out=cumd[:, half * 512:(half + 1) * 512], data0=onesc[:], data1=ugsc[:],
                        initial=0.0, op0=ALU.mult, op1=ALU.add))
                else:
                    ohalf = 1 - half
                    scn = OP("vector", lambda En, half=half, ohalf=ohalf: En.tensor_tensor_scan(
                        out=cumd[:, half * 512:(half + 1) * 512], data0=onesc[:], data1=ugsc[:],
                        initial=cumd[:, ohalf * 512 + 511: ohalf * 512 + 512],
                        op0=ALU.mult, op1=ALU.add))
                WT("sync", cv, scn)
                cum_store[half] = DMA("sync", sx, cum_r[:, rc, s4 * TS:(s4 + 1) * TS],
                                      cumd[:, half * 512:(half + 1) * 512])
            OP("vector", lambda En, rc=rc: En.tensor_copy(
                pay_sb[:, rc:rc + 1], cumd[:, 3 % 2 * 512 + 511: 3 % 2 * 512 + 512]))
            att_upfree[slot].append((ct, ct.n))
        paydone = cv.n

        pay_w = sx.n
        def exch_sync(En):
            with En.register("rp1") as rp1, En.register("rp2") as rp2:
                En.reg_load(rp1, par_sb[0:1, 0:1])
                En.reg_load(rp2, par_sb[0:1, 1:2])
                pv1 = En.snap(rp1, min_val=0, max_val=1)
                pv2 = En.snap(rp2, min_val=0, max_val=128)
                En.dma_start(out=shm.ap()[bass.ds(pv2, 128), :], in_=pay_sb[:]).then_inc(sx.h, 16)
                En.wait_ge(sx.h, pay_w + 16)
                En.dma_start(out=flg.ap()[bass.ds(pv1, 1), :], in_=ep_sb[:]).then_inc(sx.h, 16)
                En.wait_ge(sx.h, pay_w + 32)
        WT("sync", cv, paydone)
        EM("sync", exch_sync)
        sx.n += 32
        flag_w = sx.n

        def exch_poll(En):
            with En.register("pr") as pr, En.register("pe") as pe:
                En.reg_load(pe, ep_sb[0:1, 0:1])
                for q in (0, 1):
                    En.reg_mov(pr, 1)
                    with En.While(pr):
                        En.load(pr, flg.ap()[q:q + 1, 0:1])
                        En.reg_sub(pr, pr, pe)
        WT("gpsimd", cg, cg.n)
        EM("gpsimd", exch_poll)
        vA = DMA("gpsimd", sg, slA_sb[:], shm.ap()[0:128, :])
        vB = DMA("gpsimd", sg, slB_sb[:], shm.ap()[128:256, :])
        WT("vector", sg, vB)
        OP("vector", lambda En: En.tensor_scalar(
            out=offs_sb[:], in0=slA_sb[:, 0:16], scalar1=modd_sb[:, 0:1],
            scalar2=0.0, op0=ALU.mult, op1=ALU.add))
        for c in range(NCH):
            OP("vector", lambda En, c=c: En.tensor_scalar(
                out=xn2e[:, c * 2056: c * 2056 + 1], in0=slA_sb[:, 32 + c:33 + c],
                scalar1=modd_sb[:, 0:1], scalar2=0.0, op0=ALU.mult, op1=ALU.add))
            OP("vector", lambda En, c=c: En.tensor_scalar(
                out=xn2e[:, c * 2056 + 2049: c * 2056 + 2050], in0=slB_sb[:, 16 + c:17 + c],
                scalar1=mevn_sb[:, 0:1], scalar2=0.0, op0=ALU.mult, op1=ALU.add))
        exch_done = cv.n

        for rc in range(NCH):
            lc, lv = load_attn_col(wvt_rc, rc, rc % 2, NCH)
            WT("tensor", lc, lv)
            slot = rc % 2
            for s4 in range(NSUB):
                b = s4
                use_bank(b)
                vm = None
                for c in range(NCH):
                    vm = MM(ps[b][:], wgb[:, slot * 2048 + c * 128: slot * 2048 + (c + 1) * 128],
                            xn2sl(c, s4 * TS, (s4 + 1) * TS),
                            start=(c == 0), stop=(c == NCH - 1), inc=(c == NCH - 1))
                WT("sync", cv, cv.n)
                cl = DMA("sync", sx, stg3[:], cum_r[:, rc, s4 * TS:(s4 + 1) * TS])
                WT("vector", ct, vm)
                vg = OP("vector", lambda En, rc=rc, s4=s4, b=b: En.scalar_tensor_tensor(
                    out=vgsc[:], in0=ps[b][:], scalar=bv_sb[:, rc:rc + 1],
                    in1=gates[:, rc * T + s4 * TS: rc * T + (s4 + 1) * TS],
                    op0=ALU.add, op1=ALU.mult))
                release_bank(b, cv, vg)
                WT("vector", sx, cl)
                WT("vector", cv, exch_done)
                go = OP("vector", lambda En, rc=rc, s4=s4: En.scalar_tensor_tensor(
                    out=gates[:, rc * T + s4 * TS: rc * T + (s4 + 1) * TS],
                    in0=stg3[:], scalar=offs_sb[:, rc:rc + 1], in1=vgsc[:],
                    op0=ALU.add, op1=ALU.mult))
            att_upfree[slot].append((ct, ct.n))
        barrier()
        if DEBUG:
            for c in range(NCH):
                DMA("sync", sx, dgo.rearrange("(c p) t -> p c t", p=128)[:, c, :],
                    gates[:, c * T:(c + 1) * T])
            barrier()

        for dc in range(NCH):
            lc, lv = load_attn_col(wut_rc, dc, dc % 2, NCH)
            WT("tensor", lc, lv)
            slot = dc % 2
            for s4 in range(NSUB):
                b = 4 + 2 * (dc % 2) + (s4 % 2)
                use_bank(b)
                uvm = None
                for rc in range(NCH):
                    uvm = MM(ps[b][:], wgb[:, slot * 2048 + rc * 128: slot * 2048 + (rc + 1) * 128],
                             gates[:, rc * T + s4 * TS: rc * T + (s4 + 1) * TS],
                             start=(rc == 0), stop=(rc == NCH - 1), inc=(rc == NCH - 1))
                base = dc * 2056 + 1 + s4 * TS
                cva = OP("vector", lambda En, dc=dc, base=base: En.tensor_scalar(
                    out=cnvs[:], in0=xn2e[:, base - 1: base - 1 + TS],
                    scalar1=cw_sb[:, dc * 3: dc * 3 + 1], scalar2=cb_sb[:, dc:dc + 1],
                    op0=ALU.mult, op1=ALU.add))
                OP("vector", lambda En, dc=dc, base=base: En.scalar_tensor_tensor(
                    out=cnvs[:], in0=xn2e[:, base: base + TS],
                    scalar=cw_sb[:, dc * 3 + 1: dc * 3 + 2], in1=cnvs[:],
                    op0=ALU.mult, op1=ALU.add))
                OP("vector", lambda En, dc=dc, base=base: En.scalar_tensor_tensor(
                    out=cnvs[:], in0=xn2e[:, base + 1: base + 1 + TS],
                    scalar=cw_sb[:, dc * 3 + 2: dc * 3 + 3], in1=cnvs[:],
                    op0=ALU.mult, op1=ALU.add))
                WT("sync", cv, cv.n)
                xl = DMA("sync", sx, stg1[:], x1_r[:, dc, s4 * TS:(s4 + 1) * TS])
                WT("vector", ct, uvm)
                OP("vector", lambda En, b=b: En.tensor_add(cnvs[:], cnvs[:], ps[b][:]))
                x2v0 = cv.n
                release_bank(b, cv, x2v0)
                WT("vector", sx, xl)
                x2v = OP("vector", lambda En: En.tensor_add(stg2[:], cnvs[:], stg1[:]))
                WT("sync", cv, x2v)
                dst = x2_r if stage >= 3 else y_r
                DMA("sync", sx, dst[:, dc, s4 * TS:(s4 + 1) * TS], stg2[:])
            att_upfree[slot].append((ct, ct.n))
        barrier()
        smalls.close()
        attn_scope.close()

        if stage >= 3:
            gsb = open_ffn_scope()
            xn_sbuf, acc_sbuf, hbuf = gsb["xn"], gsb["acc"], gsb["h"]
            xring, xn2ring = gsb["xr"], gsb["x2r"]
            t1sc, t2sc, cbc = gsb["t1"], gsb["t2"], gsb["cbc"]
            emit_ffn(1, x2_r, xn_sbuf, acc_sbuf)
            barrier()
            ffn_scope.close()

    if DEBUG and stage >= 2:
        DMA("sync", sx, dx1, x1_s.ap())
        DMA("sync", sx, dxn2, xn2_s.ap())
        DMA("sync", sx, dcum, cum_s.ap())
    WT("sync", sx, sx.n)
    for c in ALLC:
        WT("sync", c, c.n)

    with nc.Block() as block:
        @block.sync
        def _(En):
            for f in Q["sync"]:
                f(En)

        @block.gpsimd
        def _(En):
            for f in Q["gpsimd"]:
                f(En)

        @block.vector
        def _(En):
            for f in Q["vector"]:
                f(En)

        @block.scalar
        def _(En):
            for f in Q["scalar"]:
                f(En)

        @block.tensor
        def _(En):
            for f in Q["tensor"]:
                f(En)

    est.close()
    return nc



def kernel(L0_attn_x, L0_attn_Wu, L0_attn_bu, L0_attn_Wv, L0_attn_bv,
           L0_attn_Wg1, L0_attn_bg1, L0_attn_Wg2, L0_attn_bg2,
           L0_attn_conv_w, L0_attn_conv_b,
           L0_ffn_Wr, L0_ffn_Wg, L0_ffn_Wu, L0_ffn_Wd):
    x = np.ascontiguousarray(np.asarray(L0_attn_x, dtype=np.float32))
    B, L, Dm = x.shape
    assert (B, L, Dm) == (4, 4096, 2048)

    wg = np.ascontiguousarray(np.asarray(L0_ffn_Wg, dtype=np.float32))
    wu = np.ascontiguousarray(np.asarray(L0_ffn_Wu, dtype=np.float32))
    wd = np.ascontiguousarray(np.asarray(L0_ffn_Wd, dtype=np.float32))
    wr = np.ascontiguousarray(np.asarray(L0_ffn_Wr, dtype=np.float32)
                              .reshape(16, 128, 8).transpose(1, 0, 2).reshape(128, 128))
    wut = np.ascontiguousarray(np.asarray(L0_attn_Wu, dtype=np.float32).T)
    wvt = np.ascontiguousarray(np.asarray(L0_attn_Wv, dtype=np.float32).T)
    wg1t = np.ascontiguousarray(np.asarray(L0_attn_Wg1, dtype=np.float32).T)
    wg2t = np.ascontiguousarray(np.asarray(L0_attn_Wg2, dtype=np.float32).T)
    def chan_swz(a, nch):
        a = np.asarray(a, dtype=np.float32)
        return np.ascontiguousarray(a.reshape(nch, 128).T)
    bu = chan_swz(L0_attn_bu, 16)
    bv = chan_swz(L0_attn_bv, 16)
    bg1 = chan_swz(L0_attn_bg1, 4)
    bg2 = chan_swz(L0_attn_bg2, 16)
    cw = np.ascontiguousarray(np.asarray(L0_attn_conv_w, dtype=np.float32)
                              .reshape(16, 128, 3).transpose(1, 0, 2).reshape(128, 48))
    cb = chan_swz(L0_attn_conv_b, 16)
    ident = np.eye(128, dtype=np.float32)
    esel = np.zeros((8, 8 * 128), dtype=np.float32)
    for e_ in range(8):
        esel[e_, e_ * 128:(e_ + 1) * 128] = 1.0
    epoch = np.full((1, 16), (int(time.time() * 1000) % (1 << 29)) | 1, np.int32)

    nc = build(STAGE)
    in_maps = []
    for c in range(NCORES):
        b = c // 2
        h = c % 2
        xs = np.ascontiguousarray(x[b, h * T:(h + 1) * T, :].T)
        par = np.zeros((1, 16), np.int32)
        par[0, 0] = h
        par[0, 1] = h * 128
        modd = np.full((128, 1), 1.0 if h == 1 else 0.0, np.float32)
        mevn = np.full((128, 1), 1.0 if h == 0 else 0.0, np.float32)
        in_maps.append({
            "xin": xs, "fwg": wg, "fwu": wu, "fwd": wd, "fwr": wr,
            "awut": wut, "awvt": wvt, "awg1t": wg1t, "awg2t": wg2t,
            "abu": bu, "abv": bv, "abg1": bg1, "abg2": bg2,
            "acw": cw, "acb": cb, "ident": ident, "esel": esel, "par": par,
            "epoch": epoch, "modd": modd, "mevn": mevn,
        })
    res = run_bass_kernel_spmd(nc, in_maps, core_ids=list(range(NCORES)))
    out = np.empty((4, 4096, 2048), np.float32)
    for c in range(NCORES):
        b = c // 2
        h = c % 2
        out[b, h * T:(h + 1) * T, :] = res.results[c]["yout"].T
    return out


# version 07
# speedup vs baseline: 1.4644x; 1.1785x over previous
import time
import numpy as np
from contextlib import ExitStack

import concourse.bass as bass
import concourse.mybir as mybir
from concourse.bass_utils import run_bass_kernel_spmd

F32 = mybir.dt.float32
BF16 = mybir.dt.bfloat16
I32 = mybir.dt.int32
AF = mybir.ActivationFunctionType
ALU = mybir.AluOpType

NCORES = 8
D = 2048
NCH = 16
T = 2048
E = 8
IFF = 1408
NI = 11
RT = 1024
TS = 512
NSUB = 4
EPS = 1e-5

STAGE = 3
DEBUG = False



def build(stage=STAGE):
    nc = bass.Bass("TRN2", target_bir_lowering=False)

    xin = nc.dram_tensor("xin", [D, T], F32, kind="ExternalInput").ap()
    wg_h = nc.dram_tensor("fwg", [E, D, IFF], F32, kind="ExternalInput").ap()
    wu_h = nc.dram_tensor("fwu", [E, D, IFF], F32, kind="ExternalInput").ap()
    wd_h = nc.dram_tensor("fwd", [E, IFF, D], F32, kind="ExternalInput").ap()
    wr_h = nc.dram_tensor("fwr", [128, 128], F32, kind="ExternalInput").ap()
    wut_h = nc.dram_tensor("awut", [D, D], F32, kind="ExternalInput").ap()
    wvt_h = nc.dram_tensor("awvt", [D, D], F32, kind="ExternalInput").ap()
    wg1t_h = nc.dram_tensor("awg1t", [D, 512], F32, kind="ExternalInput").ap()
    wg2t_h = nc.dram_tensor("awg2t", [512, D], F32, kind="ExternalInput").ap()
    bu_h = nc.dram_tensor("abu", [128, 16], F32, kind="ExternalInput").ap()
    bv_h = nc.dram_tensor("abv", [128, 16], F32, kind="ExternalInput").ap()
    bg1_h = nc.dram_tensor("abg1", [128, 4], F32, kind="ExternalInput").ap()
    bg2_h = nc.dram_tensor("abg2", [128, 16], F32, kind="ExternalInput").ap()
    cw_h = nc.dram_tensor("acw", [128, 48], F32, kind="ExternalInput").ap()
    cb_h = nc.dram_tensor("acb", [128, 16], F32, kind="ExternalInput").ap()
    ident_h = nc.dram_tensor("ident", [128, 128], F32, kind="ExternalInput").ap()
    esel_h = nc.dram_tensor("esel", [8, 8 * 128], F32, kind="ExternalInput").ap()
    par_h = nc.dram_tensor("par", [1, 16], I32, kind="ExternalInput").ap()
    epoch_h = nc.dram_tensor("epoch", [1, 16], I32, kind="ExternalInput").ap()
    modd_h = nc.dram_tensor("modd", [128, 1], F32, kind="ExternalInput").ap()
    mevn_h = nc.dram_tensor("mevn", [128, 1], F32, kind="ExternalInput").ap()
    yout = nc.dram_tensor("yout", [D, T], F32, kind="ExternalOutput").ap()
    if DEBUG:
        dx1 = nc.dram_tensor("dx1", [D, T], F32, kind="ExternalOutput").ap()
        dxn2 = nc.dram_tensor("dxn2", [D, T], BF16, kind="ExternalOutput").ap()
        dcum = nc.dram_tensor("dcum", [D, T], F32, kind="ExternalOutput").ap()
        dgo = nc.dram_tensor("dgo", [D, T], BF16, kind="ExternalOutput").ap()

    x1_s = nc.dram_tensor("x1_s", [D, T], F32)
    x2_s = nc.dram_tensor("x2_s", [D, T], F32)
    xn2_s = nc.dram_tensor("xn2_s", [D, T], BF16)
    cum_s = nc.dram_tensor("cum_s", [D, T], F32)
    wcu = nc.dram_tensor("wcu", [E * 2 * NI, NCH * 128 * 128], BF16)
    wcd = nc.dram_tensor("wcd", [E * NCH, NI * 128 * 128], BF16)
    shm = nc.dram_tensor("shmx", [256, 48], F32, addr_space="Shared")
    flg = nc.dram_tensor("flgx", [2, 16], I32, addr_space="Shared")

    xin_r = xin.rearrange("(c p) t -> p c t", p=128)
    y_r = yout.rearrange("(c p) t -> p c t", p=128)
    x1_r = x1_s.ap().rearrange("(c p) t -> p c t", p=128)
    x2_r = x2_s.ap().rearrange("(c p) t -> p c t", p=128)
    xn2_r = xn2_s.ap().rearrange("(c p) t -> p c t", p=128)
    cum_r = cum_s.ap().rearrange("(c p) t -> p c t", p=128)
    wg_r = wg_h.rearrange("e (c p) (i f) -> e i p c f", p=128, f=128)
    wu_r = wu_h.rearrange("e (c p) (i f) -> e i p c f", p=128, f=128)
    wd_r = wd_h.rearrange("e (ci p) (d f) -> e d p ci f", p=128, f=128)
    wut_rc = wut_h.rearrange("(c p) (rc f) -> rc p c f", p=128, f=128)
    wvt_rc = wvt_h.rearrange("(c p) (rc f) -> rc p c f", p=128, f=128)
    wg1_rc = wg1t_h.rearrange("(c p) (hc f) -> hc p c f", p=128, f=128)
    wg2_rc = wg2t_h.rearrange("(hc p) (rc f) -> rc p hc f", p=128, f=128)
    wcu_v = wcu.ap().rearrange("n (p f) -> n p f", p=128)
    wcd_v = wcd.ap().rearrange("n (p f) -> n p f", p=128)

    ENGS = ("sync", "gpsimd", "vector", "scalar", "tensor")
    Q = {e: [] for e in ENGS}

    def EM(eng, fn):
        Q[eng].append(fn)

    class Ctr:
        def __init__(self, name):
            self.cm = nc.semaphore(name)
            self.h = self.cm.__enter__()
            self.n = 0

    sx = Ctr("sx")
    sc = Ctr("sc")
    sg = Ctr("sg")
    cv = Ctr("cv")
    cs = Ctr("cs")
    cg = Ctr("cg")
    ct = Ctr("ct")
    ALLC = [sx, sc, sg, cv, cs, cg, ct]

    def WT(eng, ctr, val):
        if val <= 0:
            return
        EM(eng, lambda En, h=ctr.h, v=val: En.wait_ge(h, v))

    def DMA(eng, ctr, out, in_):
        def f(En, o=out, i=in_, h=ctr.h):
            En.dma_start(out=o, in_=i).then_inc(h, 16)
        EM(eng, f)
        ctr.n += 16
        return ctr.n

    ENG2CTR = {"vector": cv, "scalar": cs, "gpsimd": cg, "tensor": ct}

    def OP(eng, fn, flush=True):
        ctr = ENG2CTR[eng]
        if flush and ctr.n > 0:
            WT(eng, ctr, ctr.n)
        def g(En, fn=fn, h=ctr.h):
            fn(En).then_inc(h, 1)
        EM(eng, g)
        ctr.n += 1
        return ctr.n

    def MM(out, lhsT, rhs, start, stop, inc=False):
        if inc:
            def f(En, o=out, l=lhsT, r=rhs, st=start, sp=stop, h=ct.h):
                En.matmul(o, l, r, start=st, stop=sp).then_inc(h, 1)
            EM("tensor", f)
            ct.n += 1
            return ct.n
        else:
            def f(En, o=out, l=lhsT, r=rhs, st=start, sp=stop):
                En.matmul(o, l, r, start=st, stop=sp)
            EM("tensor", f)
            return None

    def barrier():
        for e in ENGS:
            for c in ALLC:
                WT(e, c, c.n)

    bank_free = [[] for _ in range(8)]

    def use_bank(b):
        for (c, v) in bank_free[b]:
            WT("tensor", c, v)
        bank_free[b] = []

    def release_bank(b, ctr, val):
        bank_free[b].append((ctr, val))

    est = ExitStack()

    def sb(name, shape, dt):
        return est.enter_context(nc.sbuf_tensor(name, shape, dt))

    bu_sb = sb("bu_sb", [128, 16], F32)
    bv_sb = sb("bv_sb", [128, 16], F32)
    bg2_sb = sb("bg2_sb", [128, 16], F32)
    cb_sb = sb("cb_sb", [128, 16], F32)
    bg1_sb = sb("bg1_sb", [128, 4], F32)
    cw_sb = sb("cw_sb", [128, 48], F32)
    wr_sb = sb("wr_sb", [128, 128], F32)
    wrb_sb = sb("wrb_sb", [128, 128], BF16)
    id_sb = sb("id_sb", [128, 128], F32)
    esel_sb = sb("esel_sb", [8, 8 * 128], F32)
    ones_b = sb("ones_b", [128, 1], BF16)
    ones1 = sb("ones1", [1, 128], F32)
    onesc = sb("onesc", [128, 512], F32)
    crows = sb("crows", [8, 2048], F32)
    lnsm = sb("lnsm", [1, 4 * 512], F32)
    lg_sb = sb("lg_sb", [8, 512], F32)
    tm_sb = sb("tm_sb", [128, 32], F32)
    tm2_sb = sb("tm2_sb", [128, 32], F32)
    pay_sb = sb("pay_sb", [128, 48], F32)
    slA_sb = sb("slA_sb", [128, 48], F32)
    slB_sb = sb("slB_sb", [128, 48], F32)
    offs_sb = sb("offs_sb", [128, 16], F32)
    modd_sb = sb("modd_sb", [128, 1], F32)
    mevn_sb = sb("mevn_sb", [128, 1], F32)
    par_sb = sb("par_sb", [1, 16], I32)
    ep_sb = sb("ep_sb", [1, 16], I32)
    wgb = sb("wgb", [128, 2 * 2048], BF16)
    wub = sb("wub", [128, 2 * 2048], BF16)
    wdb = sb("wdb", [128, 2 * 1408], BF16)
    xbfc = sb("xbfc", [128, 2 * 512], BF16)
    epsv = sb("epsv", [1, 1], F32)
    sqfc = sb("sqfc", [128, 2 * 512], BF16)

    ps = [est.enter_context(nc.psum_tensor(f"ps{i}", [128, 512], F32)) for i in range(8)]

    small_loads = [
        (bu_sb[:], bu_h), (bv_sb[:], bv_h), (bg2_sb[:], bg2_h), (cb_sb[:], cb_h),
        (bg1_sb[:], bg1_h), (cw_sb[:], cw_h), (wr_sb[:], wr_h), (id_sb[:], ident_h),
        (modd_sb[:], modd_h), (mevn_sb[:], mevn_h), (par_sb[:], par_h), (ep_sb[:], epoch_h),
        (esel_sb[:], esel_h),
    ]
    for dst, srcv in small_loads:
        DMA("sync", sx, dst, srcv)
    v_pro = sx.n
    OP("vector", lambda En: En.memset(ones_b[:], 1.0))
    OP("vector", lambda En: En.memset(ones1[:], 1.0))
    OP("vector", lambda En: En.memset(onesc[:], 1.0))
    OP("vector", lambda En: En.memset(epsv[:], EPS))
    WT("vector", sx, v_pro)
    OP("vector", lambda En: En.tensor_copy(wrb_sb[:], wr_sb[:]))
    pro_done = cv.n

    xn2war = {0: 0, 1: 0}
    xn2war_c = [None]
    PREV_DN = [0]
    X2PRE = [0]
    BGQ = []

    def drain_bg(n=1):
        for _ in range(n):
            if not BGQ:
                return
            BGQ.pop(0)()

    def flush_bg():
        while BGQ:
            BGQ.pop(0)()

    def ln_thunks(chunk_src, xn_dst, c_off, with_router, src_guard=None,
                  post_norm=None, stq=None, stq2=None):
        th = []
        stq_e, stq_c = stq

        def q(c):
            return stq if (stq2 is None or c % 2 == 0) else stq2
        state = {"war": {0: 0, 1: 0, 2: 0, 3: 0}, "ld": {}, "bmm": 0, "norm_done": 0}

        def stats_group(c0):
            def f(c0=c0):
                if c0 == 0:
                    use_bank(6)
                    use_bank(7)
                    if src_guard is not None:
                        WT(stq_e, src_guard[0], src_guard[1])
                        if stq2 is not None:
                            WT(stq2[0], src_guard[0], src_guard[1])
                for c in range(c0, c0 + 4):
                    sl = c % 2
                    qe, qc = q(c)
                    WT(qe, cv, state["war"][sl])
                    lv = DMA(qe, qc, xring[:, sl * 512:(sl + 1) * 512], chunk_src(c))
                    WT("vector", qc, lv)
                    WT("vector", ct, state["ld"].get(c - 2, 0))
                    vcp = OP("vector", lambda En, sl=sl: En.tensor_copy(
                        xbfc[:, sl * 512:(sl + 1) * 512], xring[:, sl * 512:(sl + 1) * 512]))
                    state["war"][sl] = vcp
                    WT("scalar", ct, state["ld"].get(c - 2, 0))
                    WT("scalar", cv, vcp)
                    scp = OP("scalar", lambda En, sl=sl: En.activation(
                        out=sqfc[:, sl * 512:(sl + 1) * 512],
                        in_=xbfc[:, sl * 512:(sl + 1) * 512], func=AF.Square))
                    WT("tensor", cv, vcp)
                    WT("tensor", cs, scp)
                    MM(ps[6][0:1, :], ones_b[:], xbfc[:, sl * 512:(sl + 1) * 512],
                       start=(c == 0), stop=(c == NCH - 1), inc=False)
                    state["ld"][c] = MM(ps[7][0:1, :], ones_b[:], sqfc[:, sl * 512:(sl + 1) * 512],
                                        start=(c == 0), stop=(c == NCH - 1), inc=True)
                    if c + 2 <= NCH - 1:
                        pass
            return f

        def stats_group_guarded(c0):
            base = stats_group(c0)
            def f():
                base()
            return f

        for c0 in (0, 4, 8, 12):
            th.append(stats_group(c0))

        def meanvar():
            stats_done = ct.n
            WT("scalar", ct, stats_done)
            s1 = OP("scalar", lambda En: En.mul(lnsm[0:1, 0:512], ps[6][0:1, :], 1.0 / D))
            s2 = OP("scalar", lambda En: En.mul(lnsm[0:1, 512:1024], ps[7][0:1, :], 1.0 / D))
            release_bank(6, cs, s2)
            release_bank(7, cs, s2)
            WT("vector", cs, s2)
            OP("vector", lambda En: En.tensor_mul(
                lnsm[0:1, 1536:2048], lnsm[0:1, 0:512], lnsm[0:1, 0:512]))
            OP("vector", lambda En: En.tensor_sub(
                lnsm[0:1, 512:1024], lnsm[0:1, 512:1024], lnsm[0:1, 1536:2048]))
            vv = cv.n
            WT("scalar", cv, vv)
            sd = OP("scalar", lambda En: En.activation(
                out=lnsm[0:1, 1024:1536], in_=lnsm[0:1, 512:1024], func=AF.Sqrt,
                bias=epsv[0:1, 0:1]))
            WT("vector", cs, sd)
            rst = OP("vector", lambda En: En.reciprocal(lnsm[0:1, 1024:1536], lnsm[0:1, 1024:1536]))
            use_bank(6)
            use_bank(7)
            WT("tensor", cv, rst)
            MM(ps[6][:], ones1[:], lnsm[0:1, 0:512], start=True, stop=True, inc=False)
            state["bmm"] = MM(ps[7][:], ones1[:], lnsm[0:1, 1024:1536],
                              start=True, stop=True, inc=True)
        th.append(meanvar)

        def norm_group(c0):
            def f(c0=c0):
                if c0 == 0:
                    WT("vector", ct, state["bmm"])
                for c in range(c0, c0 + 4):
                    sl = 2 + c % 2
                    qe, qc = q(c)
                    WT(qe, cv, state["war"][sl])
                    lv = DMA(qe, qc, xring[:, sl * 512:(sl + 1) * 512], chunk_src(c))
                    WT("vector", qc, lv)
                    OP("vector", lambda En, sl=sl: En.tensor_sub(
                        xring[:, sl * 512:(sl + 1) * 512],
                        xring[:, sl * 512:(sl + 1) * 512], ps[6][:]))
                    if xn_dst is not None:
                        mv = OP("vector", lambda En, sl=sl, d=xn_dst(c): En.tensor_mul(
                            d, xring[:, sl * 512:(sl + 1) * 512], ps[7][:]))
                        state["war"][sl] = mv
                    else:
                        rsl = c % 2
                        WT("vector", xn2war_c[0], xn2war[rsl])
                        mv = OP("vector", lambda En, sl=sl, rsl=rsl: En.tensor_mul(
                            xn2ring[:, rsl * 512:(rsl + 1) * 512],
                            xring[:, sl * 512:(sl + 1) * 512], ps[7][:]))
                        state["war"][sl] = mv
                        post_norm(c, rsl, mv)
                if c0 == 12:
                    state["norm_done"] = cv.n
                    release_bank(6, cv, cv.n)
                    release_bank(7, cv, cv.n)
            return f

        for c0 in (0, 4, 8, 12):
            th.append(norm_group(c0))

        if with_router:
            def router1():
                use_bank(6)
                WT("tensor", cv, state["norm_done"])
                lgd = None
                for c in range(NCH):
                    lgd = MM(ps[6][0:8, :], wrb_sb[:, c * 8:(c + 1) * 8], xn_dst(c),
                             start=(c == 0), stop=(c == NCH - 1), inc=(c == NCH - 1))
                WT("scalar", ct, lgd)
                lge = OP("scalar", lambda En: En.copy(lg_sb[:], ps[6][0:8, :]))
                release_bank(6, cs, lge)
                use_bank(7)
                for j in range(4):
                    WT("tensor", cs, cs.n)
                    tj = MM(ps[7][0:128, 0:8], lg_sb[0:8, j * 128:(j + 1) * 128],
                            id_sb[0:8, 0:8], start=True, stop=True, inc=True)
                    WT("scalar", ct, tj)
                    OP("scalar", lambda En, j=j: En.copy(tm_sb[:, j * 8:(j + 1) * 8],
                                                         ps[7][0:128, 0:8]))
                release_bank(7, cs, cs.n)
            th.append(router1)

            def router2():
                WT("scalar", cs, cs.n)
                ex = OP("scalar", lambda En: En.activation(out=tm_sb[:], in_=tm_sb[:], func=AF.Exp))
                WT("vector", cs, ex)
                for j in range(4):
                    b0 = j * 8
                    OP("vector", lambda En, b0=b0: En.reduce_max(
                        offs_sb[:, 0:1], tm_sb[:, b0:b0 + 8], axis=mybir.AxisListType.X))
                    OP("vector", lambda En, b0=b0: En.tensor_scalar(
                        out=tm2_sb[:, b0:b0 + 8], in0=tm_sb[:, b0:b0 + 8],
                        scalar1=offs_sb[:, 0:1], scalar2=1.0, op0=ALU.is_equal, op1=ALU.mult))
                    OP("vector", lambda En, b0=b0: En.scalar_tensor_tensor(
                        out=tm2_sb[:, b0:b0 + 8], in0=tm2_sb[:, b0:b0 + 8], scalar=-1e30,
                        in1=tm_sb[:, b0:b0 + 8], op0=ALU.mult, op1=ALU.add))
                    OP("vector", lambda En, b0=b0: En.reduce_max(
                        offs_sb[:, 1:2], tm2_sb[:, b0:b0 + 8], axis=mybir.AxisListType.X))
                    OP("vector", lambda En: En.tensor_add(offs_sb[:, 2:3], offs_sb[:, 0:1], offs_sb[:, 1:2]))
                    OP("vector", lambda En: En.reciprocal(offs_sb[:, 2:3], offs_sb[:, 2:3]))
                    OP("vector", lambda En, b0=b0: En.tensor_scalar(
                        out=tm2_sb[:, b0:b0 + 8], in0=tm_sb[:, b0:b0 + 8],
                        scalar1=offs_sb[:, 1:2], scalar2=1.0, op0=ALU.is_ge, op1=ALU.mult))
                    OP("vector", lambda En, b0=b0: En.tensor_mul(
                        tm2_sb[:, b0:b0 + 8], tm2_sb[:, b0:b0 + 8], tm_sb[:, b0:b0 + 8]))
                    OP("vector", lambda En, b0=b0: En.tensor_scalar(
                        out=tm2_sb[:, b0:b0 + 8], in0=tm2_sb[:, b0:b0 + 8],
                        scalar1=offs_sb[:, 2:3], scalar2=0.5, op0=ALU.mult, op1=ALU.mult))
                c2done = cv.n
                use_bank(6)
                for j in range(4):
                    WT("tensor", cv, c2done)
                    if j > 0:
                        WT("tensor", cs, cs.n)
                    tj = MM(ps[6][0:8, 0:128], tm2_sb[:, j * 8:(j + 1) * 8],
                            id_sb[:, :], start=True, stop=True, inc=True)
                    WT("scalar", ct, tj)
                    OP("scalar", lambda En, j=j: En.copy(
                        crows[0:8, c_off + j * 128: c_off + (j + 1) * 128], ps[6][0:8, 0:128]))
                release_bank(6, cs, cs.n)
            th.append(router2)
        return th

    upfree = {0: [], 1: []}
    dnfree = {0: [], 1: []}

    def load_up_col(call, rnd, e, i, cast_mode):
        slot = i % 2
        for (cc, vv2) in upfree[slot]:
            WT("gpsimd" if cast_mode else "sync", cc, vv2)
        upfree[slot] = []
        wg_dst = wgb[:, slot * 2048:(slot + 1) * 2048].rearrange("p (c f) -> p c f", f=128)
        wu_dst = wub[:, slot * 2048:(slot + 1) * 2048].rearrange("p (c f) -> p c f", f=128)
        if cast_mode:
            v1 = DMA("gpsimd", sg, wg_dst, wg_r[e, i])
            v2 = DMA("gpsimd", sg, wu_dst, wu_r[e, i])
            WT("scalar", sg, v2)
            cw1 = DMA("scalar", sc, wcu_v[(e * 2 + 0) * NI + i],
                      wgb[:, slot * 2048:(slot + 1) * 2048])
            cw2 = DMA("scalar", sc, wcu_v[(e * 2 + 1) * NI + i],
                      wub[:, slot * 2048:(slot + 1) * 2048])
            return (sg, v2), (sc, cw2)
        else:
            v1 = DMA("sync", sx, wgb[:, slot * 2048:(slot + 1) * 2048],
                     wcu_v[(e * 2 + 0) * NI + i])
            v2 = DMA("sync", sx, wub[:, slot * 2048:(slot + 1) * 2048],
                     wcu_v[(e * 2 + 1) * NI + i])
            return (sx, v2), None

    def load_dn_col(call, rnd, e, d, cast_mode):
        slot = d % 2
        for (cc, vv2) in dnfree[slot]:
            WT("gpsimd" if cast_mode else "sync", cc, vv2)
        dnfree[slot] = []
        wd_dst = wdb[:, slot * 1408:(slot + 1) * 1408].rearrange("p (ci f) -> p ci f", f=128)
        if cast_mode:
            v1 = DMA("gpsimd", sg, wd_dst, wd_r[e, d])
            WT("scalar", sg, v1)
            cwv = DMA("scalar", sc, wcd_v[e * NCH + d],
                      wdb[:, slot * 1408:(slot + 1) * 1408])
            return (sg, v1), (sc, cwv)
        else:
            v1 = DMA("sync", sx, wdb[:, slot * 1408:(slot + 1) * 1408],
                     wcd_v[e * NCH + d])
            return (sx, v1), None

    def fin_quick(call, rnd, src_r, acc_sb, stq):
        stq_e, stq_c = stq
        if call == 0 and stage >= 2:
            dst = x1_r
        else:
            dst = y_r
        war = {0: 0, 1: 0}
        last = 0
        for s in range(2):
            toff = rnd * RT + s * TS
            for c in range(NCH):
                sl = c % 2
                WT(stq_e, stq_c, war[sl])
                lv = DMA(stq_e, stq_c, xring[:, sl * 512:(sl + 1) * 512],
                         src_r[:, c, toff:toff + TS])
                WT("vector", stq_c, lv)
                av = OP("vector", lambda En, c=c, s=s, sl=sl: En.tensor_add(
                    xring[:, sl * 512:(sl + 1) * 512], xring[:, sl * 512:(sl + 1) * 512],
                    acc_sb[:, c * RT + s * TS: c * RT + (s + 1) * TS]))
                WT(stq_e, cv, av)
                war[sl] = DMA(stq_e, stq_c, dst[:, c, toff:toff + TS],
                              xring[:, sl * 512:(sl + 1) * 512])
                last = war[sl]
        return dst, last

    def fin_ln2_thunks(rnd, store_done, stq, store_ctr):
        ths = []
        stq_e, stq_c = stq
        xn2war_c[0] = stq_c
        for s in range(2):
            toff = rnd * RT + s * TS

            def post_norm(c, rsl, mv, toff=toff, rnd=rnd, s=s):
                if rnd == 0 and s == 0:
                    OP("vector", lambda En, c=c, rsl=rsl: En.tensor_copy(
                        pay_sb[:, 16 + c:17 + c], xn2ring[:, rsl * 512:rsl * 512 + 1]))
                if rnd == 1 and s == 1:
                    OP("vector", lambda En, c=c, rsl=rsl: En.tensor_copy(
                        pay_sb[:, 32 + c:33 + c], xn2ring[:, rsl * 512 + 511:rsl * 512 + 512]))
                WT(stq_e, cv, cv.n)
                xn2war[rsl] = DMA(stq_e, stq_c, xn2_r[:, c, toff:toff + TS],
                                  xn2ring[:, rsl * 512:(rsl + 1) * 512])

            ths += ln_thunks(
                lambda c, toff=toff: x1_r[:, c, toff:toff + TS],
                None, toff, False, src_guard=(store_ctr, store_done), post_norm=post_norm,
                stq=stq)
        return ths

    def emit_experts(call, rnd, xn_r_sb, acc_sb, fin=None):
        cast_mode = (call == 0 and rnd == 0)
        t0 = rnd * RT
        prev_dn_end = PREV_DN[0]
        for e in range(E):
            if e == E - 1 and fin is not None:
                flush_bg()
                fe, fc = fin["stq"]
                fin_base_cv = cv.n
                xld = {}
                xr_war = [fin_base_cv] * 4
                st_war = [0, 0]

                def fin_load(k):
                    d2, s2 = divmod(k, 2)
                    WT(fe, cv, xr_war[k % 4])
                    xld[k] = DMA(fe, fc, xring[:, (k % 4) * 512:(k % 4 + 1) * 512],
                                 fin["src_r"][:, d2, t0 + s2 * TS: t0 + (s2 + 1) * TS])
            use_bank(4)
            use_bank(5)
            WT("tensor", cs, cs.n)
            MM(ps[4][:], esel_sb[0:8, e * 128:(e + 1) * 128], crows[0:8, t0:t0 + TS],
               start=True, stop=True, inc=False)
            cb2 = MM(ps[5][:], esel_sb[0:8, e * 128:(e + 1) * 128],
                     crows[0:8, t0 + TS:t0 + 2 * TS], start=True, stop=True, inc=True)
            WT("scalar", ct, cb2)
            cbe1 = OP("scalar", lambda En: En.copy(cbc[:, 0:512], ps[4][:]))
            cbe2 = OP("scalar", lambda En: En.copy(cbc[:, 512:1024], ps[5][:]))
            release_bank(4, cs, cbe2)
            release_bank(5, cs, cbe2)
            for i in range(NI):
                (lc, lv), cwp = load_up_col(call, rnd, e, i, cast_mode)
                WT("tensor", lc, lv)
                slot = i % 2
                for s in range(2):
                    use_bank(s)
                    use_bank(2 + s)
                    gmm = None
                    for c in range(NCH):
                        gmm = MM(ps[s][:], wgb[:, slot * 2048 + c * 128: slot * 2048 + (c + 1) * 128],
                                 xn_r_sb[:, c * T + t0 + s * TS: c * T + t0 + (s + 1) * TS],
                                 start=(c == 0), stop=(c == NCH - 1), inc=(c == NCH - 1))
                    for c in range(NCH):
                        umm = MM(ps[2 + s][:], wub[:, slot * 2048 + c * 128: slot * 2048 + (c + 1) * 128],
                                 xn_r_sb[:, c * T + t0 + s * TS: c * T + t0 + (s + 1) * TS],
                                 start=(c == 0), stop=(c == NCH - 1), inc=(c == NCH - 1))
                    WT("scalar", ct, gmm)
                    sil = OP("scalar", lambda En, s=s: En.activation(
                        out=t1sc[:, s * 512:(s + 1) * 512], in_=ps[s][:], func=AF.Silu))
                    release_bank(s, cs, sil)
                    WT("vector", cs, sil)
                    WT("vector", ct, umm)
                    t2v = OP("vector", lambda En, s=s: En.tensor_mul(
                        t2sc[:, s * 512:(s + 1) * 512], t1sc[:, s * 512:(s + 1) * 512], ps[2 + s][:]))
                    release_bank(2 + s, cv, t2v)
                    WT("vector", cs, cbe2)
                    if prev_dn_end:
                        WT("vector", ct, prev_dn_end)
                    hv = OP("vector", lambda En, i=i, s=s: En.tensor_mul(
                        hbuf[:, i * RT + s * TS: i * RT + (s + 1) * TS],
                        t2sc[:, s * 512:(s + 1) * 512], cbc[:, s * 512:(s + 1) * 512]))
                upfree[slot].append((ct, ct.n))
                if cwp is not None:
                    upfree[slot].append(cwp)
                drain_bg(1)
            hready = cv.n
            if e == E - 1 and fin is not None:
                for k in range(4):
                    fin_load(k)
            for d in range(NCH):
                (lc, lv), cwp = load_dn_col(call, rnd, e, d, cast_mode)
                WT("tensor", lc, lv)
                WT("tensor", cv, hready)
                slot = d % 2
                for s in range(2):
                    b = 4 + s
                    use_bank(b)
                    ymm = None
                    for ci in range(NI):
                        ymm = MM(ps[b][:], wdb[:, slot * 1408 + ci * 128: slot * 1408 + (ci + 1) * 128],
                                 hbuf[:, ci * RT + s * TS: ci * RT + (s + 1) * TS],
                                 start=(ci == 0), stop=(ci == NI - 1), inc=(ci == NI - 1))
                    WT("vector", ct, ymm)
                    if e == 0:
                        av = OP("vector", lambda En, d=d, s=s, b=b: En.tensor_copy(
                            acc_sb[:, d * RT + s * TS: d * RT + (s + 1) * TS], ps[b][:]))
                        release_bank(b, cv, av)
                    elif e < E - 1 or fin is None:
                        av = OP("vector", lambda En, d=d, s=s, b=b: En.tensor_add(
                            acc_sb[:, d * RT + s * TS: d * RT + (s + 1) * TS],
                            acc_sb[:, d * RT + s * TS: d * RT + (s + 1) * TS], ps[b][:]))
                        release_bank(b, cv, av)
                    else:
                        k = d * 2 + s
                        tv = OP("vector", lambda En, d=d, s=s, b=b: En.tensor_add(
                            t2sc[:, s * 512:(s + 1) * 512],
                            acc_sb[:, d * RT + s * TS: d * RT + (s + 1) * TS], ps[b][:]))
                        release_bank(b, cv, tv)
                        WT("vector", fc, xld[k])
                        WT("vector", fc, st_war[s])
                        yv = OP("vector", lambda En, s=s, k=k: En.tensor_add(
                            t1sc[:, s * 512:(s + 1) * 512], t2sc[:, s * 512:(s + 1) * 512],
                            xring[:, (k % 4) * 512:(k % 4 + 1) * 512]))
                        xr_war[k % 4] = yv
                        WT(fe, cv, yv)
                        st = DMA(fe, fc, fin["dst"][:, d, t0 + s * TS: t0 + (s + 1) * TS],
                                 t1sc[:, s * 512:(s + 1) * 512])
                        st_war[s] = st
                        fin["last"] = st
                        if k + 4 < 2 * NCH:
                            fin_load(k + 4)
                dnfree[slot].append((ct, ct.n))
                if cwp is not None:
                    dnfree[slot].append(cwp)
                if d % 2 == 1:
                    drain_bg(1)
            prev_dn_end = ct.n
        PREV_DN[0] = ct.n

    def emit_ffn(call, src_r, xn_r_sb, acc_sb):
        def xn_dst(rnd, s):
            def f(c, rnd=rnd, s=s):
                return xn_r_sb[:, c * T + rnd * RT + s * TS: c * T + rnd * RT + (s + 1) * TS]
            return f

        cast0 = (call == 0)
        stq_r0 = ("sync", sx) if cast0 else ("gpsimd", sg)
        stq_r1 = ("gpsimd", sg)
        dst = x1_r if (call == 0 and stage >= 2) else y_r
        stq2_0 = ("scalar", sc) if (call == 0 and cast0) else None
        for s in range(2):
            for t in ln_thunks(lambda c, s=s: src_r[:, c, s * TS:(s + 1) * TS],
                               xn_dst(0, s), 0 * RT + s * TS, True, stq=stq_r0,
                               stq2=stq2_0):
                t()
        WT("tensor", cv, cv.n)
        for s in range(2):
            BGQ.extend(ln_thunks(lambda c, s=s: src_r[:, c, RT + s * TS: RT + (s + 1) * TS],
                                 xn_dst(1, s), RT + s * TS, True, stq=stq_r0))
        fin0 = {"src_r": src_r, "dst": dst, "stq": stq_r0, "last": 0}
        emit_experts(call, 0, xn_r_sb, acc_sb, fin=fin0)
        flush_bg()
        WT("tensor", cv, cv.n)

        if call == 0 and stage >= 2:
            BGQ.extend(fin_ln2_thunks(0, fin0["last"], stq_r1, fin0["stq"][1]))
        fin1 = {"src_r": src_r, "dst": dst, "stq": stq_r1, "last": 0}
        emit_experts(call, 1, xn_r_sb, acc_sb, fin=fin1)
        flush_bg()
        if call == 0 and stage >= 2:
            WT("sync", sg, sg.n)
            x2a = 0
            for c in range(NCH):
                x2a = DMA("sync", sx,
                          xn_r_sb[:, c * 2056 + 1: c * 2056 + 1 + RT],
                          xn2_r[:, c, 0:RT])
            X2PRE[0] = x2a
            for t in fin_ln2_thunks(1, fin1["last"], ("gpsimd", sg), sg):
                t()

    ffn_scope = ExitStack()
    scope_id = [0]

    def open_ffn_scope():
        k = scope_id[0]
        scope_id[0] += 1
        g = {}
        g["xn"] = ffn_scope.enter_context(nc.sbuf_tensor(f"xn_r_sb{k}", [128, NCH * T], BF16))
        g["acc"] = ffn_scope.enter_context(nc.sbuf_tensor(f"acc_sb{k}", [128, NCH * RT], BF16))
        g["h"] = ffn_scope.enter_context(nc.sbuf_tensor(f"hbuf{k}", [128, NI * RT], BF16))
        g["xr"] = ffn_scope.enter_context(nc.sbuf_tensor(f"xring{k}", [128, 4 * 512], F32))
        g["x2r"] = ffn_scope.enter_context(nc.sbuf_tensor(f"xn2ring{k}", [128, 2 * 512], BF16))
        g["t1"] = ffn_scope.enter_context(nc.sbuf_tensor(f"t1sc{k}", [128, 1024], F32))
        g["t2"] = ffn_scope.enter_context(nc.sbuf_tensor(f"t2sc{k}", [128, 1024], F32))
        g["cbc"] = ffn_scope.enter_context(nc.sbuf_tensor(f"cbc{k}", [128, 1024], F32))
        return g

    gsb = open_ffn_scope()
    xn_sbuf, acc_sbuf, hbuf = gsb["xn"], gsb["acc"], gsb["h"]
    xring, xn2ring = gsb["xr"], gsb["x2r"]
    t1sc, t2sc, cbc = gsb["t1"], gsb["t2"], gsb["cbc"]
    xn0_addr = nc.lookup_mloc(xn_sbuf).addr

    emit_ffn(0, xin_r, xn_sbuf, acc_sbuf)
    barrier()
    ffn_scope.close()

    if stage >= 2:
        attn_scope = ExitStack()
        xn2e = attn_scope.enter_context(nc.sbuf_tensor("xn2e", [128, NCH * 2056], BF16))
        assert nc.lookup_mloc(xn2e).addr == xn0_addr, "xn2e must alias xn"
        gates = attn_scope.enter_context(nc.sbuf_tensor("gates", [128, NCH * T], BF16))
        sub_scope = ExitStack()
        hat = sub_scope.enter_context(nc.sbuf_tensor("hat", [128, 4 * T], BF16))

        def xn2sl(c, a, b):
            return xn2e[:, c * 2056 + 1 + a: c * 2056 + 1 + b]

        x2l = 0
        for c in range(NCH):
            x2l = DMA("sync", sx, xn2sl(c, RT, T), xn2_r[:, c, RT:T])
        WT("tensor", sx, max(x2l, X2PRE[0]))

        att_upfree = {0: [], 1: []}

        def load_attn_col(src_view, idx, slot, nch_k):
            for (cc, vv2) in att_upfree[slot]:
                WT("gpsimd", cc, vv2)
            att_upfree[slot] = []
            dst = wgb[:, slot * 2048: slot * 2048 + nch_k * 128].rearrange(
                "p (c f) -> p c f", f=128)
            v = DMA("gpsimd", sg, dst, src_view[idx])
            return sg, v

        for hc in range(4):
            lc, lv = load_attn_col(wg1_rc, hc, hc % 2, NCH)
            WT("tensor", lc, lv)
            slot = hc % 2
            for s4 in range(NSUB):
                b = s4 % 2
                use_bank(b)
                gm = None
                for c in range(NCH):
                    gm = MM(ps[b][:], wgb[:, slot * 2048 + c * 128: slot * 2048 + (c + 1) * 128],
                            xn2sl(c, s4 * TS, (s4 + 1) * TS),
                            start=(c == 0), stop=(c == NCH - 1), inc=(c == NCH - 1))
                WT("scalar", ct, gm)
                sv = OP("scalar", lambda En, hc=hc, s4=s4, b=b: En.activation(
                    out=hat[:, hc * T + s4 * TS: hc * T + (s4 + 1) * TS], in_=ps[b][:],
                    func=AF.Relu, bias=bg1_sb[:, hc:hc + 1]))
                release_bank(b, cs, sv)
            att_upfree[slot].append((ct, ct.n))

        for rc in range(NCH):
            lc, lv = load_attn_col(wg2_rc, rc, rc % 2, 4)
            WT("tensor", lc, lv)
            slot = rc % 2
            for s4 in range(NSUB):
                b = 2 + s4 % 2
                use_bank(b)
                gm = None
                for hc in range(4):
                    gm = MM(ps[b][:], wgb[:, slot * 2048 + hc * 128: slot * 2048 + (hc + 1) * 128],
                            hat[:, hc * T + s4 * TS: hc * T + (s4 + 1) * TS],
                            start=(hc == 0), stop=(hc == 3), inc=(hc == 3))
                WT("scalar", ct, gm)
                sv = OP("scalar", lambda En, rc=rc, s4=s4, b=b: En.activation(
                    out=gates[:, rc * T + s4 * TS: rc * T + (s4 + 1) * TS], in_=ps[b][:],
                    func=AF.Sigmoid, bias=bg2_sb[:, rc:rc + 1]))
                release_bank(b, cs, sv)
            att_upfree[slot].append((ct, ct.n))
        barrier()
        sub_scope.close()
        smalls = ExitStack()
        ugsc = smalls.enter_context(nc.sbuf_tensor("ugsc", [128, 512], F32))
        cumd = smalls.enter_context(nc.sbuf_tensor("cumd", [128, 2 * 512], F32))
        cnvr = smalls.enter_context(nc.sbuf_tensor("cnvr", [128, 3 * 512], F32))
        xr1 = smalls.enter_context(nc.sbuf_tensor("xr1", [128, 3 * 512], F32))
        st2 = smalls.enter_context(nc.sbuf_tensor("st2", [128, 2 * 512], F32))

        cum_store = {0: 0, 1: 0}
        for rc in range(NCH):
            lc, lv = load_attn_col(wut_rc, rc, rc % 2, NCH)
            WT("tensor", lc, lv)
            slot = rc % 2
            for s4 in range(NSUB):
                b = s4
                use_bank(b)
                um = None
                for c in range(NCH):
                    um = MM(ps[b][:], wgb[:, slot * 2048 + c * 128: slot * 2048 + (c + 1) * 128],
                            xn2sl(c, s4 * TS, (s4 + 1) * TS),
                            start=(c == 0), stop=(c == NCH - 1), inc=(c == NCH - 1))
                WT("vector", ct, um)
                ug = OP("vector", lambda En, rc=rc, s4=s4, b=b: En.scalar_tensor_tensor(
                    out=ugsc[:], in0=ps[b][:], scalar=bu_sb[:, rc:rc + 1],
                    in1=gates[:, rc * T + s4 * TS: rc * T + (s4 + 1) * TS],
                    op0=ALU.add, op1=ALU.mult))
                release_bank(b, cv, ug)
                half = s4 % 2
                if s4 >= 2:
                    WT("vector", sx, cum_store[half])
                if s4 == 0:
                    scn = OP("vector", lambda En, half=half: En.tensor_tensor_scan(
                        out=cumd[:, half * 512:(half + 1) * 512], data0=onesc[:], data1=ugsc[:],
                        initial=0.0, op0=ALU.mult, op1=ALU.add))
                else:
                    ohalf = 1 - half
                    scn = OP("vector", lambda En, half=half, ohalf=ohalf: En.tensor_tensor_scan(
                        out=cumd[:, half * 512:(half + 1) * 512], data0=onesc[:], data1=ugsc[:],
                        initial=cumd[:, ohalf * 512 + 511: ohalf * 512 + 512],
                        op0=ALU.mult, op1=ALU.add))
                WT("sync", cv, scn)
                cum_store[half] = DMA("sync", sx, cum_r[:, rc, s4 * TS:(s4 + 1) * TS],
                                      cumd[:, half * 512:(half + 1) * 512])
            OP("vector", lambda En, rc=rc: En.tensor_copy(
                pay_sb[:, rc:rc + 1], cumd[:, 3 % 2 * 512 + 511: 3 % 2 * 512 + 512]))
            att_upfree[slot].append((ct, ct.n))
        paydone = cv.n

        pay_w = sx.n
        def exch_sync(En):
            with En.register("rp1") as rp1, En.register("rp2") as rp2:
                En.reg_load(rp1, par_sb[0:1, 0:1])
                En.reg_load(rp2, par_sb[0:1, 1:2])
                pv1 = En.snap(rp1, min_val=0, max_val=1)
                pv2 = En.snap(rp2, min_val=0, max_val=128)
                En.dma_start(out=shm.ap()[bass.ds(pv2, 128), :], in_=pay_sb[:]).then_inc(sx.h, 16)
                En.wait_ge(sx.h, pay_w + 16)
                En.dma_start(out=flg.ap()[bass.ds(pv1, 1), :], in_=ep_sb[:]).then_inc(sx.h, 16)
                En.wait_ge(sx.h, pay_w + 32)
        WT("sync", cv, paydone)
        EM("sync", exch_sync)
        sx.n += 32
        flag_w = sx.n

        def exch_poll(En):
            with En.register("pr") as pr, En.register("pe") as pe:
                En.reg_load(pe, ep_sb[0:1, 0:1])
                for q in (0, 1):
                    En.reg_mov(pr, 1)
                    with En.While(pr):
                        En.load(pr, flg.ap()[q:q + 1, 0:1])
                        En.reg_sub(pr, pr, pe)
        WT("gpsimd", cg, cg.n)
        EM("gpsimd", exch_poll)
        vA = DMA("gpsimd", sg, slA_sb[:], shm.ap()[0:128, :])
        vB = DMA("gpsimd", sg, slB_sb[:], shm.ap()[128:256, :])
        WT("vector", sg, vB)
        OP("vector", lambda En: En.tensor_scalar(
            out=offs_sb[:], in0=slA_sb[:, 0:16], scalar1=modd_sb[:, 0:1],
            scalar2=0.0, op0=ALU.mult, op1=ALU.add))
        for c in range(NCH):
            OP("vector", lambda En, c=c: En.tensor_scalar(
                out=xn2e[:, c * 2056: c * 2056 + 1], in0=slA_sb[:, 32 + c:33 + c],
                scalar1=modd_sb[:, 0:1], scalar2=0.0, op0=ALU.mult, op1=ALU.add))
            OP("vector", lambda En, c=c: En.tensor_scalar(
                out=xn2e[:, c * 2056 + 2049: c * 2056 + 2050], in0=slB_sb[:, 16 + c:17 + c],
                scalar1=mevn_sb[:, 0:1], scalar2=0.0, op0=ALU.mult, op1=ALU.add))
        exch_done = cv.n

        NTV = NCH * NSUB
        vcl = {}

        def v_cum_load(k, guard):
            rck, s4k = divmod(k, NSUB)
            WT("sync", cv, guard)
            vcl[k] = DMA("sync", sx, cumd[:, (k % 2) * 512:(k % 2 + 1) * 512],
                         cum_r[:, rck, s4k * TS:(s4k + 1) * TS])

        v_cum_load(0, 0)
        v_cum_load(1, 0)
        for rc in range(NCH):
            lc, lv = load_attn_col(wvt_rc, rc, rc % 2, NCH)
            WT("tensor", lc, lv)
            slot = rc % 2
            for s4 in range(NSUB):
                k = rc * NSUB + s4
                b = s4
                use_bank(b)
                vm = None
                for c in range(NCH):
                    vm = MM(ps[b][:], wgb[:, slot * 2048 + c * 128: slot * 2048 + (c + 1) * 128],
                            xn2sl(c, s4 * TS, (s4 + 1) * TS),
                            start=(c == 0), stop=(c == NCH - 1), inc=(c == NCH - 1))
                WT("vector", ct, vm)
                vg = OP("vector", lambda En, rc=rc, s4=s4, b=b: En.scalar_tensor_tensor(
                    out=ugsc[:], in0=ps[b][:], scalar=bv_sb[:, rc:rc + 1],
                    in1=gates[:, rc * T + s4 * TS: rc * T + (s4 + 1) * TS],
                    op0=ALU.add, op1=ALU.mult))
                release_bank(b, cv, vg)
                WT("vector", sx, vcl[k])
                if k == 0:
                    WT("vector", cv, exch_done)
                go = OP("vector", lambda En, rc=rc, s4=s4, k=k: En.scalar_tensor_tensor(
                    out=gates[:, rc * T + s4 * TS: rc * T + (s4 + 1) * TS],
                    in0=cumd[:, (k % 2) * 512:(k % 2 + 1) * 512],
                    scalar=offs_sb[:, rc:rc + 1], in1=ugsc[:],
                    op0=ALU.add, op1=ALU.mult))
                if k + 2 < NTV:
                    v_cum_load(k + 2, go)
            att_upfree[slot].append((ct, ct.n))
        barrier()
        if DEBUG:
            for c in range(NCH):
                DMA("sync", sx, dgo.rearrange("(c p) t -> p c t", p=128)[:, c, :],
                    gates[:, c * T:(c + 1) * T])
            barrier()

        NTU = NCH * NSUB
        uv_dst = x2_r if stage >= 3 else y_r
        uxr_war = [0, 0, 0]
        uxld = {}
        ustw = [0, 0]

        def uv_load(k):
            dck, s4k = divmod(k, NSUB)
            WT("sync", cv, uxr_war[k % 3])
            uxld[k] = DMA("sync", sx, xr1[:, (k % 3) * 512:(k % 3 + 1) * 512],
                          x1_r[:, dck, s4k * TS:(s4k + 1) * TS])

        def uv_conv(k):
            dck, s4k = divmod(k, NSUB)
            base = dck * 2056 + 1 + s4k * TS
            sl = k % 3
            OP("vector", lambda En, dc=dck, base=base, sl=sl: En.tensor_scalar(
                out=cnvr[:, sl * 512:(sl + 1) * 512], in0=xn2e[:, base - 1: base - 1 + TS],
                scalar1=cw_sb[:, dc * 3: dc * 3 + 1], scalar2=cb_sb[:, dc:dc + 1],
                op0=ALU.mult, op1=ALU.add))
            OP("vector", lambda En, dc=dck, base=base, sl=sl: En.scalar_tensor_tensor(
                out=cnvr[:, sl * 512:(sl + 1) * 512], in0=xn2e[:, base: base + TS],
                scalar=cw_sb[:, dc * 3 + 1: dc * 3 + 2],
                in1=cnvr[:, sl * 512:(sl + 1) * 512],
                op0=ALU.mult, op1=ALU.add))
            OP("vector", lambda En, dc=dck, base=base, sl=sl: En.scalar_tensor_tensor(
                out=cnvr[:, sl * 512:(sl + 1) * 512], in0=xn2e[:, base + 1: base + 1 + TS],
                scalar=cw_sb[:, dc * 3 + 2: dc * 3 + 3],
                in1=cnvr[:, sl * 512:(sl + 1) * 512],
                op0=ALU.mult, op1=ALU.add))

        for k in range(3):
            uv_load(k)
        uv_conv(0)
        uv_conv(1)
        for dc in range(NCH):
            lc, lv = load_attn_col(wut_rc, dc, dc % 2, NCH)
            WT("tensor", lc, lv)
            slot = dc % 2
            for s4 in range(NSUB):
                k = dc * NSUB + s4
                b = 4 + (k % 4)
                use_bank(b)
                uvm = None
                for rc in range(NCH):
                    uvm = MM(ps[b][:], wgb[:, slot * 2048 + rc * 128: slot * 2048 + (rc + 1) * 128],
                             gates[:, rc * T + s4 * TS: rc * T + (s4 + 1) * TS],
                             start=(rc == 0), stop=(rc == NCH - 1), inc=(rc == NCH - 1))
                if k + 2 < NTU:
                    uv_conv(k + 2)
                WT("vector", ct, uvm)
                bav = OP("vector", lambda En, b=b, sl=k % 3: En.tensor_add(
                    cnvr[:, sl * 512:(sl + 1) * 512], cnvr[:, sl * 512:(sl + 1) * 512],
                    ps[b][:]))
                release_bank(b, cv, bav)
                WT("vector", sx, uxld[k])
                WT("vector", sx, ustw[k % 2])
                x2v = OP("vector", lambda En, sl=k % 3, so=k % 2: En.tensor_add(
                    st2[:, so * 512:(so + 1) * 512], cnvr[:, sl * 512:(sl + 1) * 512],
                    xr1[:, sl * 512:(sl + 1) * 512]))
                uxr_war[k % 3] = x2v
                WT("sync", cv, x2v)
                ustw[k % 2] = DMA("sync", sx, uv_dst[:, dc, s4 * TS:(s4 + 1) * TS],
                                  st2[:, (k % 2) * 512:(k % 2 + 1) * 512])
                if k + 3 < NTU:
                    uv_load(k + 3)
            att_upfree[slot].append((ct, ct.n))
        barrier()
        smalls.close()
        attn_scope.close()

        if stage >= 3:
            gsb = open_ffn_scope()
            xn_sbuf, acc_sbuf, hbuf = gsb["xn"], gsb["acc"], gsb["h"]
            xring, xn2ring = gsb["xr"], gsb["x2r"]
            t1sc, t2sc, cbc = gsb["t1"], gsb["t2"], gsb["cbc"]
            emit_ffn(1, x2_r, xn_sbuf, acc_sbuf)
            barrier()
            ffn_scope.close()

    if DEBUG and stage >= 2:
        DMA("sync", sx, dx1, x1_s.ap())
        DMA("sync", sx, dxn2, xn2_s.ap())
        DMA("sync", sx, dcum, cum_s.ap())
    WT("sync", sx, sx.n)
    for c in ALLC:
        WT("sync", c, c.n)

    with nc.Block() as block:
        @block.sync
        def _(En):
            for f in Q["sync"]:
                f(En)

        @block.gpsimd
        def _(En):
            for f in Q["gpsimd"]:
                f(En)

        @block.vector
        def _(En):
            for f in Q["vector"]:
                f(En)

        @block.scalar
        def _(En):
            for f in Q["scalar"]:
                f(En)

        @block.tensor
        def _(En):
            for f in Q["tensor"]:
                f(En)

    est.close()
    return nc



def kernel(L0_attn_x, L0_attn_Wu, L0_attn_bu, L0_attn_Wv, L0_attn_bv,
           L0_attn_Wg1, L0_attn_bg1, L0_attn_Wg2, L0_attn_bg2,
           L0_attn_conv_w, L0_attn_conv_b,
           L0_ffn_Wr, L0_ffn_Wg, L0_ffn_Wu, L0_ffn_Wd):
    x = np.ascontiguousarray(np.asarray(L0_attn_x, dtype=np.float32))
    B, L, Dm = x.shape
    assert (B, L, Dm) == (4, 4096, 2048)

    wg = np.ascontiguousarray(np.asarray(L0_ffn_Wg, dtype=np.float32))
    wu = np.ascontiguousarray(np.asarray(L0_ffn_Wu, dtype=np.float32))
    wd = np.ascontiguousarray(np.asarray(L0_ffn_Wd, dtype=np.float32))
    wr = np.ascontiguousarray(np.asarray(L0_ffn_Wr, dtype=np.float32)
                              .reshape(16, 128, 8).transpose(1, 0, 2).reshape(128, 128))
    wut = np.ascontiguousarray(np.asarray(L0_attn_Wu, dtype=np.float32).T)
    wvt = np.ascontiguousarray(np.asarray(L0_attn_Wv, dtype=np.float32).T)
    wg1t = np.ascontiguousarray(np.asarray(L0_attn_Wg1, dtype=np.float32).T)
    wg2t = np.ascontiguousarray(np.asarray(L0_attn_Wg2, dtype=np.float32).T)
    def chan_swz(a, nch):
        a = np.asarray(a, dtype=np.float32)
        return np.ascontiguousarray(a.reshape(nch, 128).T)
    bu = chan_swz(L0_attn_bu, 16)
    bv = chan_swz(L0_attn_bv, 16)
    bg1 = chan_swz(L0_attn_bg1, 4)
    bg2 = chan_swz(L0_attn_bg2, 16)
    cw = np.ascontiguousarray(np.asarray(L0_attn_conv_w, dtype=np.float32)
                              .reshape(16, 128, 3).transpose(1, 0, 2).reshape(128, 48))
    cb = chan_swz(L0_attn_conv_b, 16)
    ident = np.eye(128, dtype=np.float32)
    esel = np.zeros((8, 8 * 128), dtype=np.float32)
    for e_ in range(8):
        esel[e_, e_ * 128:(e_ + 1) * 128] = 1.0
    epoch = np.full((1, 16), (int(time.time() * 1000) % (1 << 29)) | 1, np.int32)

    nc = build(STAGE)
    in_maps = []
    for c in range(NCORES):
        b = c // 2
        h = c % 2
        xs = np.ascontiguousarray(x[b, h * T:(h + 1) * T, :].T)
        par = np.zeros((1, 16), np.int32)
        par[0, 0] = h
        par[0, 1] = h * 128
        modd = np.full((128, 1), 1.0 if h == 1 else 0.0, np.float32)
        mevn = np.full((128, 1), 1.0 if h == 0 else 0.0, np.float32)
        in_maps.append({
            "xin": xs, "fwg": wg, "fwu": wu, "fwd": wd, "fwr": wr,
            "awut": wut, "awvt": wvt, "awg1t": wg1t, "awg2t": wg2t,
            "abu": bu, "abv": bv, "abg1": bg1, "abg2": bg2,
            "acw": cw, "acb": cb, "ident": ident, "esel": esel, "par": par,
            "epoch": epoch, "modd": modd, "mevn": mevn,
        })
    res = run_bass_kernel_spmd(nc, in_maps, core_ids=list(range(NCORES)))
    out = np.empty((4, 4096, 2048), np.float32)
    for c in range(NCORES):
        b = c // 2
        h = c % 2
        out[b, h * T:(h + 1) * T, :] = res.results[c]["yout"].T
    return out


# version 08
# speedup vs baseline: 1.5940x; 1.0885x over previous
import time
import numpy as np
from contextlib import ExitStack

import concourse.bass as bass
import concourse.mybir as mybir
from concourse.bass_utils import run_bass_kernel_spmd

F32 = mybir.dt.float32
BF16 = mybir.dt.bfloat16
I32 = mybir.dt.int32
AF = mybir.ActivationFunctionType
ALU = mybir.AluOpType

NCORES = 8
D = 2048
NCH = 16
T = 2048
E = 8
IFF = 1408
NI = 11
RT = 1024
TS = 512
NSUB = 4
EPS = 1e-5

STAGE = 3
DEBUG = False



def build(stage=STAGE):
    nc = bass.Bass("TRN2", target_bir_lowering=False)

    xin = nc.dram_tensor("xin", [D, T], F32, kind="ExternalInput").ap()
    wgu_h = nc.dram_tensor("fwgu", [E * NI * 128, 2 * 2048], F32, kind="ExternalInput").ap()
    wd_h = nc.dram_tensor("fwd", [E * NCH * 128, IFF], F32, kind="ExternalInput").ap()
    wr_h = nc.dram_tensor("fwr", [128, 128], F32, kind="ExternalInput").ap()
    wut_h = nc.dram_tensor("awut", [NCH * 128, 2048], F32, kind="ExternalInput").ap()
    wvt_h = nc.dram_tensor("awvt", [NCH * 128, 2048], F32, kind="ExternalInput").ap()
    wg1t_h = nc.dram_tensor("awg1t", [4 * 128, 2048], F32, kind="ExternalInput").ap()
    wg2t_h = nc.dram_tensor("awg2t", [NCH * 128, 512], F32, kind="ExternalInput").ap()
    bu_h = nc.dram_tensor("abu", [128, 16], F32, kind="ExternalInput").ap()
    bv_h = nc.dram_tensor("abv", [128, 16], F32, kind="ExternalInput").ap()
    bg1_h = nc.dram_tensor("abg1", [128, 4], F32, kind="ExternalInput").ap()
    bg2_h = nc.dram_tensor("abg2", [128, 16], F32, kind="ExternalInput").ap()
    cw_h = nc.dram_tensor("acw", [128, 48], F32, kind="ExternalInput").ap()
    cb_h = nc.dram_tensor("acb", [128, 16], F32, kind="ExternalInput").ap()
    ident_h = nc.dram_tensor("ident", [128, 128], F32, kind="ExternalInput").ap()
    esel_h = nc.dram_tensor("esel", [8, 8 * 128], F32, kind="ExternalInput").ap()
    par_h = nc.dram_tensor("par", [1, 16], I32, kind="ExternalInput").ap()
    epoch_h = nc.dram_tensor("epoch", [1, 16], I32, kind="ExternalInput").ap()
    modd_h = nc.dram_tensor("modd", [128, 1], F32, kind="ExternalInput").ap()
    mevn_h = nc.dram_tensor("mevn", [128, 1], F32, kind="ExternalInput").ap()
    yout = nc.dram_tensor("yout", [D, T], F32, kind="ExternalOutput").ap()
    if DEBUG:
        dx1 = nc.dram_tensor("dx1", [D, T], F32, kind="ExternalOutput").ap()
        dxn2 = nc.dram_tensor("dxn2", [D, T], BF16, kind="ExternalOutput").ap()
        dcum = nc.dram_tensor("dcum", [D, T], F32, kind="ExternalOutput").ap()
        dgo = nc.dram_tensor("dgo", [D, T], BF16, kind="ExternalOutput").ap()

    x1_s = nc.dram_tensor("x1_s", [D, T], F32)
    x2_s = nc.dram_tensor("x2_s", [D, T], F32)
    xn2_s = nc.dram_tensor("xn2_s", [D, T], BF16)
    cum_s = nc.dram_tensor("cum_s", [D, T], F32)
    wcu = nc.dram_tensor("wcu", [E * NI, 2 * NCH * 128 * 128], BF16)
    wcd = nc.dram_tensor("wcd", [E * NCH, NI * 128 * 128], BF16)
    shm = nc.dram_tensor("shmx", [256, 48], F32, addr_space="Shared")
    flg = nc.dram_tensor("flgx", [2, 16], I32, addr_space="Shared")

    xin_r = xin.rearrange("(c p) t -> p c t", p=128)
    y_r = yout.rearrange("(c p) t -> p c t", p=128)
    x1_r = x1_s.ap().rearrange("(c p) t -> p c t", p=128)
    x2_r = x2_s.ap().rearrange("(c p) t -> p c t", p=128)
    xn2_r = xn2_s.ap().rearrange("(c p) t -> p c t", p=128)
    cum_r = cum_s.ap().rearrange("(c p) t -> p c t", p=128)
    wgu_v = wgu_h.rearrange("(n p) f -> n p f", p=128)
    wd_v = wd_h.rearrange("(n p) f -> n p f", p=128)
    wut_rc = wut_h.rearrange("(n p) f -> n p f", p=128)
    wvt_rc = wvt_h.rearrange("(n p) f -> n p f", p=128)
    wg1_rc = wg1t_h.rearrange("(n p) f -> n p f", p=128)
    wg2_rc = wg2t_h.rearrange("(n p) f -> n p f", p=128)
    wcu_v = wcu.ap().rearrange("n (p f) -> n p f", p=128)
    wcd_v = wcd.ap().rearrange("n (p f) -> n p f", p=128)

    ENGS = ("sync", "gpsimd", "vector", "scalar", "tensor")
    Q = {e: [] for e in ENGS}

    def EM(eng, fn):
        Q[eng].append(fn)

    class Ctr:
        def __init__(self, name):
            self.cm = nc.semaphore(name)
            self.h = self.cm.__enter__()
            self.n = 0

    sx = Ctr("sx")
    sc = Ctr("sc")
    sg = Ctr("sg")
    cv = Ctr("cv")
    cs = Ctr("cs")
    cg = Ctr("cg")
    ct = Ctr("ct")
    ALLC = [sx, sc, sg, cv, cs, cg, ct]

    def WT(eng, ctr, val):
        if val <= 0:
            return
        EM(eng, lambda En, h=ctr.h, v=val: En.wait_ge(h, v))

    def DMA(eng, ctr, out, in_):
        def f(En, o=out, i=in_, h=ctr.h):
            En.dma_start(out=o, in_=i).then_inc(h, 16)
        EM(eng, f)
        ctr.n += 16
        return ctr.n

    ENG2CTR = {"vector": cv, "scalar": cs, "gpsimd": cg, "tensor": ct}

    def OP(eng, fn, flush=True):
        ctr = ENG2CTR[eng]
        if flush and ctr.n > 0:
            WT(eng, ctr, ctr.n)
        def g(En, fn=fn, h=ctr.h):
            fn(En).then_inc(h, 1)
        EM(eng, g)
        ctr.n += 1
        return ctr.n

    def MM(out, lhsT, rhs, start, stop, inc=False):
        if inc:
            def f(En, o=out, l=lhsT, r=rhs, st=start, sp=stop, h=ct.h):
                En.matmul(o, l, r, start=st, stop=sp).then_inc(h, 1)
            EM("tensor", f)
            ct.n += 1
            return ct.n
        else:
            def f(En, o=out, l=lhsT, r=rhs, st=start, sp=stop):
                En.matmul(o, l, r, start=st, stop=sp)
            EM("tensor", f)
            return None

    def barrier():
        for e in ENGS:
            for c in ALLC:
                WT(e, c, c.n)

    bank_free = [[] for _ in range(8)]

    def use_bank(b):
        for (c, v) in bank_free[b]:
            WT("tensor", c, v)
        bank_free[b] = []

    def release_bank(b, ctr, val):
        bank_free[b].append((ctr, val))

    est = ExitStack()

    def sb(name, shape, dt):
        return est.enter_context(nc.sbuf_tensor(name, shape, dt))

    bu_sb = sb("bu_sb", [128, 16], F32)
    bv_sb = sb("bv_sb", [128, 16], F32)
    bg2_sb = sb("bg2_sb", [128, 16], F32)
    cb_sb = sb("cb_sb", [128, 16], F32)
    bg1_sb = sb("bg1_sb", [128, 4], F32)
    cw_sb = sb("cw_sb", [128, 48], F32)
    wr_sb = sb("wr_sb", [128, 128], F32)
    wrb_sb = sb("wrb_sb", [128, 128], BF16)
    id_sb = sb("id_sb", [128, 128], F32)
    esel_sb = sb("esel_sb", [8, 8 * 128], F32)
    ones_b = sb("ones_b", [128, 1], BF16)
    ones1 = sb("ones1", [1, 128], F32)
    onesc = sb("onesc", [128, 512], F32)
    crows = sb("crows", [8, 2048], F32)
    lnsm = sb("lnsm", [1, 4 * 512], F32)
    lg_sb = sb("lg_sb", [8, 512], F32)
    tm_sb = sb("tm_sb", [128, 32], F32)
    tm2_sb = sb("tm2_sb", [128, 32], F32)
    pay_sb = sb("pay_sb", [128, 48], F32)
    slA_sb = sb("slA_sb", [128, 48], F32)
    slB_sb = sb("slB_sb", [128, 48], F32)
    offs_sb = sb("offs_sb", [128, 16], F32)
    modd_sb = sb("modd_sb", [128, 1], F32)
    mevn_sb = sb("mevn_sb", [128, 1], F32)
    par_sb = sb("par_sb", [1, 16], I32)
    ep_sb = sb("ep_sb", [1, 16], I32)
    wgub = sb("wgub", [128, 2 * 4096], BF16)
    wdb = sb("wdb", [128, 2 * 1408], BF16)
    xbfc = sb("xbfc", [128, 2 * 512], BF16)
    epsv = sb("epsv", [1, 1], F32)
    sqfc = sb("sqfc", [128, 2 * 512], BF16)

    ps = [est.enter_context(nc.psum_tensor(f"ps{i}", [128, 512], F32)) for i in range(8)]

    small_loads = [
        (bu_sb[:], bu_h), (bv_sb[:], bv_h), (bg2_sb[:], bg2_h), (cb_sb[:], cb_h),
        (bg1_sb[:], bg1_h), (cw_sb[:], cw_h), (wr_sb[:], wr_h), (id_sb[:], ident_h),
        (modd_sb[:], modd_h), (mevn_sb[:], mevn_h), (par_sb[:], par_h), (ep_sb[:], epoch_h),
        (esel_sb[:], esel_h),
    ]
    for dst, srcv in small_loads:
        DMA("sync", sx, dst, srcv)
    v_pro = sx.n
    OP("vector", lambda En: En.memset(ones_b[:], 1.0))
    OP("vector", lambda En: En.memset(ones1[:], 1.0))
    OP("vector", lambda En: En.memset(onesc[:], 1.0))
    OP("vector", lambda En: En.memset(epsv[:], EPS))
    WT("vector", sx, v_pro)
    OP("vector", lambda En: En.tensor_copy(wrb_sb[:], wr_sb[:]))
    pro_done = cv.n

    xn2war = {0: 0, 1: 0}
    xn2war_c = [None]
    PREV_DN = [0]
    X2PRE = [0]
    BGQ = []

    def drain_bg(n=1):
        for _ in range(n):
            if not BGQ:
                return
            BGQ.pop(0)()

    def flush_bg():
        while BGQ:
            BGQ.pop(0)()

    def ln_thunks(chunk_src, xn_dst, c_off, with_router, src_guard=None,
                  post_norm=None, stq=None, stq2=None):
        th = []
        stq_e, stq_c = stq

        def q(c):
            return stq if (stq2 is None or c % 2 == 0) else stq2
        state = {"war": {0: 0, 1: 0, 2: 0, 3: 0}, "ld": {}, "bmm": 0, "norm_done": 0}

        def stats_group(c0):
            def f(c0=c0):
                if c0 == 0:
                    use_bank(6)
                    use_bank(7)
                    if src_guard is not None:
                        WT(stq_e, src_guard[0], src_guard[1])
                        if stq2 is not None:
                            WT(stq2[0], src_guard[0], src_guard[1])
                for c in range(c0, c0 + 4):
                    sl = c % 2
                    qe, qc = q(c)
                    WT(qe, cv, state["war"][sl])
                    lv = DMA(qe, qc, xring[:, sl * 512:(sl + 1) * 512], chunk_src(c))
                    WT("vector", qc, lv)
                    WT("vector", ct, state["ld"].get(c - 2, 0))
                    vcp = OP("vector", lambda En, sl=sl: En.tensor_copy(
                        xbfc[:, sl * 512:(sl + 1) * 512], xring[:, sl * 512:(sl + 1) * 512]))
                    state["war"][sl] = vcp
                    WT("scalar", ct, state["ld"].get(c - 2, 0))
                    WT("scalar", cv, vcp)
                    scp = OP("scalar", lambda En, sl=sl: En.activation(
                        out=sqfc[:, sl * 512:(sl + 1) * 512],
                        in_=xbfc[:, sl * 512:(sl + 1) * 512], func=AF.Square))
                    WT("tensor", cv, vcp)
                    WT("tensor", cs, scp)
                    MM(ps[6][0:1, :], ones_b[:], xbfc[:, sl * 512:(sl + 1) * 512],
                       start=(c == 0), stop=(c == NCH - 1), inc=False)
                    state["ld"][c] = MM(ps[7][0:1, :], ones_b[:], sqfc[:, sl * 512:(sl + 1) * 512],
                                        start=(c == 0), stop=(c == NCH - 1), inc=True)
                    if c + 2 <= NCH - 1:
                        pass
            return f

        def stats_group_guarded(c0):
            base = stats_group(c0)
            def f():
                base()
            return f

        for c0 in (0, 4, 8, 12):
            th.append(stats_group(c0))

        def meanvar():
            stats_done = ct.n
            WT("scalar", ct, stats_done)
            s1 = OP("scalar", lambda En: En.mul(lnsm[0:1, 0:512], ps[6][0:1, :], 1.0 / D))
            s2 = OP("scalar", lambda En: En.mul(lnsm[0:1, 512:1024], ps[7][0:1, :], 1.0 / D))
            release_bank(6, cs, s2)
            release_bank(7, cs, s2)
            WT("vector", cs, s2)
            OP("vector", lambda En: En.tensor_mul(
                lnsm[0:1, 1536:2048], lnsm[0:1, 0:512], lnsm[0:1, 0:512]))
            OP("vector", lambda En: En.tensor_sub(
                lnsm[0:1, 512:1024], lnsm[0:1, 512:1024], lnsm[0:1, 1536:2048]))
            vv = cv.n
            WT("scalar", cv, vv)
            sd = OP("scalar", lambda En: En.activation(
                out=lnsm[0:1, 1024:1536], in_=lnsm[0:1, 512:1024], func=AF.Sqrt,
                bias=epsv[0:1, 0:1]))
            WT("vector", cs, sd)
            rst = OP("vector", lambda En: En.reciprocal(lnsm[0:1, 1024:1536], lnsm[0:1, 1024:1536]))
            use_bank(6)
            use_bank(7)
            WT("tensor", cv, rst)
            MM(ps[6][:], ones1[:], lnsm[0:1, 0:512], start=True, stop=True, inc=False)
            state["bmm"] = MM(ps[7][:], ones1[:], lnsm[0:1, 1024:1536],
                              start=True, stop=True, inc=True)
        th.append(meanvar)

        def norm_group(c0):
            def f(c0=c0):
                if c0 == 0:
                    WT("vector", ct, state["bmm"])
                for c in range(c0, c0 + 4):
                    sl = 2 + c % 2
                    qe, qc = q(c)
                    WT(qe, cv, state["war"][sl])
                    lv = DMA(qe, qc, xring[:, sl * 512:(sl + 1) * 512], chunk_src(c))
                    WT("vector", qc, lv)
                    OP("vector", lambda En, sl=sl: En.tensor_sub(
                        xring[:, sl * 512:(sl + 1) * 512],
                        xring[:, sl * 512:(sl + 1) * 512], ps[6][:]))
                    if xn_dst is not None:
                        mv = OP("vector", lambda En, sl=sl, d=xn_dst(c): En.tensor_mul(
                            d, xring[:, sl * 512:(sl + 1) * 512], ps[7][:]))
                        state["war"][sl] = mv
                    else:
                        rsl = c % 2
                        WT("vector", xn2war_c[0], xn2war[rsl])
                        mv = OP("vector", lambda En, sl=sl, rsl=rsl: En.tensor_mul(
                            xn2ring[:, rsl * 512:(rsl + 1) * 512],
                            xring[:, sl * 512:(sl + 1) * 512], ps[7][:]))
                        state["war"][sl] = mv
                        post_norm(c, rsl, mv)
                if c0 == 12:
                    state["norm_done"] = cv.n
                    release_bank(6, cv, cv.n)
                    release_bank(7, cv, cv.n)
            return f

        for c0 in (0, 4, 8, 12):
            th.append(norm_group(c0))

        if with_router:
            def router1():
                use_bank(6)
                WT("tensor", cv, state["norm_done"])
                lgd = None
                for c in range(NCH):
                    lgd = MM(ps[6][0:8, :], wrb_sb[:, c * 8:(c + 1) * 8], xn_dst(c),
                             start=(c == 0), stop=(c == NCH - 1), inc=(c == NCH - 1))
                WT("scalar", ct, lgd)
                lge = OP("scalar", lambda En: En.copy(lg_sb[:], ps[6][0:8, :]))
                release_bank(6, cs, lge)
                use_bank(7)
                for j in range(4):
                    WT("tensor", cs, cs.n)
                    tj = MM(ps[7][0:128, 0:8], lg_sb[0:8, j * 128:(j + 1) * 128],
                            id_sb[0:8, 0:8], start=True, stop=True, inc=True)
                    WT("scalar", ct, tj)
                    OP("scalar", lambda En, j=j: En.copy(tm_sb[:, j * 8:(j + 1) * 8],
                                                         ps[7][0:128, 0:8]))
                release_bank(7, cs, cs.n)
            th.append(router1)

            def router2():
                WT("scalar", cs, cs.n)
                ex = OP("scalar", lambda En: En.activation(out=tm_sb[:], in_=tm_sb[:], func=AF.Exp))
                WT("vector", cs, ex)
                for j in range(4):
                    b0 = j * 8
                    OP("vector", lambda En, b0=b0: En.reduce_max(
                        offs_sb[:, 0:1], tm_sb[:, b0:b0 + 8], axis=mybir.AxisListType.X))
                    OP("vector", lambda En, b0=b0: En.tensor_scalar(
                        out=tm2_sb[:, b0:b0 + 8], in0=tm_sb[:, b0:b0 + 8],
                        scalar1=offs_sb[:, 0:1], scalar2=1.0, op0=ALU.is_equal, op1=ALU.mult))
                    OP("vector", lambda En, b0=b0: En.scalar_tensor_tensor(
                        out=tm2_sb[:, b0:b0 + 8], in0=tm2_sb[:, b0:b0 + 8], scalar=-1e30,
                        in1=tm_sb[:, b0:b0 + 8], op0=ALU.mult, op1=ALU.add))
                    OP("vector", lambda En, b0=b0: En.reduce_max(
                        offs_sb[:, 1:2], tm2_sb[:, b0:b0 + 8], axis=mybir.AxisListType.X))
                    OP("vector", lambda En: En.tensor_add(offs_sb[:, 2:3], offs_sb[:, 0:1], offs_sb[:, 1:2]))
                    OP("vector", lambda En: En.reciprocal(offs_sb[:, 2:3], offs_sb[:, 2:3]))
                    OP("vector", lambda En, b0=b0: En.tensor_scalar(
                        out=tm2_sb[:, b0:b0 + 8], in0=tm_sb[:, b0:b0 + 8],
                        scalar1=offs_sb[:, 1:2], scalar2=1.0, op0=ALU.is_ge, op1=ALU.mult))
                    OP("vector", lambda En, b0=b0: En.tensor_mul(
                        tm2_sb[:, b0:b0 + 8], tm2_sb[:, b0:b0 + 8], tm_sb[:, b0:b0 + 8]))
                    OP("vector", lambda En, b0=b0: En.tensor_scalar(
                        out=tm2_sb[:, b0:b0 + 8], in0=tm2_sb[:, b0:b0 + 8],
                        scalar1=offs_sb[:, 2:3], scalar2=0.5, op0=ALU.mult, op1=ALU.mult))
                c2done = cv.n
                use_bank(6)
                for j in range(4):
                    WT("tensor", cv, c2done)
                    if j > 0:
                        WT("tensor", cs, cs.n)
                    tj = MM(ps[6][0:8, 0:128], tm2_sb[:, j * 8:(j + 1) * 8],
                            id_sb[:, :], start=True, stop=True, inc=True)
                    WT("scalar", ct, tj)
                    OP("scalar", lambda En, j=j: En.copy(
                        crows[0:8, c_off + j * 128: c_off + (j + 1) * 128], ps[6][0:8, 0:128]))
                release_bank(6, cs, cs.n)
            th.append(router2)
        return th

    upfree = {0: [], 1: []}
    dnfree = {0: [], 1: []}

    def load_up_col(call, rnd, e, i, cast_mode):
        slot = i % 2
        for (cc, vv2) in upfree[slot]:
            WT("gpsimd" if cast_mode else "sync", cc, vv2)
        upfree[slot] = []
        dst = wgub[:, slot * 4096:(slot + 1) * 4096]
        if cast_mode:
            v = DMA("gpsimd", sg, dst, wgu_v[e * NI + i])
            WT("scalar", sg, v)
            cw = DMA("scalar", sc, wcu_v[e * NI + i], dst)
            return (sg, v), (sc, cw)
        else:
            v = DMA("sync", sx, dst, wcu_v[e * NI + i])
            return (sx, v), None

    def load_dn_col(call, rnd, e, d, cast_mode):
        slot = d % 2
        for (cc, vv2) in dnfree[slot]:
            WT("gpsimd" if cast_mode else "sync", cc, vv2)
        dnfree[slot] = []
        dst = wdb[:, slot * 1408:(slot + 1) * 1408]
        if cast_mode:
            v1 = DMA("gpsimd", sg, dst, wd_v[e * NCH + d])
            WT("scalar", sg, v1)
            cwv = DMA("scalar", sc, wcd_v[e * NCH + d], dst)
            return (sg, v1), (sc, cwv)
        else:
            v1 = DMA("sync", sx, dst, wcd_v[e * NCH + d])
            return (sx, v1), None

    def fin_quick(call, rnd, src_r, acc_sb, stq):
        stq_e, stq_c = stq
        if call == 0 and stage >= 2:
            dst = x1_r
        else:
            dst = y_r
        war = {0: 0, 1: 0}
        last = 0
        for s in range(2):
            toff = rnd * RT + s * TS
            for c in range(NCH):
                sl = c % 2
                WT(stq_e, stq_c, war[sl])
                lv = DMA(stq_e, stq_c, xring[:, sl * 512:(sl + 1) * 512],
                         src_r[:, c, toff:toff + TS])
                WT("vector", stq_c, lv)
                av = OP("vector", lambda En, c=c, s=s, sl=sl: En.tensor_add(
                    xring[:, sl * 512:(sl + 1) * 512], xring[:, sl * 512:(sl + 1) * 512],
                    acc_sb[:, c * RT + s * TS: c * RT + (s + 1) * TS]))
                WT(stq_e, cv, av)
                war[sl] = DMA(stq_e, stq_c, dst[:, c, toff:toff + TS],
                              xring[:, sl * 512:(sl + 1) * 512])
                last = war[sl]
        return dst, last

    def fin_ln2_thunks(rnd, store_done, stq, store_ctr):
        ths = []
        stq_e, stq_c = stq
        xn2war_c[0] = stq_c
        for s in range(2):
            toff = rnd * RT + s * TS

            def post_norm(c, rsl, mv, toff=toff, rnd=rnd, s=s):
                if rnd == 0 and s == 0:
                    OP("vector", lambda En, c=c, rsl=rsl: En.tensor_copy(
                        pay_sb[:, 16 + c:17 + c], xn2ring[:, rsl * 512:rsl * 512 + 1]))
                if rnd == 1 and s == 1:
                    OP("vector", lambda En, c=c, rsl=rsl: En.tensor_copy(
                        pay_sb[:, 32 + c:33 + c], xn2ring[:, rsl * 512 + 511:rsl * 512 + 512]))
                WT(stq_e, cv, cv.n)
                xn2war[rsl] = DMA(stq_e, stq_c, xn2_r[:, c, toff:toff + TS],
                                  xn2ring[:, rsl * 512:(rsl + 1) * 512])

            ths += ln_thunks(
                lambda c, toff=toff: x1_r[:, c, toff:toff + TS],
                None, toff, False, src_guard=(store_ctr, store_done), post_norm=post_norm,
                stq=stq)
        return ths

    def emit_experts(call, rnd, xn_r_sb, acc_sb, fin=None):
        cast_mode = (call == 0 and rnd == 0)
        t0 = rnd * RT
        prev_dn_end = PREV_DN[0]
        for e in range(E):
            if e == E - 1 and fin is not None:
                flush_bg()
                fe, fc = fin["stq"]
                fin_base_cv = cv.n
                xld = {}
                xr_war = [fin_base_cv] * 4
                st_war = [0, 0]

                def fin_load(k):
                    d2, s2 = divmod(k, 2)
                    WT(fe, cv, xr_war[k % 4])
                    xld[k] = DMA(fe, fc, xring[:, (k % 4) * 512:(k % 4 + 1) * 512],
                                 fin["src_r"][:, d2, t0 + s2 * TS: t0 + (s2 + 1) * TS])
            use_bank(4)
            use_bank(5)
            WT("tensor", cs, cs.n)
            MM(ps[4][:], esel_sb[0:8, e * 128:(e + 1) * 128], crows[0:8, t0:t0 + TS],
               start=True, stop=True, inc=False)
            cb2 = MM(ps[5][:], esel_sb[0:8, e * 128:(e + 1) * 128],
                     crows[0:8, t0 + TS:t0 + 2 * TS], start=True, stop=True, inc=True)
            WT("scalar", ct, cb2)
            cbe1 = OP("scalar", lambda En: En.copy(cbc[:, 0:512], ps[4][:]))
            cbe2 = OP("scalar", lambda En: En.copy(cbc[:, 512:1024], ps[5][:]))
            release_bank(4, cs, cbe2)
            release_bank(5, cs, cbe2)
            for i in range(NI):
                (lc, lv), cwp = load_up_col(call, rnd, e, i, cast_mode)
                WT("tensor", lc, lv)
                slot = i % 2
                for s in range(2):
                    use_bank(s)
                    use_bank(2 + s)
                    gmm = None
                    for c in range(NCH):
                        gmm = MM(ps[s][:], wgub[:, slot * 4096 + c * 128: slot * 4096 + (c + 1) * 128],
                                 xn_r_sb[:, c * T + t0 + s * TS: c * T + t0 + (s + 1) * TS],
                                 start=(c == 0), stop=(c == NCH - 1), inc=(c == NCH - 1))
                    for c in range(NCH):
                        umm = MM(ps[2 + s][:], wgub[:, slot * 4096 + 2048 + c * 128: slot * 4096 + 2048 + (c + 1) * 128],
                                 xn_r_sb[:, c * T + t0 + s * TS: c * T + t0 + (s + 1) * TS],
                                 start=(c == 0), stop=(c == NCH - 1), inc=(c == NCH - 1))
                    WT("scalar", ct, gmm)
                    sil = OP("scalar", lambda En, s=s: En.activation(
                        out=t1sc[:, s * 512:(s + 1) * 512], in_=ps[s][:], func=AF.Silu))
                    release_bank(s, cs, sil)
                    WT("vector", cs, sil)
                    WT("vector", ct, umm)
                    t2v = OP("vector", lambda En, s=s: En.tensor_mul(
                        t2sc[:, s * 512:(s + 1) * 512], t1sc[:, s * 512:(s + 1) * 512], ps[2 + s][:]))
                    release_bank(2 + s, cv, t2v)
                    WT("vector", cs, cbe2)
                    if prev_dn_end:
                        WT("vector", ct, prev_dn_end)
                    hv = OP("vector", lambda En, i=i, s=s: En.tensor_mul(
                        hbuf[:, i * RT + s * TS: i * RT + (s + 1) * TS],
                        t2sc[:, s * 512:(s + 1) * 512], cbc[:, s * 512:(s + 1) * 512]))
                upfree[slot].append((ct, ct.n))
                if cwp is not None:
                    upfree[slot].append(cwp)
                drain_bg(1)
            hready = cv.n
            if e == E - 1 and fin is not None:
                for k in range(4):
                    fin_load(k)
            for d in range(NCH):
                (lc, lv), cwp = load_dn_col(call, rnd, e, d, cast_mode)
                WT("tensor", lc, lv)
                WT("tensor", cv, hready)
                slot = d % 2
                for s in range(2):
                    b = 4 + s
                    use_bank(b)
                    ymm = None
                    for ci in range(NI):
                        ymm = MM(ps[b][:], wdb[:, slot * 1408 + ci * 128: slot * 1408 + (ci + 1) * 128],
                                 hbuf[:, ci * RT + s * TS: ci * RT + (s + 1) * TS],
                                 start=(ci == 0), stop=(ci == NI - 1), inc=(ci == NI - 1))
                    WT("vector", ct, ymm)
                    if e == 0:
                        av = OP("vector", lambda En, d=d, s=s, b=b: En.tensor_copy(
                            acc_sb[:, d * RT + s * TS: d * RT + (s + 1) * TS], ps[b][:]))
                        release_bank(b, cv, av)
                    elif e < E - 1 or fin is None:
                        av = OP("vector", lambda En, d=d, s=s, b=b: En.tensor_add(
                            acc_sb[:, d * RT + s * TS: d * RT + (s + 1) * TS],
                            acc_sb[:, d * RT + s * TS: d * RT + (s + 1) * TS], ps[b][:]))
                        release_bank(b, cv, av)
                    else:
                        k = d * 2 + s
                        tv = OP("vector", lambda En, d=d, s=s, b=b: En.tensor_add(
                            t2sc[:, s * 512:(s + 1) * 512],
                            acc_sb[:, d * RT + s * TS: d * RT + (s + 1) * TS], ps[b][:]))
                        release_bank(b, cv, tv)
                        WT("vector", fc, xld[k])
                        WT("vector", fc, st_war[s])
                        yv = OP("vector", lambda En, s=s, k=k: En.tensor_add(
                            t1sc[:, s * 512:(s + 1) * 512], t2sc[:, s * 512:(s + 1) * 512],
                            xring[:, (k % 4) * 512:(k % 4 + 1) * 512]))
                        xr_war[k % 4] = yv
                        WT(fe, cv, yv)
                        st = DMA(fe, fc, fin["dst"][:, d, t0 + s * TS: t0 + (s + 1) * TS],
                                 t1sc[:, s * 512:(s + 1) * 512])
                        st_war[s] = st
                        fin["last"] = st
                        if k + 4 < 2 * NCH:
                            fin_load(k + 4)
                dnfree[slot].append((ct, ct.n))
                if cwp is not None:
                    dnfree[slot].append(cwp)
                if d % 2 == 1:
                    drain_bg(1)
            prev_dn_end = ct.n
        PREV_DN[0] = ct.n

    def emit_ffn(call, src_r, xn_r_sb, acc_sb):
        def xn_dst(rnd, s):
            def f(c, rnd=rnd, s=s):
                return xn_r_sb[:, c * T + rnd * RT + s * TS: c * T + rnd * RT + (s + 1) * TS]
            return f

        cast0 = (call == 0)
        stq_r0 = ("sync", sx) if cast0 else ("gpsimd", sg)
        stq_r1 = ("gpsimd", sg)
        dst = x1_r if (call == 0 and stage >= 2) else y_r
        stq2_0 = ("scalar", sc) if (call == 0 and cast0) else None
        for s in range(2):
            for t in ln_thunks(lambda c, s=s: src_r[:, c, s * TS:(s + 1) * TS],
                               xn_dst(0, s), 0 * RT + s * TS, True, stq=stq_r0,
                               stq2=stq2_0):
                t()
        WT("tensor", cv, cv.n)
        for s in range(2):
            BGQ.extend(ln_thunks(lambda c, s=s: src_r[:, c, RT + s * TS: RT + (s + 1) * TS],
                                 xn_dst(1, s), RT + s * TS, True, stq=stq_r0))
        fin0 = {"src_r": src_r, "dst": dst, "stq": stq_r0, "last": 0}
        emit_experts(call, 0, xn_r_sb, acc_sb, fin=fin0)
        flush_bg()
        WT("tensor", cv, cv.n)

        if call == 0 and stage >= 2:
            BGQ.extend(fin_ln2_thunks(0, fin0["last"], stq_r1, fin0["stq"][1]))
        fin1 = {"src_r": src_r, "dst": dst, "stq": stq_r1, "last": 0}
        emit_experts(call, 1, xn_r_sb, acc_sb, fin=fin1)
        flush_bg()
        if call == 0 and stage >= 2:
            WT("sync", sg, sg.n)
            x2a = 0
            for c in range(NCH):
                x2a = DMA("sync", sx,
                          xn_r_sb[:, c * 2056 + 1: c * 2056 + 1 + RT],
                          xn2_r[:, c, 0:RT])
            X2PRE[0] = x2a
            for t in fin_ln2_thunks(1, fin1["last"], ("gpsimd", sg), sg):
                t()

    ffn_scope = ExitStack()
    scope_id = [0]

    def open_ffn_scope():
        k = scope_id[0]
        scope_id[0] += 1
        g = {}
        g["xn"] = ffn_scope.enter_context(nc.sbuf_tensor(f"xn_r_sb{k}", [128, NCH * T], BF16))
        g["acc"] = ffn_scope.enter_context(nc.sbuf_tensor(f"acc_sb{k}", [128, NCH * RT], BF16))
        g["h"] = ffn_scope.enter_context(nc.sbuf_tensor(f"hbuf{k}", [128, NI * RT], BF16))
        g["xr"] = ffn_scope.enter_context(nc.sbuf_tensor(f"xring{k}", [128, 4 * 512], F32))
        g["x2r"] = ffn_scope.enter_context(nc.sbuf_tensor(f"xn2ring{k}", [128, 2 * 512], BF16))
        g["t1"] = ffn_scope.enter_context(nc.sbuf_tensor(f"t1sc{k}", [128, 1024], F32))
        g["t2"] = ffn_scope.enter_context(nc.sbuf_tensor(f"t2sc{k}", [128, 1024], F32))
        g["cbc"] = ffn_scope.enter_context(nc.sbuf_tensor(f"cbc{k}", [128, 1024], F32))
        return g

    gsb = open_ffn_scope()
    xn_sbuf, acc_sbuf, hbuf = gsb["xn"], gsb["acc"], gsb["h"]
    xring, xn2ring = gsb["xr"], gsb["x2r"]
    t1sc, t2sc, cbc = gsb["t1"], gsb["t2"], gsb["cbc"]
    xn0_addr = nc.lookup_mloc(xn_sbuf).addr

    emit_ffn(0, xin_r, xn_sbuf, acc_sbuf)
    barrier()
    ffn_scope.close()

    if stage >= 2:
        attn_scope = ExitStack()
        xn2e = attn_scope.enter_context(nc.sbuf_tensor("xn2e", [128, NCH * 2056], BF16))
        assert nc.lookup_mloc(xn2e).addr == xn0_addr, "xn2e must alias xn"
        gates = attn_scope.enter_context(nc.sbuf_tensor("gates", [128, NCH * T], BF16))
        sub_scope = ExitStack()
        hat = sub_scope.enter_context(nc.sbuf_tensor("hat", [128, 4 * T], BF16))

        def xn2sl(c, a, b):
            return xn2e[:, c * 2056 + 1 + a: c * 2056 + 1 + b]

        x2l = 0
        for c in range(NCH):
            x2l = DMA("sync", sx, xn2sl(c, RT, T), xn2_r[:, c, RT:T])
        WT("tensor", sx, max(x2l, X2PRE[0]))

        att_upfree = {0: [], 1: [], 2: [], 3: []}

        def load_attn_col(src_view, idx, slot, nch_k):
            for (cc, vv2) in att_upfree[slot]:
                WT("gpsimd", cc, vv2)
            att_upfree[slot] = []
            dst = wgub[:, slot * 2048: slot * 2048 + nch_k * 128]
            v = DMA("gpsimd", sg, dst, src_view[idx])
            return sg, v

        for hc in range(4):
            lc, lv = load_attn_col(wg1_rc, hc, hc % 4, NCH)
            WT("tensor", lc, lv)
            slot = hc % 4
            for s4 in range(NSUB):
                b = s4 % 2
                use_bank(b)
                gm = None
                for c in range(NCH):
                    gm = MM(ps[b][:], wgub[:, slot * 2048 + c * 128: slot * 2048 + (c + 1) * 128],
                            xn2sl(c, s4 * TS, (s4 + 1) * TS),
                            start=(c == 0), stop=(c == NCH - 1), inc=(c == NCH - 1))
                WT("scalar", ct, gm)
                sv = OP("scalar", lambda En, hc=hc, s4=s4, b=b: En.activation(
                    out=hat[:, hc * T + s4 * TS: hc * T + (s4 + 1) * TS], in_=ps[b][:],
                    func=AF.Relu, bias=bg1_sb[:, hc:hc + 1]))
                release_bank(b, cs, sv)
            att_upfree[slot].append((ct, ct.n))

        for rc in range(NCH):
            lc, lv = load_attn_col(wg2_rc, rc, rc % 4, 4)
            WT("tensor", lc, lv)
            slot = rc % 4
            for s4 in range(NSUB):
                b = 2 + s4 % 2
                use_bank(b)
                gm = None
                for hc in range(4):
                    gm = MM(ps[b][:], wgub[:, slot * 2048 + hc * 128: slot * 2048 + (hc + 1) * 128],
                            hat[:, hc * T + s4 * TS: hc * T + (s4 + 1) * TS],
                            start=(hc == 0), stop=(hc == 3), inc=(hc == 3))
                WT("scalar", ct, gm)
                sv = OP("scalar", lambda En, rc=rc, s4=s4, b=b: En.activation(
                    out=gates[:, rc * T + s4 * TS: rc * T + (s4 + 1) * TS], in_=ps[b][:],
                    func=AF.Sigmoid, bias=bg2_sb[:, rc:rc + 1]))
                release_bank(b, cs, sv)
            att_upfree[slot].append((ct, ct.n))
        barrier()
        sub_scope.close()
        smalls = ExitStack()
        ugsc = smalls.enter_context(nc.sbuf_tensor("ugsc", [128, 512], F32))
        cumd = smalls.enter_context(nc.sbuf_tensor("cumd", [128, 2 * 512], F32))
        cnvr = smalls.enter_context(nc.sbuf_tensor("cnvr", [128, 3 * 512], F32))
        xr1 = smalls.enter_context(nc.sbuf_tensor("xr1", [128, 3 * 512], F32))
        st2 = smalls.enter_context(nc.sbuf_tensor("st2", [128, 2 * 512], F32))

        cum_store = {0: 0, 1: 0}
        for rc in range(NCH):
            lc, lv = load_attn_col(wut_rc, rc, rc % 4, NCH)
            WT("tensor", lc, lv)
            slot = rc % 4
            for s4 in range(NSUB):
                b = s4
                use_bank(b)
                um = None
                for c in range(NCH):
                    um = MM(ps[b][:], wgub[:, slot * 2048 + c * 128: slot * 2048 + (c + 1) * 128],
                            xn2sl(c, s4 * TS, (s4 + 1) * TS),
                            start=(c == 0), stop=(c == NCH - 1), inc=(c == NCH - 1))
                WT("vector", ct, um)
                ug = OP("vector", lambda En, rc=rc, s4=s4, b=b: En.scalar_tensor_tensor(
                    out=ugsc[:], in0=ps[b][:], scalar=bu_sb[:, rc:rc + 1],
                    in1=gates[:, rc * T + s4 * TS: rc * T + (s4 + 1) * TS],
                    op0=ALU.add, op1=ALU.mult))
                release_bank(b, cv, ug)
                half = s4 % 2
                if s4 >= 2:
                    WT("vector", sx, cum_store[half])
                if s4 == 0:
                    scn = OP("vector", lambda En, half=half: En.tensor_tensor_scan(
                        out=cumd[:, half * 512:(half + 1) * 512], data0=onesc[:], data1=ugsc[:],
                        initial=0.0, op0=ALU.mult, op1=ALU.add))
                else:
                    ohalf = 1 - half
                    scn = OP("vector", lambda En, half=half, ohalf=ohalf: En.tensor_tensor_scan(
                        out=cumd[:, half * 512:(half + 1) * 512], data0=onesc[:], data1=ugsc[:],
                        initial=cumd[:, ohalf * 512 + 511: ohalf * 512 + 512],
                        op0=ALU.mult, op1=ALU.add))
                WT("sync", cv, scn)
                cum_store[half] = DMA("sync", sx, cum_r[:, rc, s4 * TS:(s4 + 1) * TS],
                                      cumd[:, half * 512:(half + 1) * 512])
            OP("vector", lambda En, rc=rc: En.tensor_copy(
                pay_sb[:, rc:rc + 1], cumd[:, 3 % 2 * 512 + 511: 3 % 2 * 512 + 512]))
            att_upfree[slot].append((ct, ct.n))
        paydone = cv.n

        pay_w = sx.n
        def exch_sync(En):
            with En.register("rp1") as rp1, En.register("rp2") as rp2:
                En.reg_load(rp1, par_sb[0:1, 0:1])
                En.reg_load(rp2, par_sb[0:1, 1:2])
                pv1 = En.snap(rp1, min_val=0, max_val=1)
                pv2 = En.snap(rp2, min_val=0, max_val=128)
                En.dma_start(out=shm.ap()[bass.ds(pv2, 128), :], in_=pay_sb[:]).then_inc(sx.h, 16)
                En.wait_ge(sx.h, pay_w + 16)
                En.dma_start(out=flg.ap()[bass.ds(pv1, 1), :], in_=ep_sb[:]).then_inc(sx.h, 16)
                En.wait_ge(sx.h, pay_w + 32)
        WT("sync", cv, paydone)
        EM("sync", exch_sync)
        sx.n += 32
        flag_w = sx.n

        def exch_poll(En):
            with En.register("pr") as pr, En.register("pe") as pe:
                En.reg_load(pe, ep_sb[0:1, 0:1])
                for q in (0, 1):
                    En.reg_mov(pr, 1)
                    with En.While(pr):
                        En.load(pr, flg.ap()[q:q + 1, 0:1])
                        En.reg_sub(pr, pr, pe)
        WT("gpsimd", cg, cg.n)
        EM("gpsimd", exch_poll)
        vA = DMA("gpsimd", sg, slA_sb[:], shm.ap()[0:128, :])
        vB = DMA("gpsimd", sg, slB_sb[:], shm.ap()[128:256, :])
        WT("vector", sg, vB)
        OP("vector", lambda En: En.tensor_scalar(
            out=offs_sb[:], in0=slA_sb[:, 0:16], scalar1=modd_sb[:, 0:1],
            scalar2=0.0, op0=ALU.mult, op1=ALU.add))
        for c in range(NCH):
            OP("vector", lambda En, c=c: En.tensor_scalar(
                out=xn2e[:, c * 2056: c * 2056 + 1], in0=slA_sb[:, 32 + c:33 + c],
                scalar1=modd_sb[:, 0:1], scalar2=0.0, op0=ALU.mult, op1=ALU.add))
            OP("vector", lambda En, c=c: En.tensor_scalar(
                out=xn2e[:, c * 2056 + 2049: c * 2056 + 2050], in0=slB_sb[:, 16 + c:17 + c],
                scalar1=mevn_sb[:, 0:1], scalar2=0.0, op0=ALU.mult, op1=ALU.add))
        exch_done = cv.n

        NTV = NCH * NSUB
        vcl = {}

        def v_cum_load(k, guard):
            rck, s4k = divmod(k, NSUB)
            WT("sync", cv, guard)
            vcl[k] = DMA("sync", sx, cumd[:, (k % 2) * 512:(k % 2 + 1) * 512],
                         cum_r[:, rck, s4k * TS:(s4k + 1) * TS])

        v_cum_load(0, 0)
        v_cum_load(1, 0)
        for rc in range(NCH):
            lc, lv = load_attn_col(wvt_rc, rc, rc % 4, NCH)
            WT("tensor", lc, lv)
            slot = rc % 4
            for s4 in range(NSUB):
                k = rc * NSUB + s4
                b = s4
                use_bank(b)
                vm = None
                for c in range(NCH):
                    vm = MM(ps[b][:], wgub[:, slot * 2048 + c * 128: slot * 2048 + (c + 1) * 128],
                            xn2sl(c, s4 * TS, (s4 + 1) * TS),
                            start=(c == 0), stop=(c == NCH - 1), inc=(c == NCH - 1))
                WT("vector", ct, vm)
                vg = OP("vector", lambda En, rc=rc, s4=s4, b=b: En.scalar_tensor_tensor(
                    out=ugsc[:], in0=ps[b][:], scalar=bv_sb[:, rc:rc + 1],
                    in1=gates[:, rc * T + s4 * TS: rc * T + (s4 + 1) * TS],
                    op0=ALU.add, op1=ALU.mult))
                release_bank(b, cv, vg)
                WT("vector", sx, vcl[k])
                if k == 0:
                    WT("vector", cv, exch_done)
                go = OP("vector", lambda En, rc=rc, s4=s4, k=k: En.scalar_tensor_tensor(
                    out=gates[:, rc * T + s4 * TS: rc * T + (s4 + 1) * TS],
                    in0=cumd[:, (k % 2) * 512:(k % 2 + 1) * 512],
                    scalar=offs_sb[:, rc:rc + 1], in1=ugsc[:],
                    op0=ALU.add, op1=ALU.mult))
                if k + 2 < NTV:
                    v_cum_load(k + 2, go)
            att_upfree[slot].append((ct, ct.n))
        barrier()
        if DEBUG:
            for c in range(NCH):
                DMA("sync", sx, dgo.rearrange("(c p) t -> p c t", p=128)[:, c, :],
                    gates[:, c * T:(c + 1) * T])
            barrier()

        NTU = NCH * NSUB
        uv_dst = x2_r if stage >= 3 else y_r
        uxr_war = [0, 0, 0]
        uxld = {}
        ustw = [0, 0]

        def uv_load(k):
            dck, s4k = divmod(k, NSUB)
            WT("sync", cv, uxr_war[k % 3])
            uxld[k] = DMA("sync", sx, xr1[:, (k % 3) * 512:(k % 3 + 1) * 512],
                          x1_r[:, dck, s4k * TS:(s4k + 1) * TS])

        def uv_conv(k):
            dck, s4k = divmod(k, NSUB)
            base = dck * 2056 + 1 + s4k * TS
            sl = k % 3
            OP("vector", lambda En, dc=dck, base=base, sl=sl: En.tensor_scalar(
                out=cnvr[:, sl * 512:(sl + 1) * 512], in0=xn2e[:, base - 1: base - 1 + TS],
                scalar1=cw_sb[:, dc * 3: dc * 3 + 1], scalar2=cb_sb[:, dc:dc + 1],
                op0=ALU.mult, op1=ALU.add))
            OP("vector", lambda En, dc=dck, base=base, sl=sl: En.scalar_tensor_tensor(
                out=cnvr[:, sl * 512:(sl + 1) * 512], in0=xn2e[:, base: base + TS],
                scalar=cw_sb[:, dc * 3 + 1: dc * 3 + 2],
                in1=cnvr[:, sl * 512:(sl + 1) * 512],
                op0=ALU.mult, op1=ALU.add))
            OP("vector", lambda En, dc=dck, base=base, sl=sl: En.scalar_tensor_tensor(
                out=cnvr[:, sl * 512:(sl + 1) * 512], in0=xn2e[:, base + 1: base + 1 + TS],
                scalar=cw_sb[:, dc * 3 + 2: dc * 3 + 3],
                in1=cnvr[:, sl * 512:(sl + 1) * 512],
                op0=ALU.mult, op1=ALU.add))

        for k in range(3):
            uv_load(k)
        uv_conv(0)
        uv_conv(1)
        for dc in range(NCH):
            lc, lv = load_attn_col(wut_rc, dc, dc % 4, NCH)
            WT("tensor", lc, lv)
            slot = dc % 4
            for s4 in range(NSUB):
                k = dc * NSUB + s4
                b = 4 + (k % 4)
                use_bank(b)
                uvm = None
                for rc in range(NCH):
                    uvm = MM(ps[b][:], wgub[:, slot * 2048 + rc * 128: slot * 2048 + (rc + 1) * 128],
                             gates[:, rc * T + s4 * TS: rc * T + (s4 + 1) * TS],
                             start=(rc == 0), stop=(rc == NCH - 1), inc=(rc == NCH - 1))
                if k + 2 < NTU:
                    uv_conv(k + 2)
                WT("vector", ct, uvm)
                bav = OP("vector", lambda En, b=b, sl=k % 3: En.tensor_add(
                    cnvr[:, sl * 512:(sl + 1) * 512], cnvr[:, sl * 512:(sl + 1) * 512],
                    ps[b][:]))
                release_bank(b, cv, bav)
                WT("vector", sx, uxld[k])
                WT("vector", sx, ustw[k % 2])
                x2v = OP("vector", lambda En, sl=k % 3, so=k % 2: En.tensor_add(
                    st2[:, so * 512:(so + 1) * 512], cnvr[:, sl * 512:(sl + 1) * 512],
                    xr1[:, sl * 512:(sl + 1) * 512]))
                uxr_war[k % 3] = x2v
                WT("sync", cv, x2v)
                ustw[k % 2] = DMA("sync", sx, uv_dst[:, dc, s4 * TS:(s4 + 1) * TS],
                                  st2[:, (k % 2) * 512:(k % 2 + 1) * 512])
                if k + 3 < NTU:
                    uv_load(k + 3)
            att_upfree[slot].append((ct, ct.n))
        barrier()
        smalls.close()
        attn_scope.close()

        if stage >= 3:
            gsb = open_ffn_scope()
            xn_sbuf, acc_sbuf, hbuf = gsb["xn"], gsb["acc"], gsb["h"]
            xring, xn2ring = gsb["xr"], gsb["x2r"]
            t1sc, t2sc, cbc = gsb["t1"], gsb["t2"], gsb["cbc"]
            emit_ffn(1, x2_r, xn_sbuf, acc_sbuf)
            barrier()
            ffn_scope.close()

    if DEBUG and stage >= 2:
        DMA("sync", sx, dx1, x1_s.ap())
        DMA("sync", sx, dxn2, xn2_s.ap())
        DMA("sync", sx, dcum, cum_s.ap())
    WT("sync", sx, sx.n)
    for c in ALLC:
        WT("sync", c, c.n)

    with nc.Block() as block:
        @block.sync
        def _(En):
            for f in Q["sync"]:
                f(En)

        @block.gpsimd
        def _(En):
            for f in Q["gpsimd"]:
                f(En)

        @block.vector
        def _(En):
            for f in Q["vector"]:
                f(En)

        @block.scalar
        def _(En):
            for f in Q["scalar"]:
                f(En)

        @block.tensor
        def _(En):
            for f in Q["tensor"]:
                f(En)

    est.close()
    return nc



def kernel(L0_attn_x, L0_attn_Wu, L0_attn_bu, L0_attn_Wv, L0_attn_bv,
           L0_attn_Wg1, L0_attn_bg1, L0_attn_Wg2, L0_attn_bg2,
           L0_attn_conv_w, L0_attn_conv_b,
           L0_ffn_Wr, L0_ffn_Wg, L0_ffn_Wu, L0_ffn_Wd):
    x = np.ascontiguousarray(np.asarray(L0_attn_x, dtype=np.float32))
    B, L, Dm = x.shape
    assert (B, L, Dm) == (4, 4096, 2048)

    wg = np.asarray(L0_ffn_Wg, dtype=np.float32)
    wu = np.asarray(L0_ffn_Wu, dtype=np.float32)
    wd = np.asarray(L0_ffn_Wd, dtype=np.float32)
    wgq = wg.reshape(8, 16, 128, 11, 128).transpose(0, 3, 2, 1, 4).reshape(8, 11, 128, 2048)
    wuq = wu.reshape(8, 16, 128, 11, 128).transpose(0, 3, 2, 1, 4).reshape(8, 11, 128, 2048)
    wgu = np.ascontiguousarray(np.concatenate([wgq, wuq], axis=3)).reshape(8 * 11 * 128, 4096)
    wdq = np.ascontiguousarray(
        wd.reshape(8, 11, 128, 16, 128).transpose(0, 3, 2, 1, 4)).reshape(8 * 16 * 128, 1408)
    wr = np.ascontiguousarray(np.asarray(L0_ffn_Wr, dtype=np.float32)
                              .reshape(16, 128, 8).transpose(1, 0, 2).reshape(128, 128))
    def att_pack(w, ncol):
        w = np.asarray(w, dtype=np.float32)
        r, d2 = w.shape
        a = w.reshape(ncol, 128, d2 // 128, 128).transpose(0, 3, 2, 1)
        return np.ascontiguousarray(a).reshape(ncol * 128, d2)
    wut = att_pack(L0_attn_Wu, 16)
    wvt = att_pack(L0_attn_Wv, 16)
    wg1t = att_pack(L0_attn_Wg1, 4)
    wg2t = att_pack(L0_attn_Wg2, 16)
    def chan_swz(a, nch):
        a = np.asarray(a, dtype=np.float32)
        return np.ascontiguousarray(a.reshape(nch, 128).T)
    bu = chan_swz(L0_attn_bu, 16)
    bv = chan_swz(L0_attn_bv, 16)
    bg1 = chan_swz(L0_attn_bg1, 4)
    bg2 = chan_swz(L0_attn_bg2, 16)
    cw = np.ascontiguousarray(np.asarray(L0_attn_conv_w, dtype=np.float32)
                              .reshape(16, 128, 3).transpose(1, 0, 2).reshape(128, 48))
    cb = chan_swz(L0_attn_conv_b, 16)
    ident = np.eye(128, dtype=np.float32)
    esel = np.zeros((8, 8 * 128), dtype=np.float32)
    for e_ in range(8):
        esel[e_, e_ * 128:(e_ + 1) * 128] = 1.0
    epoch = np.full((1, 16), (int(time.time() * 1000) % (1 << 29)) | 1, np.int32)

    nc = build(STAGE)
    in_maps = []
    for c in range(NCORES):
        b = c // 2
        h = c % 2
        xs = np.ascontiguousarray(x[b, h * T:(h + 1) * T, :].T)
        par = np.zeros((1, 16), np.int32)
        par[0, 0] = h
        par[0, 1] = h * 128
        modd = np.full((128, 1), 1.0 if h == 1 else 0.0, np.float32)
        mevn = np.full((128, 1), 1.0 if h == 0 else 0.0, np.float32)
        in_maps.append({
            "xin": xs, "fwgu": wgu, "fwd": wdq, "fwr": wr,
            "awut": wut, "awvt": wvt, "awg1t": wg1t, "awg2t": wg2t,
            "abu": bu, "abv": bv, "abg1": bg1, "abg2": bg2,
            "acw": cw, "acb": cb, "ident": ident, "esel": esel, "par": par,
            "epoch": epoch, "modd": modd, "mevn": mevn,
        })
    res = run_bass_kernel_spmd(nc, in_maps, core_ids=list(range(NCORES)))
    out = np.empty((4, 4096, 2048), np.float32)
    for c in range(NCORES):
        b = c // 2
        h = c % 2
        out[b, h * T:(h + 1) * T, :] = res.results[c]["yout"].T
    return out


# version 09
# speedup vs baseline: 1.8627x; 1.1686x over previous
import time
import numpy as np
from contextlib import ExitStack

import concourse.bass as bass
import concourse.mybir as mybir
from concourse.bass_utils import run_bass_kernel_spmd

F32 = mybir.dt.float32
BF16 = mybir.dt.bfloat16
I32 = mybir.dt.int32
AF = mybir.ActivationFunctionType
ALU = mybir.AluOpType

NCORES = 8
D = 2048
NCH = 16
T = 2048
E = 8
IFF = 1408
NI = 11
RT = 1024
TS = 512
NSUB = 4
EPS = 1e-5

STAGE = 3
DEBUG = False



def build(stage=STAGE):
    nc = bass.Bass("TRN2", target_bir_lowering=False)

    xin = nc.dram_tensor("xin", [D, T], F32, kind="ExternalInput").ap()
    wgu_h = nc.dram_tensor("fwgu", [E * NI * 128, 2 * 2048], F32, kind="ExternalInput").ap()
    wd_h = nc.dram_tensor("fwd", [E * NCH * 128, IFF], F32, kind="ExternalInput").ap()
    wr_h = nc.dram_tensor("fwr", [128, 128], F32, kind="ExternalInput").ap()
    wut_h = nc.dram_tensor("awut", [NCH * 128, 2048], F32, kind="ExternalInput").ap()
    wvt_h = nc.dram_tensor("awvt", [NCH * 128, 2048], F32, kind="ExternalInput").ap()
    wg1t_h = nc.dram_tensor("awg1t", [4 * 128, 2048], F32, kind="ExternalInput").ap()
    wg2t_h = nc.dram_tensor("awg2t", [NCH * 128, 512], F32, kind="ExternalInput").ap()
    bu_h = nc.dram_tensor("abu", [128, 16], F32, kind="ExternalInput").ap()
    bv_h = nc.dram_tensor("abv", [128, 16], F32, kind="ExternalInput").ap()
    bg1_h = nc.dram_tensor("abg1", [128, 4], F32, kind="ExternalInput").ap()
    bg2_h = nc.dram_tensor("abg2", [128, 16], F32, kind="ExternalInput").ap()
    cw_h = nc.dram_tensor("acw", [128, 48], F32, kind="ExternalInput").ap()
    cb_h = nc.dram_tensor("acb", [128, 16], F32, kind="ExternalInput").ap()
    ident_h = nc.dram_tensor("ident", [128, 128], F32, kind="ExternalInput").ap()
    esel_h = nc.dram_tensor("esel", [8, 8 * 128], F32, kind="ExternalInput").ap()
    par_h = nc.dram_tensor("par", [1, 16], I32, kind="ExternalInput").ap()
    epoch_h = nc.dram_tensor("epoch", [1, 16], I32, kind="ExternalInput").ap()
    modd_h = nc.dram_tensor("modd", [128, 1], F32, kind="ExternalInput").ap()
    mevn_h = nc.dram_tensor("mevn", [128, 1], F32, kind="ExternalInput").ap()
    yout = nc.dram_tensor("yout", [D, T], F32, kind="ExternalOutput").ap()
    if DEBUG:
        dx1 = nc.dram_tensor("dx1", [D, T], F32, kind="ExternalOutput").ap()
        dxn2 = nc.dram_tensor("dxn2", [D, T], BF16, kind="ExternalOutput").ap()
        dcum = nc.dram_tensor("dcum", [D, T], F32, kind="ExternalOutput").ap()
        dgo = nc.dram_tensor("dgo", [D, T], BF16, kind="ExternalOutput").ap()

    x1_s = nc.dram_tensor("x1_s", [D, T], F32)
    x2_s = nc.dram_tensor("x2_s", [D, T], F32)
    xn2_s = nc.dram_tensor("xn2_s", [D, T], BF16)
    cum_s = nc.dram_tensor("cum_s", [D, T], F32)
    wcu = nc.dram_tensor("wcu", [E * NI, 2 * NCH * 128 * 128], BF16)
    wcd = nc.dram_tensor("wcd", [E * NCH, NI * 128 * 128], BF16)
    shm = nc.dram_tensor("shmx", [256, 48], F32, addr_space="Shared")
    flg = nc.dram_tensor("flgx", [2, 16], I32, addr_space="Shared")

    xin_r = xin.rearrange("(c p) t -> p c t", p=128)
    y_r = yout.rearrange("(c p) t -> p c t", p=128)
    x1_r = x1_s.ap().rearrange("(c p) t -> p c t", p=128)
    x2_r = x2_s.ap().rearrange("(c p) t -> p c t", p=128)
    xn2_r = xn2_s.ap().rearrange("(c p) t -> p c t", p=128)
    cum_r = cum_s.ap().rearrange("(c p) t -> p c t", p=128)
    wgu_v = wgu_h.rearrange("(n p) f -> n p f", p=128)
    wd_v = wd_h.rearrange("(n p) f -> n p f", p=128)
    wut_rc = wut_h.rearrange("(n p) f -> n p f", p=128)
    wvt_rc = wvt_h.rearrange("(n p) f -> n p f", p=128)
    wg1_rc = wg1t_h.rearrange("(n p) f -> n p f", p=128)
    wg2_rc = wg2t_h.rearrange("(n p) f -> n p f", p=128)
    wcu_v = wcu.ap().rearrange("n (p f) -> n p f", p=128)
    wcd_v = wcd.ap().rearrange("n (p f) -> n p f", p=128)

    ENGS = ("sync", "gpsimd", "vector", "scalar", "tensor")
    Q = {e: [] for e in ENGS}

    def EM(eng, fn):
        Q[eng].append(fn)

    class Ctr:
        def __init__(self, name):
            self.cm = nc.semaphore(name)
            self.h = self.cm.__enter__()
            self.n = 0

    sx = Ctr("sx")
    sc = Ctr("sc")
    sg = Ctr("sg")
    cv = Ctr("cv")
    cs = Ctr("cs")
    cg = Ctr("cg")
    ct = Ctr("ct")
    ALLC = [sx, sc, sg, cv, cs, cg, ct]

    def WT(eng, ctr, val):
        if val <= 0:
            return
        EM(eng, lambda En, h=ctr.h, v=val: En.wait_ge(h, v))

    def DMA(eng, ctr, out, in_):
        def f(En, o=out, i=in_, h=ctr.h):
            En.dma_start(out=o, in_=i).then_inc(h, 16)
        EM(eng, f)
        ctr.n += 16
        return ctr.n

    ENG2CTR = {"vector": cv, "scalar": cs, "gpsimd": cg, "tensor": ct}

    def OP(eng, fn, flush=True):
        ctr = ENG2CTR[eng]
        if flush and ctr.n > 0:
            WT(eng, ctr, ctr.n)
        def g(En, fn=fn, h=ctr.h):
            fn(En).then_inc(h, 1)
        EM(eng, g)
        ctr.n += 1
        return ctr.n

    def MM(out, lhsT, rhs, start, stop, inc=False):
        if inc:
            def f(En, o=out, l=lhsT, r=rhs, st=start, sp=stop, h=ct.h):
                En.matmul(o, l, r, start=st, stop=sp).then_inc(h, 1)
            EM("tensor", f)
            ct.n += 1
            return ct.n
        else:
            def f(En, o=out, l=lhsT, r=rhs, st=start, sp=stop):
                En.matmul(o, l, r, start=st, stop=sp)
            EM("tensor", f)
            return None

    def barrier():
        for e in ENGS:
            for c in ALLC:
                WT(e, c, c.n)

    bank_free = [[] for _ in range(8)]

    def use_bank(b):
        for (c, v) in bank_free[b]:
            WT("tensor", c, v)
        bank_free[b] = []

    def release_bank(b, ctr, val):
        bank_free[b].append((ctr, val))

    est = ExitStack()

    def sb(name, shape, dt):
        return est.enter_context(nc.sbuf_tensor(name, shape, dt))

    bu_sb = sb("bu_sb", [128, 16], F32)
    bv_sb = sb("bv_sb", [128, 16], F32)
    bg2_sb = sb("bg2_sb", [128, 16], F32)
    cb_sb = sb("cb_sb", [128, 16], F32)
    bg1_sb = sb("bg1_sb", [128, 4], F32)
    cw_sb = sb("cw_sb", [128, 48], F32)
    wr_sb = sb("wr_sb", [128, 128], F32)
    wrb_sb = sb("wrb_sb", [128, 128], BF16)
    id_sb = sb("id_sb", [128, 128], F32)
    esel_sb = sb("esel_sb", [8, 8 * 128], F32)
    ones_b = sb("ones_b", [128, 1], BF16)
    ones1 = sb("ones1", [1, 128], F32)
    onesc = sb("onesc", [128, 512], F32)
    crows = sb("crows", [8, 2048], F32)
    lnsm = sb("lnsm", [1, 4 * 512], F32)
    lg_sb = sb("lg_sb", [8, 512], F32)
    tm_sb = sb("tm_sb", [128, 32], F32)
    tm2_sb = sb("tm2_sb", [128, 32], F32)
    pay_sb = sb("pay_sb", [128, 48], F32)
    slA_sb = sb("slA_sb", [128, 48], F32)
    slB_sb = sb("slB_sb", [128, 48], F32)
    offs_sb = sb("offs_sb", [128, 16], F32)
    modd_sb = sb("modd_sb", [128, 1], F32)
    mevn_sb = sb("mevn_sb", [128, 1], F32)
    par_sb = sb("par_sb", [1, 16], I32)
    ep_sb = sb("ep_sb", [1, 16], I32)
    wgub = sb("wgub", [128, 2 * 4096], BF16)
    wdb = sb("wdb", [128, 2 * 1408], BF16)
    epsv = sb("epsv", [1, 1], F32)
    sqfc = sb("sqfc", [128, 4 * 512], BF16)

    ps = [est.enter_context(nc.psum_tensor(f"ps{i}", [128, 512], F32)) for i in range(8)]

    small_loads = [
        (bu_sb[:], bu_h), (bv_sb[:], bv_h), (bg2_sb[:], bg2_h), (cb_sb[:], cb_h),
        (bg1_sb[:], bg1_h), (cw_sb[:], cw_h), (wr_sb[:], wr_h), (id_sb[:], ident_h),
        (modd_sb[:], modd_h), (mevn_sb[:], mevn_h), (par_sb[:], par_h), (ep_sb[:], epoch_h),
        (esel_sb[:], esel_h),
    ]
    for dst, srcv in small_loads:
        DMA("sync", sx, dst, srcv)
    v_pro = sx.n
    OP("vector", lambda En: En.memset(ones_b[:], 1.0))
    OP("vector", lambda En: En.memset(ones1[:], 1.0))
    OP("vector", lambda En: En.memset(onesc[:], 1.0))
    OP("vector", lambda En: En.memset(epsv[:], EPS))
    WT("vector", sx, v_pro)
    OP("vector", lambda En: En.tensor_copy(wrb_sb[:], wr_sb[:]))
    pro_done = cv.n

    PREV_DN = [0]
    X2PRE = [0]
    SQW = [0, 0, 0, 0]
    BGQ = []

    def drain_bg(n=1):
        for _ in range(n):
            if not BGQ:
                return
            BGQ.pop(0)()

    def flush_bg():
        while BGQ:
            BGQ.pop(0)()

    def ln_pass(src_chunk, cache_sl, dst_sl, c_off0, with_router, stq,
                pre_guards=None, extract=None, store_dst=None):
        th = []
        stq_e, stq_c = stq
        st = {"ld": {}, "bmm": {}, "norm_done": {}, "sqw": SQW}

        def loads(g):
            def f(g=g):
                if g == 0 and pre_guards is not None:
                    for (cc, vv) in pre_guards():
                        WT("gpsimd", cc, vv)
                for c in range(g * 8, g * 8 + 8):
                    st["ld"][c] = DMA("gpsimd", sg, cache_sl(c), src_chunk(c))
            return f
        th.append(loads(0))
        th.append(loads(1))

        def stats_group(s, c0):
            def f(s=s, c0=c0):
                if c0 == 0:
                    use_bank(6)
                    use_bank(7)
                for c in range(c0, c0 + 4):
                    sl = c % 4
                    WT("scalar", sg, st["ld"][c])
                    WT("scalar", ct, st["sqw"][sl])
                    scp = OP("scalar", lambda En, c=c, s=s, sl=sl: En.activation(
                        out=sqfc[:, sl * 512:(sl + 1) * 512],
                        in_=cache_sl(c)[:, s * TS:(s + 1) * TS], func=AF.Square))
                    WT("tensor", sg, st["ld"][c])
                    WT("tensor", cs, scp)
                    MM(ps[6][0:1, :], ones_b[:], cache_sl(c)[:, s * TS:(s + 1) * TS],
                       start=(c == 0), stop=(c == NCH - 1), inc=False)
                    st["sqw"][sl] = MM(ps[7][0:1, :], ones_b[:], sqfc[:, sl * 512:(sl + 1) * 512],
                                       start=(c == 0), stop=(c == NCH - 1), inc=True)
            return f

        def meanvar(s):
            def f(s=s):
                WT("scalar", ct, ct.n)
                s1 = OP("scalar", lambda En: En.mul(lnsm[0:1, 0:512], ps[6][0:1, :], 1.0 / D))
                s2 = OP("scalar", lambda En: En.mul(lnsm[0:1, 512:1024], ps[7][0:1, :], 1.0 / D))
                release_bank(6, cs, s2)
                release_bank(7, cs, s2)
                WT("vector", cs, s2)
                OP("vector", lambda En: En.tensor_mul(
                    lnsm[0:1, 1536:2048], lnsm[0:1, 0:512], lnsm[0:1, 0:512]))
                OP("vector", lambda En: En.tensor_sub(
                    lnsm[0:1, 512:1024], lnsm[0:1, 512:1024], lnsm[0:1, 1536:2048]))
                vv = cv.n
                WT("scalar", cv, vv)
                sd = OP("scalar", lambda En: En.activation(
                    out=lnsm[0:1, 1024:1536], in_=lnsm[0:1, 512:1024], func=AF.Sqrt,
                    bias=epsv[0:1, 0:1]))
                WT("vector", cs, sd)
                rst = OP("vector", lambda En: En.reciprocal(lnsm[0:1, 1024:1536],
                                                            lnsm[0:1, 1024:1536]))
                use_bank(6)
                use_bank(7)
                WT("tensor", cv, rst)
                MM(ps[6][:], ones1[:], lnsm[0:1, 0:512], start=True, stop=True, inc=False)
                st["bmm"][s] = MM(ps[7][:], ones1[:], lnsm[0:1, 1024:1536],
                                  start=True, stop=True, inc=True)
            return f

        def norm_group(s, c0):
            def f(s=s, c0=c0):
                if c0 == 0:
                    WT("vector", ct, st["bmm"][s])
                for c in range(c0, c0 + 4):
                    sl = c % 2
                    OP("vector", lambda En, c=c, s=s, sl=sl: En.tensor_sub(
                        xring[:, sl * 512:(sl + 1) * 512],
                        cache_sl(c)[:, s * TS:(s + 1) * TS], ps[6][:]))
                    mv = OP("vector", lambda En, c=c, s=s, sl=sl: En.tensor_mul(
                        dst_sl(c, s), xring[:, sl * 512:(sl + 1) * 512], ps[7][:]))
                    if extract is not None:
                        extract(c, s)
                    if store_dst is not None:
                        WT(stq_e, cv, mv)
                        DMA(stq_e, stq_c, store_dst(c, s), dst_sl(c, s))
                if c0 == 12:
                    st["norm_done"][s] = cv.n
                    release_bank(6, cv, cv.n)
                    release_bank(7, cv, cv.n)
            return f

        def router1(s):
          def f(s=s):
            state = st
            c_off = c_off0 + s * TS
            if True:
                use_bank(6)
                WT("tensor", cv, state["norm_done"][s])
                lgd = None
                for c in range(NCH):
                    lgd = MM(ps[6][0:8, :], wrb_sb[:, c * 8:(c + 1) * 8], dst_sl(c, s),
                             start=(c == 0), stop=(c == NCH - 1), inc=(c == NCH - 1))
                WT("scalar", ct, lgd)
                lge = OP("scalar", lambda En: En.copy(lg_sb[:], ps[6][0:8, :]))
                release_bank(6, cs, lge)
                use_bank(7)
                for j in range(4):
                    WT("tensor", cs, cs.n)
                    tj = MM(ps[7][0:128, 0:8], lg_sb[0:8, j * 128:(j + 1) * 128],
                            id_sb[0:8, 0:8], start=True, stop=True, inc=True)
                    WT("scalar", ct, tj)
                    OP("scalar", lambda En, j=j: En.copy(tm_sb[:, j * 8:(j + 1) * 8],
                                                         ps[7][0:128, 0:8]))
                release_bank(7, cs, cs.n)
          return f

        def router2(s):
          def f(s=s):
            c_off = c_off0 + s * TS
            if True:
                WT("scalar", cs, cs.n)
                ex = OP("scalar", lambda En: En.activation(out=tm_sb[:], in_=tm_sb[:], func=AF.Exp))
                WT("vector", cs, ex)
                for j in range(4):
                    b0 = j * 8
                    OP("vector", lambda En, b0=b0: En.reduce_max(
                        offs_sb[:, 0:1], tm_sb[:, b0:b0 + 8], axis=mybir.AxisListType.X))
                    OP("vector", lambda En, b0=b0: En.tensor_scalar(
                        out=tm2_sb[:, b0:b0 + 8], in0=tm_sb[:, b0:b0 + 8],
                        scalar1=offs_sb[:, 0:1], scalar2=1.0, op0=ALU.is_equal, op1=ALU.mult))
                    OP("vector", lambda En, b0=b0: En.scalar_tensor_tensor(
                        out=tm2_sb[:, b0:b0 + 8], in0=tm2_sb[:, b0:b0 + 8], scalar=-1e30,
                        in1=tm_sb[:, b0:b0 + 8], op0=ALU.mult, op1=ALU.add))
                    OP("vector", lambda En, b0=b0: En.reduce_max(
                        offs_sb[:, 1:2], tm2_sb[:, b0:b0 + 8], axis=mybir.AxisListType.X))
                    OP("vector", lambda En: En.tensor_add(offs_sb[:, 2:3], offs_sb[:, 0:1], offs_sb[:, 1:2]))
                    OP("vector", lambda En: En.reciprocal(offs_sb[:, 2:3], offs_sb[:, 2:3]))
                    OP("vector", lambda En, b0=b0: En.tensor_scalar(
                        out=tm2_sb[:, b0:b0 + 8], in0=tm_sb[:, b0:b0 + 8],
                        scalar1=offs_sb[:, 1:2], scalar2=1.0, op0=ALU.is_ge, op1=ALU.mult))
                    OP("vector", lambda En, b0=b0: En.tensor_mul(
                        tm2_sb[:, b0:b0 + 8], tm2_sb[:, b0:b0 + 8], tm_sb[:, b0:b0 + 8]))
                    OP("vector", lambda En, b0=b0: En.tensor_scalar(
                        out=tm2_sb[:, b0:b0 + 8], in0=tm2_sb[:, b0:b0 + 8],
                        scalar1=offs_sb[:, 2:3], scalar2=0.5, op0=ALU.mult, op1=ALU.mult))
                c2done = cv.n
                use_bank(6)
                for j in range(4):
                    WT("tensor", cv, c2done)
                    if j > 0:
                        WT("tensor", cs, cs.n)
                    tj = MM(ps[6][0:8, 0:128], tm2_sb[:, j * 8:(j + 1) * 8],
                            id_sb[:, :], start=True, stop=True, inc=True)
                    WT("scalar", ct, tj)
                    OP("scalar", lambda En, j=j, c_off=c_off: En.copy(
                        crows[0:8, c_off + j * 128: c_off + (j + 1) * 128], ps[6][0:8, 0:128]))
                release_bank(6, cs, cs.n)
          return f

        for s in (0, 1):
            for c0 in (0, 4, 8, 12):
                th.append(stats_group(s, c0))
            th.append(meanvar(s))
            for c0 in (0, 4, 8, 12):
                th.append(norm_group(s, c0))
            if with_router:
                th.append(router1(s))
                th.append(router2(s))
        return th

    upfree = {0: [], 1: []}
    dnfree = {0: [], 1: []}

    def load_up_col(call, rnd, e, i, cast_mode):
        slot = i % 2
        for (cc, vv2) in upfree[slot]:
            WT("gpsimd" if cast_mode else "sync", cc, vv2)
        upfree[slot] = []
        dst = wgub[:, slot * 4096:(slot + 1) * 4096]
        if cast_mode:
            v = DMA("gpsimd", sg, dst, wgu_v[e * NI + i])
            WT("scalar", sg, v)
            cw = DMA("scalar", sc, wcu_v[e * NI + i], dst)
            return (sg, v), (sc, cw)
        else:
            v = DMA("sync", sx, dst, wcu_v[e * NI + i])
            return (sx, v), None

    def load_dn_col(call, rnd, e, d, cast_mode):
        slot = d % 2
        for (cc, vv2) in dnfree[slot]:
            WT("gpsimd" if cast_mode else "sync", cc, vv2)
        dnfree[slot] = []
        dst = wdb[:, slot * 1408:(slot + 1) * 1408]
        if cast_mode:
            v1 = DMA("gpsimd", sg, dst, wd_v[e * NCH + d])
            WT("scalar", sg, v1)
            cwv = DMA("scalar", sc, wcd_v[e * NCH + d], dst)
            return (sg, v1), (sc, cwv)
        else:
            v1 = DMA("sync", sx, dst, wcd_v[e * NCH + d])
            return (sx, v1), None

    def emit_experts(call, rnd, xn_r_sb, acc_sb, fin=None):
        cast_mode = (call == 0 and rnd == 0)
        t0 = rnd * RT
        prev_dn_end = PREV_DN[0]
        for e in range(E):
            if e == E - 1 and fin is not None:
                flush_bg()
                fe, fc = fin["stq"]
                fin_base_cv = cv.n
                xld = {}
                xr_war = [fin_base_cv] * 4
                st_war = [0, 0]

                def fin_load(k):
                    d2, s2 = divmod(k, 2)
                    WT(fe, cv, xr_war[k % 4])
                    xld[k] = DMA(fe, fc, xring[:, (k % 4) * 512:(k % 4 + 1) * 512],
                                 fin["src_r"][:, d2, t0 + s2 * TS: t0 + (s2 + 1) * TS])
            use_bank(4)
            use_bank(5)
            WT("tensor", cs, cs.n)
            MM(ps[4][:], esel_sb[0:8, e * 128:(e + 1) * 128], crows[0:8, t0:t0 + TS],
               start=True, stop=True, inc=False)
            cb2 = MM(ps[5][:], esel_sb[0:8, e * 128:(e + 1) * 128],
                     crows[0:8, t0 + TS:t0 + 2 * TS], start=True, stop=True, inc=True)
            WT("scalar", ct, cb2)
            cbe1 = OP("scalar", lambda En: En.copy(cbc[:, 0:512], ps[4][:]))
            cbe2 = OP("scalar", lambda En: En.copy(cbc[:, 512:1024], ps[5][:]))
            release_bank(4, cs, cbe2)
            release_bank(5, cs, cbe2)
            for i in range(NI):
                (lc, lv), cwp = load_up_col(call, rnd, e, i, cast_mode)
                WT("tensor", lc, lv)
                slot = i % 2
                for s in range(2):
                    use_bank(s)
                    use_bank(2 + s)
                    gmm = None
                    for c in range(NCH):
                        gmm = MM(ps[s][:], wgub[:, slot * 4096 + c * 128: slot * 4096 + (c + 1) * 128],
                                 xn_r_sb[:, c * T + t0 + s * TS: c * T + t0 + (s + 1) * TS],
                                 start=(c == 0), stop=(c == NCH - 1), inc=(c == NCH - 1))
                    for c in range(NCH):
                        umm = MM(ps[2 + s][:], wgub[:, slot * 4096 + 2048 + c * 128: slot * 4096 + 2048 + (c + 1) * 128],
                                 xn_r_sb[:, c * T + t0 + s * TS: c * T + t0 + (s + 1) * TS],
                                 start=(c == 0), stop=(c == NCH - 1), inc=(c == NCH - 1))
                    WT("scalar", ct, gmm)
                    sil = OP("scalar", lambda En, s=s: En.activation(
                        out=t1sc[:, s * 512:(s + 1) * 512], in_=ps[s][:], func=AF.Silu))
                    release_bank(s, cs, sil)
                    WT("vector", cs, sil)
                    WT("vector", ct, umm)
                    t2v = OP("vector", lambda En, s=s: En.tensor_mul(
                        t2sc[:, s * 512:(s + 1) * 512], t1sc[:, s * 512:(s + 1) * 512], ps[2 + s][:]))
                    release_bank(2 + s, cv, t2v)
                    WT("vector", cs, cbe2)
                    if prev_dn_end:
                        WT("vector", ct, prev_dn_end)
                    hv = OP("vector", lambda En, i=i, s=s: En.tensor_mul(
                        hbuf[:, i * RT + s * TS: i * RT + (s + 1) * TS],
                        t2sc[:, s * 512:(s + 1) * 512], cbc[:, s * 512:(s + 1) * 512]))
                upfree[slot].append((ct, ct.n))
                if cwp is not None:
                    upfree[slot].append(cwp)
                drain_bg(1)
            hready = cv.n
            if e == E - 1 and fin is not None:
                for k in range(4):
                    fin_load(k)
            for d in range(NCH):
                (lc, lv), cwp = load_dn_col(call, rnd, e, d, cast_mode)
                WT("tensor", lc, lv)
                WT("tensor", cv, hready)
                slot = d % 2
                for s in range(2):
                    b = 4 + s
                    use_bank(b)
                    ymm = None
                    for ci in range(NI):
                        ymm = MM(ps[b][:], wdb[:, slot * 1408 + ci * 128: slot * 1408 + (ci + 1) * 128],
                                 hbuf[:, ci * RT + s * TS: ci * RT + (s + 1) * TS],
                                 start=(ci == 0), stop=(ci == NI - 1), inc=(ci == NI - 1))
                    WT("vector", ct, ymm)
                    if e == 0:
                        av = OP("vector", lambda En, d=d, s=s, b=b: En.tensor_copy(
                            acc_sb[:, d * RT + s * TS: d * RT + (s + 1) * TS], ps[b][:]))
                        release_bank(b, cv, av)
                    elif e < E - 1 or fin is None:
                        av = OP("vector", lambda En, d=d, s=s, b=b: En.tensor_add(
                            acc_sb[:, d * RT + s * TS: d * RT + (s + 1) * TS],
                            acc_sb[:, d * RT + s * TS: d * RT + (s + 1) * TS], ps[b][:]))
                        release_bank(b, cv, av)
                    else:
                        k = d * 2 + s
                        tv = OP("vector", lambda En, d=d, s=s, b=b: En.tensor_add(
                            t2sc[:, s * 512:(s + 1) * 512],
                            acc_sb[:, d * RT + s * TS: d * RT + (s + 1) * TS], ps[b][:]))
                        release_bank(b, cv, tv)
                        WT("vector", fc, xld[k])
                        WT("vector", fc, st_war[s])
                        yv = OP("vector", lambda En, s=s, k=k: En.tensor_add(
                            t1sc[:, s * 512:(s + 1) * 512], t2sc[:, s * 512:(s + 1) * 512],
                            xring[:, (k % 4) * 512:(k % 4 + 1) * 512]))
                        xr_war[k % 4] = yv
                        WT(fe, cv, yv)
                        st = DMA(fe, fc, fin["dst"][:, d, t0 + s * TS: t0 + (s + 1) * TS],
                                 t1sc[:, s * 512:(s + 1) * 512])
                        st_war[s] = st
                        fin["last"] = st
                        if k + 4 < 2 * NCH:
                            fin_load(k + 4)
                dnfree[slot].append((ct, ct.n))
                if cwp is not None:
                    dnfree[slot].append(cwp)
                if d % 2 == 1:
                    drain_bg(1)
            prev_dn_end = ct.n
        PREV_DN[0] = ct.n

    def emit_ffn(call, src_r, xn_r_sb, acc_sb):
        def xn_cache(rnd):
            return lambda c: xn_r_sb[:, c * T + rnd * RT: c * T + rnd * RT + RT]

        def xn_dst2(rnd):
            return lambda c, s: xn_r_sb[:, c * T + rnd * RT + s * TS:
                                        c * T + rnd * RT + (s + 1) * TS]

        cast0 = (call == 0)
        stq_r0 = ("sync", sx) if cast0 else ("gpsimd", sg)
        stq_r1 = ("gpsimd", sg)
        dst = x1_r if (call == 0 and stage >= 2) else y_r
        for t in ln_pass(lambda c: src_r[:, c, 0:RT], xn_cache(0), xn_dst2(0),
                         0, True, stq_r0):
            t()
        WT("tensor", cv, cv.n)
        BGQ.extend(ln_pass(lambda c: src_r[:, c, RT:T], xn_cache(1), xn_dst2(1),
                           RT, True, stq_r0))
        fin0 = {"src_r": src_r, "dst": dst, "stq": stq_r0, "last": 0}
        emit_experts(call, 0, xn_r_sb, acc_sb, fin=fin0)
        flush_bg()
        WT("tensor", cv, cv.n)

        if call == 0 and stage >= 2:
            rd0_ct = ct.n

            def ex0(c, s):
                if s == 0:
                    OP("vector", lambda En, c=c: En.tensor_copy(
                        pay_sb[:, 16 + c:17 + c], xn_r_sb[:, c * T: c * T + 1]))
            BGQ.extend(ln_pass(
                lambda c: x1_r[:, c, 0:RT], xn_cache(0), xn_dst2(0),
                0, False, stq_r1,
                pre_guards=lambda g=(fin0["stq"][1], fin0["last"]): [g, (ct, rd0_ct)],
                extract=ex0,
                store_dst=lambda c, s: xn2_r[:, c, s * TS:(s + 1) * TS]))
        fin1 = {"src_r": src_r, "dst": dst, "stq": stq_r1, "last": 0}
        emit_experts(call, 1, xn_r_sb, acc_sb, fin=fin1)
        flush_bg()
        if call == 0 and stage >= 2:
            WT("sync", sg, sg.n)
            x2a = 0
            for c in range(NCH):
                x2a = DMA("sync", sx,
                          xn_r_sb[:, c * 2056 + 1: c * 2056 + 1 + RT],
                          xn2_r[:, c, 0:RT])
            X2PRE[0] = x2a
            acc_cv = cv.n

            def ex1(c, s):
                if s == 1:
                    OP("vector", lambda En, c=c: En.tensor_copy(
                        pay_sb[:, 32 + c:33 + c],
                        xn_r_sb[:, c * 2056 + 2048: c * 2056 + 2049]))
            for t in ln_pass(
                    lambda c: x1_r[:, c, RT:T],
                    lambda c: acc_sb[:, c * RT:(c + 1) * RT],
                    lambda c, s: xn_r_sb[:, c * 2056 + 1 + RT + s * TS:
                                         c * 2056 + 1 + RT + (s + 1) * TS],
                    0, False, ("gpsimd", sg),
                    pre_guards=lambda v=fin1["last"], a=acc_cv: [(sg, v), (cv, a)],
                    extract=ex1):
                t()

    ffn_scope = ExitStack()
    scope_id = [0]

    def open_ffn_scope():
        k = scope_id[0]
        scope_id[0] += 1
        g = {}
        g["xn"] = ffn_scope.enter_context(nc.sbuf_tensor(f"xn_r_sb{k}", [128, NCH * 2056], BF16))
        g["acc"] = ffn_scope.enter_context(nc.sbuf_tensor(f"acc_sb{k}", [128, NCH * RT], BF16))
        g["h"] = ffn_scope.enter_context(nc.sbuf_tensor(f"hbuf{k}", [128, NI * RT], BF16))
        g["xr"] = ffn_scope.enter_context(nc.sbuf_tensor(f"xring{k}", [128, 4 * 512], F32))
        g["t1"] = ffn_scope.enter_context(nc.sbuf_tensor(f"t1sc{k}", [128, 1024], F32))
        g["t2"] = ffn_scope.enter_context(nc.sbuf_tensor(f"t2sc{k}", [128, 1024], F32))
        g["cbc"] = ffn_scope.enter_context(nc.sbuf_tensor(f"cbc{k}", [128, 1024], F32))
        return g

    gsb = open_ffn_scope()
    xn_sbuf, acc_sbuf, hbuf = gsb["xn"], gsb["acc"], gsb["h"]
    xring = gsb["xr"]
    t1sc, t2sc, cbc = gsb["t1"], gsb["t2"], gsb["cbc"]
    xn0_addr = nc.lookup_mloc(xn_sbuf).addr

    WT("tensor", cv, pro_done)
    emit_ffn(0, xin_r, xn_sbuf, acc_sbuf)
    barrier()
    ffn_scope.close()

    if stage >= 2:
        attn_scope = ExitStack()
        xn2e = attn_scope.enter_context(nc.sbuf_tensor("xn2e", [128, NCH * 2056], BF16))
        assert nc.lookup_mloc(xn2e).addr == xn0_addr, "xn2e must alias xn"
        gates = attn_scope.enter_context(nc.sbuf_tensor("gates", [128, NCH * T], BF16))
        sub_scope = ExitStack()
        hat = sub_scope.enter_context(nc.sbuf_tensor("hat", [128, 4 * T], BF16))

        def xn2sl(c, a, b):
            return xn2e[:, c * 2056 + 1 + a: c * 2056 + 1 + b]

        WT("tensor", sx, X2PRE[0])

        att_upfree = {0: [], 1: [], 2: [], 3: []}

        def load_attn_col(src_view, idx, slot, nch_k):
            for (cc, vv2) in att_upfree[slot]:
                WT("gpsimd", cc, vv2)
            att_upfree[slot] = []
            dst = wgub[:, slot * 2048: slot * 2048 + nch_k * 128]
            v = DMA("gpsimd", sg, dst, src_view[idx])
            return sg, v

        g1pre = {hc: load_attn_col(wg1_rc, hc, hc % 4, NCH) for hc in range(4)}
        for hc in range(4):
            lc, lv = g1pre[hc]
            WT("tensor", lc, lv)
            slot = hc % 4
            for s4 in range(NSUB):
                b = s4 % 2
                use_bank(b)
                gm = None
                for c in range(NCH):
                    gm = MM(ps[b][:], wgub[:, slot * 2048 + c * 128: slot * 2048 + (c + 1) * 128],
                            xn2sl(c, s4 * TS, (s4 + 1) * TS),
                            start=(c == 0), stop=(c == NCH - 1), inc=(c == NCH - 1))
                WT("scalar", ct, gm)
                sv = OP("scalar", lambda En, hc=hc, s4=s4, b=b: En.activation(
                    out=hat[:, hc * T + s4 * TS: hc * T + (s4 + 1) * TS], in_=ps[b][:],
                    func=AF.Relu, bias=bg1_sb[:, hc:hc + 1]))
                release_bank(b, cs, sv)
            att_upfree[slot].append((ct, ct.n))

        g2pre = {rc: load_attn_col(wg2_rc, rc, rc % 4, 4) for rc in range(4)}
        for rc in range(NCH):
            lc, lv = g2pre.pop(rc)
            WT("tensor", lc, lv)
            slot = rc % 4
            for s4 in range(NSUB):
                b = 2 + s4 % 2
                use_bank(b)
                gm = None
                for hc in range(4):
                    gm = MM(ps[b][:], wgub[:, slot * 2048 + hc * 128: slot * 2048 + (hc + 1) * 128],
                            hat[:, hc * T + s4 * TS: hc * T + (s4 + 1) * TS],
                            start=(hc == 0), stop=(hc == 3), inc=(hc == 3))
                WT("scalar", ct, gm)
                sv = OP("scalar", lambda En, rc=rc, s4=s4, b=b: En.activation(
                    out=gates[:, rc * T + s4 * TS: rc * T + (s4 + 1) * TS], in_=ps[b][:],
                    func=AF.Sigmoid, bias=bg2_sb[:, rc:rc + 1]))
                release_bank(b, cs, sv)
            att_upfree[slot].append((ct, ct.n))
            if rc + 4 < NCH:
                g2pre[rc + 4] = load_attn_col(wg2_rc, rc + 4, (rc + 4) % 4, 4)
        barrier()
        sub_scope.close()
        smalls = ExitStack()
        ugsc = smalls.enter_context(nc.sbuf_tensor("ugsc", [128, 512], F32))
        cumd = smalls.enter_context(nc.sbuf_tensor("cumd", [128, 2 * 512], F32))
        cnvr = smalls.enter_context(nc.sbuf_tensor("cnvr", [128, 3 * 512], F32))
        xr1 = smalls.enter_context(nc.sbuf_tensor("xr1", [128, 3 * 512], F32))
        st2 = smalls.enter_context(nc.sbuf_tensor("st2", [128, 2 * 512], F32))

        cum_store = {0: 0, 1: 0}
        upre = {rc: load_attn_col(wut_rc, rc, rc % 4, NCH) for rc in range(4)}
        for rc in range(NCH):
            lc, lv = upre.pop(rc)
            WT("tensor", lc, lv)
            slot = rc % 4
            for s4 in range(NSUB):
                b = s4
                use_bank(b)
                um = None
                for c in range(NCH):
                    um = MM(ps[b][:], wgub[:, slot * 2048 + c * 128: slot * 2048 + (c + 1) * 128],
                            xn2sl(c, s4 * TS, (s4 + 1) * TS),
                            start=(c == 0), stop=(c == NCH - 1), inc=(c == NCH - 1))
                WT("vector", ct, um)
                ug = OP("vector", lambda En, rc=rc, s4=s4, b=b: En.scalar_tensor_tensor(
                    out=ugsc[:], in0=ps[b][:], scalar=bu_sb[:, rc:rc + 1],
                    in1=gates[:, rc * T + s4 * TS: rc * T + (s4 + 1) * TS],
                    op0=ALU.add, op1=ALU.mult))
                release_bank(b, cv, ug)
                half = s4 % 2
                if s4 >= 2:
                    WT("vector", sx, cum_store[half])
                if s4 == 0:
                    scn = OP("vector", lambda En, half=half: En.tensor_tensor_scan(
                        out=cumd[:, half * 512:(half + 1) * 512], data0=onesc[:], data1=ugsc[:],
                        initial=0.0, op0=ALU.mult, op1=ALU.add))
                else:
                    ohalf = 1 - half
                    scn = OP("vector", lambda En, half=half, ohalf=ohalf: En.tensor_tensor_scan(
                        out=cumd[:, half * 512:(half + 1) * 512], data0=onesc[:], data1=ugsc[:],
                        initial=cumd[:, ohalf * 512 + 511: ohalf * 512 + 512],
                        op0=ALU.mult, op1=ALU.add))
                WT("sync", cv, scn)
                cum_store[half] = DMA("sync", sx, cum_r[:, rc, s4 * TS:(s4 + 1) * TS],
                                      cumd[:, half * 512:(half + 1) * 512])
            OP("vector", lambda En, rc=rc: En.tensor_copy(
                pay_sb[:, rc:rc + 1], cumd[:, 3 % 2 * 512 + 511: 3 % 2 * 512 + 512]))
            att_upfree[slot].append((ct, ct.n))
            if rc + 4 < NCH:
                upre[rc + 4] = load_attn_col(wut_rc, rc + 4, (rc + 4) % 4, NCH)
        paydone = cv.n
        vpre = {rc: load_attn_col(wvt_rc, rc, rc % 4, NCH) for rc in range(4)}

        pay_w = sx.n
        def exch_sync(En):
            with En.register("rp1") as rp1, En.register("rp2") as rp2:
                En.reg_load(rp1, par_sb[0:1, 0:1])
                En.reg_load(rp2, par_sb[0:1, 1:2])
                pv1 = En.snap(rp1, min_val=0, max_val=1)
                pv2 = En.snap(rp2, min_val=0, max_val=128)
                En.dma_start(out=shm.ap()[bass.ds(pv2, 128), :], in_=pay_sb[:]).then_inc(sx.h, 16)
                En.wait_ge(sx.h, pay_w + 16)
                En.dma_start(out=flg.ap()[bass.ds(pv1, 1), :], in_=ep_sb[:]).then_inc(sx.h, 16)
                En.wait_ge(sx.h, pay_w + 32)
        WT("sync", cv, paydone)
        EM("sync", exch_sync)
        sx.n += 32
        flag_w = sx.n

        def exch_poll(En):
            with En.register("pr") as pr, En.register("pe") as pe:
                En.reg_load(pe, ep_sb[0:1, 0:1])
                for q in (0, 1):
                    En.reg_mov(pr, 1)
                    with En.While(pr):
                        En.load(pr, flg.ap()[q:q + 1, 0:1])
                        En.reg_sub(pr, pr, pe)
        WT("gpsimd", cg, cg.n)
        EM("gpsimd", exch_poll)
        vA = DMA("gpsimd", sg, slA_sb[:], shm.ap()[0:128, :])
        vB = DMA("gpsimd", sg, slB_sb[:], shm.ap()[128:256, :])
        WT("vector", sg, vB)
        OP("vector", lambda En: En.tensor_scalar(
            out=offs_sb[:], in0=slA_sb[:, 0:16], scalar1=modd_sb[:, 0:1],
            scalar2=0.0, op0=ALU.mult, op1=ALU.add))
        for c in range(NCH):
            OP("vector", lambda En, c=c: En.tensor_scalar(
                out=xn2e[:, c * 2056: c * 2056 + 1], in0=slA_sb[:, 32 + c:33 + c],
                scalar1=modd_sb[:, 0:1], scalar2=0.0, op0=ALU.mult, op1=ALU.add))
            OP("vector", lambda En, c=c: En.tensor_scalar(
                out=xn2e[:, c * 2056 + 2049: c * 2056 + 2050], in0=slB_sb[:, 16 + c:17 + c],
                scalar1=mevn_sb[:, 0:1], scalar2=0.0, op0=ALU.mult, op1=ALU.add))
        exch_done = cv.n

        NTV = NCH * NSUB
        vcl = {}

        def v_cum_load(k, guard):
            rck, s4k = divmod(k, NSUB)
            WT("sync", cv, guard)
            vcl[k] = DMA("sync", sx, cumd[:, (k % 2) * 512:(k % 2 + 1) * 512],
                         cum_r[:, rck, s4k * TS:(s4k + 1) * TS])

        v_cum_load(0, 0)
        v_cum_load(1, 0)
        for rc in range(NCH):
            lc, lv = vpre.pop(rc)
            WT("tensor", lc, lv)
            slot = rc % 4
            for s4 in range(NSUB):
                k = rc * NSUB + s4
                b = s4
                use_bank(b)
                vm = None
                for c in range(NCH):
                    vm = MM(ps[b][:], wgub[:, slot * 2048 + c * 128: slot * 2048 + (c + 1) * 128],
                            xn2sl(c, s4 * TS, (s4 + 1) * TS),
                            start=(c == 0), stop=(c == NCH - 1), inc=(c == NCH - 1))
                WT("vector", ct, vm)
                vg = OP("vector", lambda En, rc=rc, s4=s4, b=b: En.scalar_tensor_tensor(
                    out=ugsc[:], in0=ps[b][:], scalar=bv_sb[:, rc:rc + 1],
                    in1=gates[:, rc * T + s4 * TS: rc * T + (s4 + 1) * TS],
                    op0=ALU.add, op1=ALU.mult))
                release_bank(b, cv, vg)
                WT("vector", sx, vcl[k])
                if k == 0:
                    WT("vector", cv, exch_done)
                go = OP("vector", lambda En, rc=rc, s4=s4, k=k: En.scalar_tensor_tensor(
                    out=gates[:, rc * T + s4 * TS: rc * T + (s4 + 1) * TS],
                    in0=cumd[:, (k % 2) * 512:(k % 2 + 1) * 512],
                    scalar=offs_sb[:, rc:rc + 1], in1=ugsc[:],
                    op0=ALU.add, op1=ALU.mult))
                if k + 2 < NTV:
                    v_cum_load(k + 2, go)
            att_upfree[slot].append((ct, ct.n))
            if rc + 4 < NCH:
                vpre[rc + 4] = load_attn_col(wvt_rc, rc + 4, (rc + 4) % 4, NCH)
        barrier()
        if DEBUG:
            for c in range(NCH):
                DMA("sync", sx, dgo.rearrange("(c p) t -> p c t", p=128)[:, c, :],
                    gates[:, c * T:(c + 1) * T])
            barrier()

        NTU = NCH * NSUB
        uv_dst = x2_r if stage >= 3 else y_r
        uxr_war = [0, 0, 0]
        uxld = {}
        ustw = [0, 0]

        def uv_load(k):
            dck, s4k = divmod(k, NSUB)
            WT("sync", cv, uxr_war[k % 3])
            uxld[k] = DMA("sync", sx, xr1[:, (k % 3) * 512:(k % 3 + 1) * 512],
                          x1_r[:, dck, s4k * TS:(s4k + 1) * TS])

        def uv_conv(k):
            dck, s4k = divmod(k, NSUB)
            base = dck * 2056 + 1 + s4k * TS
            sl = k % 3
            OP("vector", lambda En, dc=dck, base=base, sl=sl: En.tensor_scalar(
                out=cnvr[:, sl * 512:(sl + 1) * 512], in0=xn2e[:, base - 1: base - 1 + TS],
                scalar1=cw_sb[:, dc * 3: dc * 3 + 1], scalar2=cb_sb[:, dc:dc + 1],
                op0=ALU.mult, op1=ALU.add))
            OP("vector", lambda En, dc=dck, base=base, sl=sl: En.scalar_tensor_tensor(
                out=cnvr[:, sl * 512:(sl + 1) * 512], in0=xn2e[:, base: base + TS],
                scalar=cw_sb[:, dc * 3 + 1: dc * 3 + 2],
                in1=cnvr[:, sl * 512:(sl + 1) * 512],
                op0=ALU.mult, op1=ALU.add))
            OP("vector", lambda En, dc=dck, base=base, sl=sl: En.scalar_tensor_tensor(
                out=cnvr[:, sl * 512:(sl + 1) * 512], in0=xn2e[:, base + 1: base + 1 + TS],
                scalar=cw_sb[:, dc * 3 + 2: dc * 3 + 3],
                in1=cnvr[:, sl * 512:(sl + 1) * 512],
                op0=ALU.mult, op1=ALU.add))

        for k in range(3):
            uv_load(k)
        uv_conv(0)
        uv_conv(1)
        uvpre = {dc: load_attn_col(wut_rc, dc, dc % 4, NCH) for dc in range(4)}
        for dc in range(NCH):
            lc, lv = uvpre.pop(dc)
            WT("tensor", lc, lv)
            slot = dc % 4
            for s4 in range(NSUB):
                k = dc * NSUB + s4
                b = 4 + (k % 4)
                use_bank(b)
                uvm = None
                for rc in range(NCH):
                    uvm = MM(ps[b][:], wgub[:, slot * 2048 + rc * 128: slot * 2048 + (rc + 1) * 128],
                             gates[:, rc * T + s4 * TS: rc * T + (s4 + 1) * TS],
                             start=(rc == 0), stop=(rc == NCH - 1), inc=(rc == NCH - 1))
                if k + 2 < NTU:
                    uv_conv(k + 2)
                WT("vector", ct, uvm)
                bav = OP("vector", lambda En, b=b, sl=k % 3: En.tensor_add(
                    cnvr[:, sl * 512:(sl + 1) * 512], cnvr[:, sl * 512:(sl + 1) * 512],
                    ps[b][:]))
                release_bank(b, cv, bav)
                WT("vector", sx, uxld[k])
                WT("vector", sx, ustw[k % 2])
                x2v = OP("vector", lambda En, sl=k % 3, so=k % 2: En.tensor_add(
                    st2[:, so * 512:(so + 1) * 512], cnvr[:, sl * 512:(sl + 1) * 512],
                    xr1[:, sl * 512:(sl + 1) * 512]))
                uxr_war[k % 3] = x2v
                WT("sync", cv, x2v)
                ustw[k % 2] = DMA("sync", sx, uv_dst[:, dc, s4 * TS:(s4 + 1) * TS],
                                  st2[:, (k % 2) * 512:(k % 2 + 1) * 512])
                if k + 3 < NTU:
                    uv_load(k + 3)
            att_upfree[slot].append((ct, ct.n))
            if dc + 4 < NCH:
                uvpre[dc + 4] = load_attn_col(wut_rc, dc + 4, (dc + 4) % 4, NCH)
        barrier()
        smalls.close()
        attn_scope.close()

        if stage >= 3:
            gsb = open_ffn_scope()
            xn_sbuf, acc_sbuf, hbuf = gsb["xn"], gsb["acc"], gsb["h"]
            xring = gsb["xr"]
            t1sc, t2sc, cbc = gsb["t1"], gsb["t2"], gsb["cbc"]
            emit_ffn(1, x2_r, xn_sbuf, acc_sbuf)
            barrier()
            ffn_scope.close()

    if DEBUG and stage >= 2:
        DMA("sync", sx, dx1, x1_s.ap())
        DMA("sync", sx, dxn2, xn2_s.ap())
        DMA("sync", sx, dcum, cum_s.ap())
    WT("sync", sx, sx.n)
    for c in ALLC:
        WT("sync", c, c.n)

    with nc.Block() as block:
        @block.sync
        def _(En):
            for f in Q["sync"]:
                f(En)

        @block.gpsimd
        def _(En):
            for f in Q["gpsimd"]:
                f(En)

        @block.vector
        def _(En):
            for f in Q["vector"]:
                f(En)

        @block.scalar
        def _(En):
            for f in Q["scalar"]:
                f(En)

        @block.tensor
        def _(En):
            for f in Q["tensor"]:
                f(En)

    est.close()
    return nc



def kernel(L0_attn_x, L0_attn_Wu, L0_attn_bu, L0_attn_Wv, L0_attn_bv,
           L0_attn_Wg1, L0_attn_bg1, L0_attn_Wg2, L0_attn_bg2,
           L0_attn_conv_w, L0_attn_conv_b,
           L0_ffn_Wr, L0_ffn_Wg, L0_ffn_Wu, L0_ffn_Wd):
    x = np.ascontiguousarray(np.asarray(L0_attn_x, dtype=np.float32))
    B, L, Dm = x.shape
    assert (B, L, Dm) == (4, 4096, 2048)

    wg = np.asarray(L0_ffn_Wg, dtype=np.float32)
    wu = np.asarray(L0_ffn_Wu, dtype=np.float32)
    wd = np.asarray(L0_ffn_Wd, dtype=np.float32)
    wgq = wg.reshape(8, 16, 128, 11, 128).transpose(0, 3, 2, 1, 4).reshape(8, 11, 128, 2048)
    wuq = wu.reshape(8, 16, 128, 11, 128).transpose(0, 3, 2, 1, 4).reshape(8, 11, 128, 2048)
    wgu = np.ascontiguousarray(np.concatenate([wgq, wuq], axis=3)).reshape(8 * 11 * 128, 4096)
    wdq = np.ascontiguousarray(
        wd.reshape(8, 11, 128, 16, 128).transpose(0, 3, 2, 1, 4)).reshape(8 * 16 * 128, 1408)
    wr = np.ascontiguousarray(np.asarray(L0_ffn_Wr, dtype=np.float32)
                              .reshape(16, 128, 8).transpose(1, 0, 2).reshape(128, 128))
    def att_pack(w, ncol):
        w = np.asarray(w, dtype=np.float32)
        r, d2 = w.shape
        a = w.reshape(ncol, 128, d2 // 128, 128).transpose(0, 3, 2, 1)
        return np.ascontiguousarray(a).reshape(ncol * 128, d2)
    wut = att_pack(L0_attn_Wu, 16)
    wvt = att_pack(L0_attn_Wv, 16)
    wg1t = att_pack(L0_attn_Wg1, 4)
    wg2t = att_pack(L0_attn_Wg2, 16)
    def chan_swz(a, nch):
        a = np.asarray(a, dtype=np.float32)
        return np.ascontiguousarray(a.reshape(nch, 128).T)
    bu = chan_swz(L0_attn_bu, 16)
    bv = chan_swz(L0_attn_bv, 16)
    bg1 = chan_swz(L0_attn_bg1, 4)
    bg2 = chan_swz(L0_attn_bg2, 16)
    cw = np.ascontiguousarray(np.asarray(L0_attn_conv_w, dtype=np.float32)
                              .reshape(16, 128, 3).transpose(1, 0, 2).reshape(128, 48))
    cb = chan_swz(L0_attn_conv_b, 16)
    ident = np.eye(128, dtype=np.float32)
    esel = np.zeros((8, 8 * 128), dtype=np.float32)
    for e_ in range(8):
        esel[e_, e_ * 128:(e_ + 1) * 128] = 1.0
    epoch = np.full((1, 16), (int(time.time() * 1000) % (1 << 29)) | 1, np.int32)

    nc = build(STAGE)
    in_maps = []
    for c in range(NCORES):
        b = c // 2
        h = c % 2
        xs = np.ascontiguousarray(x[b, h * T:(h + 1) * T, :].T)
        par = np.zeros((1, 16), np.int32)
        par[0, 0] = h
        par[0, 1] = h * 128
        modd = np.full((128, 1), 1.0 if h == 1 else 0.0, np.float32)
        mevn = np.full((128, 1), 1.0 if h == 0 else 0.0, np.float32)
        in_maps.append({
            "xin": xs, "fwgu": wgu, "fwd": wdq, "fwr": wr,
            "awut": wut, "awvt": wvt, "awg1t": wg1t, "awg2t": wg2t,
            "abu": bu, "abv": bv, "abg1": bg1, "abg2": bg2,
            "acw": cw, "acb": cb, "ident": ident, "esel": esel, "par": par,
            "epoch": epoch, "modd": modd, "mevn": mevn,
        })
    res = run_bass_kernel_spmd(nc, in_maps, core_ids=list(range(NCORES)))
    out = np.empty((4, 4096, 2048), np.float32)
    for c in range(NCORES):
        b = c // 2
        h = c % 2
        out[b, h * T:(h + 1) * T, :] = res.results[c]["yout"].T
    return out
